# version 02
# baseline (speedup 1.0000x reference)
import numpy as np
import ml_dtypes
from contextlib import ExitStack
import concourse.bass as bass
import concourse.mybir as mybir
from concourse.bass_utils import run_bass_kernel_spmd

F32 = mybir.dt.float32
BF16 = mybir.dt.bfloat16
AF = mybir.ActivationFunctionType
ALU = mybir.AluOpType
AX = mybir.AxisListType

S, D, R = 8192, 4096, 1024
NH, QR, KVR, RD, ND, VD = 32, 1536, 512, 64, 128, 128
QK = ND + RD
FF = 16384
SC = float(QK) ** -0.5
NCORES = 8
LN_EPS = 1e-5
RMS_EPS = 1e-6


def build_nc(taps=False, nheads=NH, max_phase=99):
    nc = bass.Bass("TRN2", target_bir_lowering=False)

    x_full = nc.dram_tensor("x_full", [S, D], F32, kind="ExternalInput").ap()
    x_loc = nc.dram_tensor("x_loc", [R, D], F32, kind="ExternalInput").ap()
    w_qa = nc.dram_tensor("w_qa", [D, QR], F32, kind="ExternalInput").ap()
    g_qa = nc.dram_tensor("g_qa", [1, QR], F32, kind="ExternalInput").ap()
    w_qb = nc.dram_tensor("w_qb", [QR, NH * QK], F32, kind="ExternalInput").ap()
    w_kva = nc.dram_tensor("w_kva", [D, KVR + RD], F32, kind="ExternalInput").ap()
    g_kva = nc.dram_tensor("g_kva", [1, KVR], F32, kind="ExternalInput").ap()
    w_kvb = nc.dram_tensor("w_kvb", [KVR, NH * (ND + VD)], F32, kind="ExternalInput").ap()
    w_o = nc.dram_tensor("w_o", [D, D], F32, kind="ExternalInput").ap()
    w1 = nc.dram_tensor("w1", [D, FF], F32, kind="ExternalInput").ap()
    b1r = nc.dram_tensor("b1r", [128, 128], F32, kind="ExternalInput").ap()
    w2 = nc.dram_tensor("w2", [FF, D], F32, kind="ExternalInput").ap()
    b2r = nc.dram_tensor("b2r", [1, D], F32, kind="ExternalInput").ap()
    cosk = nc.dram_tensor("cosk", [S, RD], F32, kind="ExternalInput").ap()
    sink = nc.dram_tensor("sink", [S, RD], F32, kind="ExternalInput").ap()
    cosqT = nc.dram_tensor("cosqT", [RD, R], F32, kind="ExternalInput").ap()
    sinqT = nc.dram_tensor("sinqT", [RD, R], F32, kind="ExternalInput").ap()
    identc = nc.dram_tensor("identc", [128, 128], BF16, kind="ExternalInput").ap()
    epsc = nc.dram_tensor("epsc", [128, 1], F32, kind="ExternalInput").ap()
    onesfc = nc.dram_tensor("onesfc", [1, 128], F32, kind="ExternalInput").ap()
    onesf2c = nc.dram_tensor("onesf2c", [128, 128], F32, kind="ExternalInput").ap()
    onesc = nc.dram_tensor("onesc", [128, 128], BF16, kind="ExternalInput").ap()

    out_rows = nc.dram_tensor("out_rows", [R, D], F32, kind="ExternalOutput").ap()

    d_OT = nc.dram_tensor("d_OT", [NH, 128, R], BF16, kind="Internal").ap()
    d_attn = nc.dram_tensor("d_attn", [R, D], F32, kind="Internal").ap()
    d_x2 = nc.dram_tensor("d_x2", [R, D], F32, kind="Internal").ap()
    d_hT = nc.dram_tensor("d_hT", [FF, R], BF16, kind="Internal").ap()
    d_out2 = nc.dram_tensor("d_out2", [R, D], F32, kind="Internal").ap()

    tap_aps = {}
    if taps:
        for name, shape, dt in [
            ("t_qcT", [128, 12 * R], BF16),
            ("t_kcT", [128, 4 * S], BF16),
            ("t_krT", [65, S], BF16),
            ("t_kT0", [128, S], BF16),
            ("t_V0", [128, 64 * VD], BF16),
            ("t_OT0", [128, R], BF16),
            ("t_x2", [R, D], F32),
            ("t_hT0", [128, R], BF16),
            ("t_attn", [R, D], F32),
            ("t_out2", [R, D], F32),
        ]:
            tap_aps[name] = nc.dram_tensor(name, shape, dt, kind="ExternalOutput").ap()

    with ExitStack() as ctx0:
        E0 = ctx0.enter_context

        sems = {}

        def sem(name):
            if name not in sems:
                sems[name] = E0(nc.semaphore(name))
            return sems[name]

        PH = sem("ph")
        CLD = sem("c_ld")
        OST = sem("o_st")
        OST1 = sem("o_st1")
        LA, LB = sem("la"), sem("lb")
        LC, LD2 = sem("lc"), sem("ld2")
        LE, LF = sem("le"), sem("lf")
        ST0, ST1 = sem("st0"), sem("st1")
        SA = sem("sa")
        SG = sem("sg")

        WLD = sem("w_ld")
        cld_n = [0]
        wld_n = [0]
        ph_n = [0]

        ident_sb = E0(nc.sbuf_tensor("ident_sb", [128, 128], BF16))
        eps_sb = E0(nc.sbuf_tensor("eps_sb", [128, 1], F32))
        onesf_sb = E0(nc.sbuf_tensor("onesf_sb", [1, 128], F32))
        onesf2_sb = E0(nc.sbuf_tensor("onesf2_sb", [128, 128], F32))
        ones_sb = E0(nc.sbuf_tensor("ones_sb", [128, 128], BF16))
        junk = E0(nc.sbuf_tensor("junk", [128, 2048], BF16))
        stats = E0(nc.sbuf_tensor("stats", [128, 32], F32))

        block = E0(nc.Block())
        ENG = ["tensor", "scalar", "vector", "sync", "gpsimd"]

        VFS = sem("v_fence")
        SFS = sem("s_fence")
        fence_n = {"v": 0, "s": 0}

        def vfence(v, inst):
            fence_n["v"] += 1
            inst.then_inc(VFS, 1)
            v.wait_ge(VFS, fence_n["v"])

        def sfence(s, inst):
            fence_n["s"] += 1
            inst.then_inc(SFS, 1)
            s.wait_ge(SFS, fence_n["s"])

        def barrier():
            target = ph_n[0] + len(ENG)

            @block.scalar
            def _(s):
                s.activation(stats[:, 30:31], eps_sb[:], AF.Copy).then_inc(PH, 1)
                s.wait_ge(PH, target)

            @block.vector
            def _(v):
                v.memset(stats[:, 31:32], 0.0).then_inc(PH, 1)
                v.wait_ge(PH, target)

            for name in ["tensor", "sync", "gpsimd"]:
                def mk():
                    def _(eng):
                        eng.sem_inc(PH, 1)
                        eng.wait_ge(PH, target)
                    return _
                getattr(block, name)(mk())
            ph_n[0] = target

        @block.sync
        def _(sync):
            sync.dma_start(out=ident_sb[:], in_=identc).then_inc(CLD, 16)
            sync.dma_start(out=ones_sb[:], in_=onesc).then_inc(CLD, 16)
            sync.dma_start(out=eps_sb[:], in_=epsc).then_inc(CLD, 16)
            sync.dma_start(out=onesf_sb[:], in_=onesfc).then_inc(CLD, 16)
            sync.dma_start(out=onesf2_sb[:], in_=onesf2c).then_inc(CLD, 16)
        cld_n[0] += 80

        d_qcT = nc.dram_tensor("d_qcT", [128, 12 * R], BF16, kind="Internal").ap()
        QSP = sem("q_spill")
        with ExitStack() as ctxA:
            EA = ctxA.enter_context
            if True:
                with ExitStack() as ctxQ:
                    EQ = ctxQ.enter_context
                    qcT = EQ(nc.sbuf_tensor("qcT", [128, 12 * R], BF16))
                    xt = [EQ(nc.sbuf_tensor("xt0", [128, D], F32))] * 2
                    xcb = [EQ(nc.sbuf_tensor("xcb0", [128, D], BF16))] * 2
                    xcT = [EQ(nc.sbuf_tensor("xcT0", [128, D], BF16))] * 2
                    wqa_sb = EQ(nc.sbuf_tensor("wqa_sb", [128, 32 * QR], BF16))
                    gqa_b = EQ(nc.sbuf_tensor("gqa_b", [128, QR], F32))
                    qln = EQ(nc.sbuf_tensor("qln", [128, QR], F32))
                    qcb = [EQ(nc.sbuf_tensor("qcb0", [128, QR], BF16))] * 2
                    p_aq = EQ(nc.psum_tensor("p_aq", [128, QR], F32))
                    p_tr = [EQ(nc.psum_tensor(f"p_tr{i}", [128, 1024], F32)) for i in range(2)]

                    NT = 8
                    SQ = sem("q_sq")
                    VXC = sem("q_xc")
                    VVAR = sem("q_var")
                    MTR = sem("q_mtr")
                    EXT = sem("q_ext")
                    MAQ = sem("q_maq")
                    EQL = sem("q_eql")
                    SUQ = sem("q_suq")
                    VQC = sem("q_vqc")
                    MQT = sem("q_mqt")
                    EQT = sem("q_eqt")
                    VSS = sem("q_vss")
                    SUQ2 = sem("q_suq2")

                    @block.gpsimd
                    def _(gp):
                        for s4 in range(4):
                            gp.dma_start(
                                out=wqa_sb[:].rearrange("p (s n) -> p s n", s=32)[:, s4 * 8:(s4 + 1) * 8, :],
                                in_=w_qa.rearrange("(s p) n -> p s n", p=128)[:, s4 * 8:(s4 + 1) * 8, :],
                            ).then_inc(WLD, 16)
                    wld_n[0] += 64

                    @block.sync
                    def _(sync):
                        sync.dma_start(out=gqa_b[:], in_=g_qa.broadcast_to((128, QR))).then_inc(CLD, 16)
                        for t in range(NT):
                            if t >= 1:
                                sync.wait_ge(VXC, t)
                                sync.wait_ge(SQ, t)
                            sync.dma_start(out=xt[0][:], in_=x_loc[t * 128:(t + 1) * 128, :]).then_inc(LA, 16)
                        sync.wait_ge(EQT, 16)
                        sync.dma_start(out=d_qcT, in_=qcT[:]).then_inc(QSP, 16)
                        sync.wait_ge(QSP, 16)
                    cld_n[0] += 16

                    @block.scalar
                    def _(s):
                        s.wait_ge(CLD, cld_n[0])
                        s.wait_ge(WLD, wld_n[0])
                        for t in range(NT):
                            par = t % 2
                            par2 = (t % 2) * 22
                            s.wait_ge(LA, 16 * (t + 1))
                            sfence(s, s.activation(junk[:], xt[0][:, 0:2048], AF.Square,
                                         accum_out=stats[:, 6 + par2:7 + par2]))
                            s.activation(junk[:], xt[0][:, 2048:4096], AF.Square,
                                         accum_out=stats[:, 7 + par2:8 + par2]).then_inc(SQ, 1)
                            s.wait_ge(VVAR, t + 1)
                            sfence(s, s.activation(stats[:, 12:13], stats[:, 10 + par:11 + par], AF.Ln, bias=eps_sb[:]))
                            s.activation(stats[:, 14 + par:15 + par], stats[:, 12:13], AF.Exp, scale=-0.5)
                            if t >= 1:
                                s.wait_ge(MAQ, t)
                            for g in range(4):
                                s.wait_ge(MTR, t * 4 + g + 1)
                                s.copy(xcT[par][:, g * 1024:(g + 1) * 1024], p_tr[g % 2][:]).then_inc(EXT, 1)
                            s.wait_ge(MAQ, t + 1)
                            if t >= 1:
                                s.wait_ge(VQC, t)
                            s.activation(qln[:], p_aq[:], AF.Copy,
                                         scale=stats[:, 14 + par:15 + par]).then_inc(EQL, 1)
                            s.wait_ge(EQL, t + 1)
                            s.activation(junk[:, 0:QR // 2], qln[:, 0:QR // 2], AF.Square, accum_out=stats[:, 22:23])
                            s.activation(junk[:, QR // 2:QR], qln[:, QR // 2:QR], AF.Square,
                                         accum_out=stats[:, 23:24]).then_inc(SUQ, 1)
                            s.wait_ge(VSS, t + 1)
                            sfence(s, s.activation(stats[:, 24:25], stats[:, 27:28], AF.Ln))
                            s.activation(stats[:, 16:17], stats[:, 24:25], AF.Exp, scale=0.5).then_inc(SUQ2, 1)
                            for g in range(2):
                                s.wait_ge(MQT, t * 2 + g + 1)
                                nj = 8 if g == 0 else 4
                                for j8 in range(nj):
                                    j = g * 8 + j8
                                    last = s.copy(qcT[:, j * R + t * 128:j * R + (t + 1) * 128],
                                                  p_tr[g % 2][:, j8 * 128:(j8 + 1) * 128])
                                last.then_inc(EQT, 1)

                    @block.vector
                    def _(v):
                        for t in range(NT):
                            par = t % 2
                            v.wait_ge(LA, 16 * (t + 1))
                            vfence(v, v.reduce_sum(stats[:, 0:1], xt[0][:], axis=AX.X, negate=True))
                            vfence(v, v.tensor_scalar(stats[:, 2 + par:3 + par], stats[:, 0:1], 1.0 / D, None, ALU.mult))
                            if t >= 1:
                                v.wait_ge(MTR, (t - 1) * 4 + 4)
                            v.tensor_scalar(xcb[0][:], xt[0][:], stats[:, 2 + par:3 + par],
                                            None, ALU.add).then_inc(VXC, 1)
                            v.tensor_mul(stats[:, 4:5], stats[:, 2 + par:3 + par], stats[:, 2 + par:3 + par])
                            par2 = (t % 2) * 22
                            v.wait_ge(SQ, t + 1)
                            vfence(v, v.tensor_add(stats[:, 5:6], stats[:, 6 + par2:7 + par2], stats[:, 7 + par2:8 + par2]))
                            vfence(v, v.tensor_scalar(stats[:, 8:9], stats[:, 5:6], 1.0 / D, None, ALU.mult))
                            v.tensor_sub(stats[:, 10 + par:11 + par], stats[:, 8:9], stats[:, 4:5]).then_inc(VVAR, 1)
                            v.wait_ge(SUQ, t + 1)
                            vfence(v, v.tensor_add(stats[:, 26:27], stats[:, 22:23], stats[:, 23:24]))
                            v.tensor_scalar(stats[:, 27:28], stats[:, 26:27], 1.0 / QR, None, ALU.mult).then_inc(VSS, 1)
                            v.wait_ge(SUQ2, t + 1)
                            vfence(v, v.tensor_scalar(stats[:, 20:21], stats[:, 16:17], RMS_EPS, None, ALU.add))
                            vfence(v, v.reciprocal(stats[:, 18:19], stats[:, 20:21]))
                            if t >= 1:
                                v.wait_ge(MQT, (t - 1) * 2 + 2)
                            v.scalar_tensor_tensor(qcb[0][:], qln[:], stats[:, 18:19], gqa_b[:],
                                                   ALU.mult, ALU.mult).then_inc(VQC, 1)

                    @block.tensor
                    def _(tn):
                        tn.wait_ge(CLD, cld_n[0])
                        tn.wait_ge(WLD, wld_n[0])
                        for t in range(NT):
                            par = 0
                            tn.wait_ge(VXC, t + 1)
                            for g in range(4):
                                if g >= 2:
                                    tn.wait_ge(EXT, t * 4 + g - 1)
                                elif t >= 1:
                                    if g == 0:
                                        tn.wait_ge(EQT, (t - 1) * 2 + 1)
                                    else:
                                        tn.wait_ge(EQT, (t - 1) * 2 + 2)
                                for j8 in range(8):
                                    j = g * 8 + j8
                                    mm = tn.matmul(p_tr[g % 2][:, j8 * 128:(j8 + 1) * 128],
                                                   xcb[par][:, j * 128:(j + 1) * 128], ident_sb[:],
                                                   start=True, stop=True)
                                mm.then_inc(MTR, 1)
                            tn.wait_ge(EXT, t * 4 + 4)
                            if t >= 1:
                                tn.wait_ge(EQL, t)
                            for nch in range(3):
                                for j in range(32):
                                    mm = tn.matmul(p_aq[:, nch * 512:(nch + 1) * 512],
                                                   xcT[par][:, j * 128:(j + 1) * 128],
                                                   wqa_sb[:, j * QR + nch * 512:j * QR + nch * 512 + 512],
                                                   start=(j == 0), stop=(j == 31))
                            mm.then_inc(MAQ, 1)
                            tn.wait_ge(VQC, t + 1)
                            for g in range(2):
                                tn.wait_ge(EXT, t * 4 + 3 + g)
                                nj = 8 if g == 0 else 4
                                for j8 in range(nj):
                                    j = g * 8 + j8
                                    mm = tn.matmul(p_tr[g % 2][:, j8 * 128:(j8 + 1) * 128],
                                                   qcb[par][:, j * 128:(j + 1) * 128], ident_sb[:],
                                                   start=True, stop=True)
                                mm.then_inc(MQT, 1)

                    barrier()

                kcT = EA(nc.sbuf_tensor("kcT", [128, 4 * S], BF16))
                krT = EA(nc.sbuf_tensor("krT", [65, S], BF16))
                with ExitStack() as ctxKV:
                    EKV = ctxKV.enter_context
                    xt = [EKV(nc.sbuf_tensor(f"kxt{i}", [128, D], F32)) for i in range(2)]
                    xcb = [EKV(nc.sbuf_tensor(f"kxcb{i}", [128, D], BF16)) for i in range(2)]
                    xcT = [EKV(nc.sbuf_tensor(f"kxcT{i}", [128, D], BF16)) for i in range(2)]
                    wkva_sb = EKV(nc.sbuf_tensor("wkva_sb", [128, 32 * (KVR + RD)], BF16))
                    gkva_b = EKV(nc.sbuf_tensor("gkva_b", [128, KVR], F32))
                    kvln = EKV(nc.sbuf_tensor("kvln", [128, KVR + RD], F32))
                    kcb = [EKV(nc.sbuf_tensor(f"kcb{i}", [128, KVR], BF16)) for i in range(2)]
                    ckb = [EKV(nc.sbuf_tensor(f"ckb{i}", [128, RD], F32)) for i in range(2)]
                    skb = [EKV(nc.sbuf_tensor(f"skb{i}", [128, RD], F32)) for i in range(2)]
                    krf = EKV(nc.sbuf_tensor("krf", [128, RD], BF16))
                    krt1 = EKV(nc.sbuf_tensor("krt1", [128, RD], F32))
                    krt2 = EKV(nc.sbuf_tensor("krt2", [128, RD], F32))
                    p_akv = EKV(nc.psum_tensor("p_akv", [128, 1024], F32))
                    p_t2 = [EKV(nc.psum_tensor(f"p_t2{i}", [128, 1024], F32)) for i in range(2)]

                    NT2 = 64
                    SQ2 = sem("k_sq")
                    VXC2 = sem("k_xc")
                    VVAR2 = sem("k_var")
                    MTR2 = sem("k_mtr")
                    EXT2 = sem("k_ext")
                    MKV = sem("k_mkv")
                    EKL = sem("k_ekl")
                    SUK = sem("k_suk")
                    VKC = sem("k_vkc")
                    VKR = sem("k_vkr")
                    MKT = sem("k_mkt")
                    EKT = sem("k_ekt")

                    @block.gpsimd
                    def _(gp):
                        for s2 in range(2):
                            gp.dma_start(
                                out=wkva_sb[:].rearrange("p (s n) -> p s n", s=32)[:, s2 * 16:(s2 + 1) * 16, :],
                                in_=w_kva.rearrange("(s p) n -> p s n", p=128)[:, s2 * 16:(s2 + 1) * 16, :],
                            ).then_inc(WLD, 16)
                    wld_n[0] += 32

                    @block.vector
                    def _(v):
                        v.memset(krT[64:65, :], -1.0).then_inc(VKR, 1)

                    @block.sync
                    def _(sync):
                        sync.dma_start(out=gkva_b[:], in_=g_kva.broadcast_to((128, KVR))).then_inc(CLD, 16)
                        for t in range(NT2):
                            par = t % 2
                            sl = [LA, LB][par]
                            cl = [LC, LD2][par]
                            if t >= 2:
                                sync.wait_ge(VXC2, t - 1)
                                sync.wait_ge(SQ2, t - 1)
                            sync.dma_start(out=xt[par][:], in_=x_full[t * 128:(t + 1) * 128, :]).then_inc(sl, 16)
                            if t >= 2:
                                sync.wait_ge(VKR, t)
                            sync.dma_start(out=ckb[par][:], in_=cosk[t * 128:(t + 1) * 128, :]).then_inc(cl, 16)
                            sync.dma_start(out=skb[par][:], in_=sink[t * 128:(t + 1) * 128, :]).then_inc(cl, 16)
                    cld_n[0] += 16

                    XB1 = 16 * 8
                    XB2 = 0

                    @block.scalar
                    def _(s):
                        s.wait_ge(CLD, cld_n[0])
                        s.wait_ge(WLD, wld_n[0])
                        for t in range(NT2):
                            par = t % 2
                            sl = [LA, LB][par]
                            base = [XB1, XB2][par]
                            par2 = par * 22
                            s.wait_ge(sl, base + 16 * (t // 2 + 1))
                            sfence(s, s.activation(junk[:], xt[par][:, 0:2048], AF.Square,
                                         accum_out=stats[:, 6 + par2:7 + par2]))
                            s.activation(junk[:], xt[par][:, 2048:4096], AF.Square,
                                         accum_out=stats[:, 7 + par2:8 + par2]).then_inc(SQ2, 1)
                            s.wait_ge(VVAR2, t + 1)
                            sfence(s, s.activation(stats[:, 12:13], stats[:, 10 + par:11 + par], AF.Ln, bias=eps_sb[:]))
                            s.activation(stats[:, 14 + par:15 + par], stats[:, 12:13], AF.Exp, scale=-0.5)
                            if t >= 2:
                                s.wait_ge(MKV, t - 1)
                            for g in range(4):
                                s.wait_ge(MTR2, t * 4 + g + 1)
                                s.copy(xcT[par][:, g * 1024:(g + 1) * 1024], p_t2[g % 2][:]).then_inc(EXT2, 1)
                            s.wait_ge(MKV, t + 1)
                            if t >= 1:
                                s.wait_ge(VKC, t)
                                s.wait_ge(VKR, t + 1)
                            s.activation(kvln[:], p_akv[:, 0:KVR + RD], AF.Copy,
                                         scale=stats[:, 14 + par:15 + par]).then_inc(EKL, 1)
                            s.wait_ge(EKL, t + 1)
                            sfence(s, s.activation(junk[:, 0:KVR], kvln[:, 0:KVR], AF.Square, accum_out=stats[:, 22:23]))
                            sfence(s, s.activation(stats[:, 24:25], stats[:, 22:23], AF.Ln, scale=1.0 / KVR))
                            s.activation(stats[:, 16:17], stats[:, 24:25], AF.Exp, scale=0.5).then_inc(SUK, 1)
                            s.wait_ge(MKT, t + 1)
                            for j4 in range(4):
                                s.copy(kcT[:, j4 * S + t * 128:j4 * S + (t + 1) * 128],
                                       p_t2[0][:, j4 * 128:(j4 + 1) * 128])
                            s.copy(krT[0:64, t * 128:(t + 1) * 128],
                                   p_t2[0][0:64, 512:640]).then_inc(EKT, 1)

                    @block.vector
                    def _(v):
                        for t in range(NT2):
                            par = t % 2
                            sl = [LA, LB][par]
                            cl = [LC, LD2][par]
                            base = [XB1, XB2][par]
                            v.wait_ge(sl, base + 16 * (t // 2 + 1))
                            vfence(v, v.reduce_sum(stats[:, 0:1], xt[par][:], axis=AX.X, negate=True))
                            vfence(v, v.tensor_scalar(stats[:, 2 + par:3 + par], stats[:, 0:1], 1.0 / D, None, ALU.mult))
                            if t >= 2:
                                v.wait_ge(MTR2, (t - 2) * 4 + 4)
                            v.tensor_scalar(xcb[par][:], xt[par][:], stats[:, 2 + par:3 + par],
                                            None, ALU.add).then_inc(VXC2, 1)
                            v.tensor_mul(stats[:, 4:5], stats[:, 2 + par:3 + par], stats[:, 2 + par:3 + par])
                            par2 = par * 22
                            v.wait_ge(SQ2, t + 1)
                            vfence(v, v.tensor_add(stats[:, 5:6], stats[:, 6 + par2:7 + par2], stats[:, 7 + par2:8 + par2]))
                            vfence(v, v.tensor_scalar(stats[:, 8:9], stats[:, 5:6], 1.0 / D, None, ALU.mult))
                            v.tensor_sub(stats[:, 10 + par:11 + par], stats[:, 8:9], stats[:, 4:5]).then_inc(VVAR2, 1)
                            v.wait_ge(SUK, t + 1)
                            vfence(v, v.tensor_scalar(stats[:, 20:21], stats[:, 16:17], RMS_EPS, None, ALU.add))
                            vfence(v, v.reciprocal(stats[:, 18:19], stats[:, 20:21]))
                            if t >= 2:
                                v.wait_ge(MKT, t - 1)
                            v.scalar_tensor_tensor(kcb[par][:], kvln[:, 0:KVR], stats[:, 18:19], gkva_b[:],
                                                   ALU.mult, ALU.mult).then_inc(VKC, 1)
                            v.wait_ge(cl, 32 * (t // 2 + 1))
                            if t >= 1:
                                v.wait_ge(MKT, t)
                            v.tensor_scalar(krt1[:, 0:32], kvln[:, KVR + 32:KVR + 64], -1.0, None, ALU.mult)
                            vfence(v, v.tensor_copy(krt1[:, 32:64], kvln[:, KVR:KVR + 32]))
                            v.tensor_mul(krt2[:], kvln[:, KVR:KVR + RD], ckb[par][:])
                            vfence(v, v.tensor_mul(krt1[:], krt1[:], skb[par][:]))
                            v.tensor_add(krf[:], krt2[:], krt1[:]).then_inc(VKR, 1)

                    @block.tensor
                    def _(tn):
                        tn.wait_ge(CLD, cld_n[0])
                        tn.wait_ge(WLD, wld_n[0])
                        for t in range(NT2):
                            par = t % 2
                            tn.wait_ge(VXC2, t + 1)
                            for g in range(4):
                                if g >= 2:
                                    tn.wait_ge(EXT2, t * 4 + g - 1)
                                elif t >= 1:
                                    if g == 0:
                                        tn.wait_ge(EKT, t)
                                    else:
                                        tn.wait_ge(EXT2, (t - 1) * 4 + 4)
                                for j8 in range(8):
                                    j = g * 8 + j8
                                    mm = tn.matmul(p_t2[g % 2][:, j8 * 128:(j8 + 1) * 128],
                                                   xcb[par][:, j * 128:(j + 1) * 128], ident_sb[:],
                                                   start=True, stop=True)
                                mm.then_inc(MTR2, 1)
                            tn.wait_ge(EXT2, t * 4 + 4)
                            if t >= 1:
                                tn.wait_ge(EKL, t)
                            for nch in range(2):
                                lo = nch * 512
                                wdt = 512 if nch == 0 else 64
                                for j in range(32):
                                    mm = tn.matmul(p_akv[:, lo:lo + wdt],
                                                   xcT[par][:, j * 128:(j + 1) * 128],
                                                   wkva_sb[:, j * (KVR + RD) + lo:j * (KVR + RD) + lo + wdt],
                                                   start=(j == 0), stop=(j == 31))
                            mm.then_inc(MKV, 1)
                            tn.wait_ge(VKC, t + 1)
                            tn.wait_ge(VKR, t + 2)
                            tn.wait_ge(EXT2, t * 4 + 3)
                            tn.wait_ge(EKT, t)
                            for j4 in range(4):
                                tn.matmul(p_t2[0][:, j4 * 128:(j4 + 1) * 128],
                                          kcb[par][:, j4 * 128:(j4 + 1) * 128], ident_sb[:],
                                          start=True, stop=True)
                            mm = tn.matmul(p_t2[0][0:64, 512:640], krf[:], ident_sb[:],
                                           start=True, stop=True)
                            mm.then_inc(MKT, 1)

                    barrier()

            if taps:
                @block.sync
                def _(sync):
                    sync.wait_ge(QSP, 16)
                    sync.dma_start(out=tap_aps["t_qcT"], in_=d_qcT).then_inc(SG, 16)
                    sync.dma_start(out=tap_aps["t_kcT"], in_=kcT[:]).then_inc(SG, 16)
                    sync.dma_start(out=tap_aps["t_krT"], in_=krT[:]).then_inc(SG, 16)
                    sync.wait_ge(SG, 48)
                barrier()
            if max_phase <= 1:
                return nc

            with ExitStack() as ctxT:
                ET = ctxT.enter_context
                qcT2 = ET(nc.sbuf_tensor("qcT2", [128, 12 * R], BF16))
                wqb_sb = [ET(nc.sbuf_tensor(f"wqb{i}", [128, 12 * QK], BF16)) for i in range(2)]
                wkvb_sb = [ET(nc.sbuf_tensor(f"wkvb{i}", [128, 4 * (ND + VD)], BF16)) for i in range(2)]
                cqT = ET(nc.sbuf_tensor("cqT", [RD, R], F32))
                sqT = ET(nc.sbuf_tensor("sqT", [RD, R], F32))
                qT_nope = ET(nc.sbuf_tensor("qT_nope", [128, R], BF16))
                qT_s2 = ET(nc.sbuf_tensor("qT_s2", [65, R], BF16))
                qrr = ET(nc.sbuf_tensor("qrr", [RD, R], F32))
                rt1 = ET(nc.sbuf_tensor("rt1", [RD, R], F32))
                kT_nope = ET(nc.sbuf_tensor("kT_nope", [128, S], BF16))
                Vb = ET(nc.sbuf_tensor("Vb", [128, 64 * VD], BF16))
                sqb = [ET(nc.sbuf_tensor(f"sqb{i}", [128, 512], BF16)) for i in range(2)]
                sqb2 = [ET(nc.sbuf_tensor(f"sqb2{i}", [65, 512], BF16)) for i in range(2)]
                PT = ET(nc.sbuf_tensor("PT", [128, 4 * 1024], BF16))
                maxb = ET(nc.sbuf_tensor("maxb", [128, 18], F32))
                lkv = ET(nc.sbuf_tensor("lkv", [1, 4], F32))
                nqln = ET(nc.sbuf_tensor("nqln", [1, 1024], F32))
                lacc = ET(nc.sbuf_tensor("lacc", [128, 1024], F32))
                rlb = ET(nc.sbuf_tensor("rlb", [128, 1024], F32))
                OT_h = ET(nc.sbuf_tensor("OT_h", [128, R], BF16))
                p_sp = [ET(nc.psum_tensor(f"p_sp{i}", [128, 1024], F32)) for i in range(2)]
                p_pvT = ET(nc.psum_tensor("p_pvT", [128, 1024], F32))
                p_gen = [ET(nc.psum_tensor(f"p_gen{i}", [128, 512], F32)) for i in range(2)]

                AKGM = sem("a_kgm")
                AKGE = sem("a_kge")
                ASQK = sem("a_sqk")
                ANKM = sem("a_nkm")
                ANKR = sem("a_nkr")
                AMK = sem("a_mk")
                ALK = sem("a_lk")
                ALKH = sem("a_lkh")
                AVGM = sem("a_vgm")
                AVGE = sem("a_vge")
                AQGM = sem("a_qgm")
                AQGE = sem("a_qge")
                AQRM = sem("a_qrm")
                AQRE = sem("a_qre")
                AROP = sem("a_rop")
                ASQQ = sem("a_sqq")
                ANQM = sem("a_nqm")
                AMRW = sem("a_mrw")
                ASP = sem("a_sp")
                AEXP = sem("a_exp")
                APV = sem("a_pv")
                ALC = sem("a_lc")
                ABC = sem("a_bc")
                AOTE = sem("a_ote")

                @block.sync
                def _(sync):
                    sync.dma_start(out=cqT[:], in_=cosqT).then_inc(CLD, 16)
                    sync.dma_start(out=sqT[:], in_=sinqT).then_inc(CLD, 16)
                    sync.wait_ge(QSP, 16)
                    sync.dma_start(out=qcT2[:], in_=d_qcT).then_inc(CLD, 16)
                cld_n[0] += 48

                @block.gpsimd
                def _(gp):
                    for h in range(nheads):
                        par = h % 2
                        sl = [LE, LF][par]
                        if h >= 2:
                            gp.wait_ge(AQRM, (h - 2) * 2 + 2)
                            gp.wait_ge(AVGM, (h - 2) * 64 + 64)
                        gp.dma_start(
                            out=wqb_sb[par][:].rearrange("p (s n) -> p s n", s=12),
                            in_=w_qb[:, h * QK:(h + 1) * QK].rearrange("(s p) n -> p s n", p=128),
                        ).then_inc(sl, 16)
                        gp.dma_start(
                            out=wkvb_sb[par][:].rearrange("p (s n) -> p s n", s=4),
                            in_=w_kvb[:, h * (ND + VD):(h + 1) * (ND + VD)].rearrange("(s p) n -> p s n", p=128),
                        ).then_inc(sl, 16)

                @block.tensor
                def _(tn):
                    tn.wait_ge(CLD, cld_n[0])
                    tn.wait_ge(WLD, wld_n[0])
                    for h in range(nheads):
                        par = h % 2
                        sl = [LE, LF][par]
                        tn.wait_ge(sl, 32 * (h // 2 + 1))
                        hb16 = h * 16
                        hb64 = h * 64
                        hb2 = h * 2
                        for c in range(16):
                            gb = p_gen[c % 2]
                            if c >= 2:
                                tn.wait_ge(AKGE, hb16 + c - 1)
                            elif h >= 1:
                                tn.wait_ge(AMRW, (h - 1) * 2 + (c % 2) + 1)
                            for j in range(4):
                                mm = tn.matmul(gb[:],
                                               wkvb_sb[par][:, j * (ND + VD):j * (ND + VD) + ND],
                                               kcT[:, j * S + c * 512:j * S + c * 512 + 512],
                                               start=(j == 0), stop=(j == 3))
                            mm.then_inc(AKGM, 1)
                        for c in range(16):
                            gb = p_gen[c % 2]
                            tn.wait_ge(ASQK, hb16 + c + 1)
                            if c >= 2:
                                tn.wait_ge(ANKR, hb16 + c - 1)
                            else:
                                tn.wait_ge(AKGE, hb16 + 15 + c)
                            mm = tn.matmul(gb[:], ones_sb[:], sqb[c % 2][:], start=True, stop=False)
                            mm = tn.matmul(gb[:], ones_sb[0:65, :], sqb2[c % 2][:],
                                           start=False, stop=True)
                            mm.then_inc(ANKM, 1)
                        for kt in range(64):
                            gb = p_gen[kt % 2]
                            if kt >= 2:
                                tn.wait_ge(AVGE, hb64 + kt - 1)
                            else:
                                tn.wait_ge(ANKR, hb16 + 15 + kt)
                            for j in range(4):
                                mm = tn.matmul(gb[:, 0:128],
                                               kcT[:, j * S + kt * 128:j * S + (kt + 1) * 128],
                                               wkvb_sb[par][:, j * (ND + VD) + ND:j * (ND + VD) + ND + VD],
                                               start=(j == 0), stop=(j == 3))
                            mm.then_inc(AVGM, 1)
                        for qc in range(2):
                            gb = p_gen[qc % 2]
                            tn.wait_ge(AVGE, hb64 + 63 + qc)
                            for j in range(12):
                                mm = tn.matmul(gb[:],
                                               wqb_sb[par][:, j * QK:j * QK + ND],
                                               qcT2[:, j * R + qc * 512:j * R + qc * 512 + 512],
                                               start=(j == 0), stop=(j == 11))
                            mm.then_inc(AQGM, 1)
                        for qc in range(2):
                            gb = p_gen[qc % 2]
                            tn.wait_ge(AQGE, hb2 + qc + 1)
                            for j in range(12):
                                mm = tn.matmul(gb[0:64, :],
                                               wqb_sb[par][:, j * QK + ND:j * QK + QK],
                                               qcT2[:, j * R + qc * 512:j * R + qc * 512 + 512],
                                               start=(j == 0), stop=(j == 11))
                            mm.then_inc(AQRM, 1)
                        for qc in range(2):
                            gb = p_gen[qc % 2]
                            tn.wait_ge(ASQQ, h * 4 + qc * 2 + 2)
                            tn.wait_ge(AQRE, hb2 + qc + 1)
                            mm = tn.matmul(gb[:], ones_sb[:], sqb[0][:], start=True, stop=False)
                            mm = tn.matmul(gb[:], ones_sb[0:64, :], sqb[1][0:64, :], start=False, stop=True)
                            mm.then_inc(ANQM, 1)
                        for kt in range(64):
                            spb = p_sp[kt % 2]
                            if kt >= 2:
                                tn.wait_ge(AEXP, hb64 + kt - 1)
                            elif h >= 1:
                                tn.wait_ge(AOTE, h)
                            if kt == 0:
                                tn.wait_ge(AMRW, hb2 + 2)
                                tn.wait_ge(AROP, hb2 + 2)
                                tn.wait_ge(AKGE, hb16 + 16)
                                tn.wait_ge(AVGE, hb64 + 64)
                            ks = kT_nope[:, kt * 128:(kt + 1) * 128]
                            rs = krT[:, kt * 128:(kt + 1) * 128]
                            tn.matmul(spb[:, 0:512], ks, qT_nope[:, 0:512], start=True, stop=False)
                            tn.matmul(spb[:, 512:1024], ks, qT_nope[:, 512:1024], start=True, stop=False)
                            tn.matmul(spb[:, 0:512], rs, qT_s2[:, 0:512], start=False, stop=True)
                            mm = tn.matmul(spb[:, 512:1024], rs, qT_s2[:, 512:1024], start=False, stop=True)
                            mm.then_inc(ASP, 1)
                            tn.wait_ge(AEXP, hb64 + kt + 1)
                            vs = Vb[:, kt * VD:(kt + 1) * VD]
                            slot = (kt % 4) * 1024
                            tn.matmul(p_pvT[:, 0:512], vs, PT[:, slot:slot + 512],
                                      start=(kt == 0), stop=(kt == 63))
                            mm = tn.matmul(p_pvT[:, 512:1024], vs, PT[:, slot + 512:slot + 1024],
                                           start=(kt == 0), stop=(kt == 63))
                            mm.then_inc(APV, 1)
                        tn.wait_ge(ALC, hb64 + 64)
                        tn.wait_ge(AEXP, hb64 + 64)
                        tn.matmul(p_sp[0][:, 0:512], onesf2_sb[:], lacc[:, 0:512], start=True, stop=True)
                        mm = tn.matmul(p_sp[0][:, 512:1024], onesf2_sb[:], lacc[:, 512:1024], start=True, stop=True)
                        mm.then_inc(ABC, 1)

                @block.scalar
                def _(s):
                    for h in range(nheads):
                        hb16, hb64, hb2 = h * 16, h * 64, h * 2
                        for c in range(16):
                            s.wait_ge(AKGM, hb16 + c + 1)
                            s.copy(kT_nope[:, c * 512:(c + 1) * 512], p_gen[c % 2][:]).then_inc(AKGE, 1)
                        s.wait_ge(AMK, h + 1)
                        s.activation(lkv[0:1, 0:1], maxb[0:1, 16:17], AF.Ln).then_inc(ALK, 1)
                        for qc in range(2):
                            s.wait_ge(AQGM, hb2 + qc + 1)
                            s.activation(qT_nope[:, qc * 512:qc * 512 + 512], p_gen[qc % 2][:],
                                         AF.Copy, scale=SC).then_inc(AQGE, 1)
                        for qc in range(2):
                            s.wait_ge(AQRM, hb2 + qc + 1)
                            s.activation(qrr[:, qc * 512:qc * 512 + 512], p_gen[qc % 2][0:64, :],
                                         AF.Copy, scale=SC).then_inc(AQRE, 1)
                        for qc in range(2):
                            s.wait_ge(ANQM, hb2 + qc + 1)
                            sfence(s, s.activation(nqln[0:1, qc * 512:qc * 512 + 512], p_gen[qc % 2][0:1, :], AF.Ln))
                            s.wait_ge(ALKH, h + 1)
                            s.activation(qT_s2[64:65, qc * 512:qc * 512 + 512],
                                         nqln[0:1, qc * 512:qc * 512 + 512], AF.Exp,
                                         scale=0.5, bias=lkv[0:1, 1:2]).then_inc(AMRW, 1)
                        for kt in range(64):
                            s.wait_ge(ASP, hb64 + kt + 1)
                            pvw = hb64 + kt - 3
                            if pvw >= 1:
                                s.wait_ge(APV, pvw)
                                s.wait_ge(ALC, pvw)
                            s.activation(PT[:, (kt % 4) * 1024:(kt % 4) * 1024 + 1024],
                                         p_sp[kt % 2][:], AF.Exp).then_inc(AEXP, 1)

                @block.vector
                def _(v):
                    for h in range(nheads):
                        hb16, hb64, hb2 = h * 16, h * 64, h * 2
                        for c in range(16):
                            v.wait_ge(AKGE, hb16 + c + 1)
                            if c >= 2:
                                v.wait_ge(ANKM, hb16 + c - 1)
                            elif h >= 1:
                                v.wait_ge(ANQM, (h - 1) * 2 + 2)
                            v.tensor_mul(sqb[c % 2][:], kT_nope[:, c * 512:(c + 1) * 512],
                                         kT_nope[:, c * 512:(c + 1) * 512])
                            v.tensor_mul(sqb2[c % 2][:], krT[:, c * 512:(c + 1) * 512],
                                         krT[:, c * 512:(c + 1) * 512]).then_inc(ASQK, 1)
                            v.wait_ge(ANKM, hb16 + c + 1)
                            v.reduce_max(maxb[:, c:c + 1], p_gen[c % 2][:], axis=AX.X).then_inc(ANKR, 1)
                        v.wait_ge(ANKR, hb16 + 16)
                        v.reduce_max(maxb[:, 16:17], maxb[:, 0:16], axis=AX.X).then_inc(AMK, 1)
                        v.wait_ge(ALK, h + 1)
                        v.tensor_scalar(lkv[0:1, 1:2], lkv[0:1, 0:1], 0.5, None, ALU.mult).then_inc(ALKH, 1)
                        for kt in range(64):
                            v.wait_ge(AVGM, hb64 + kt + 1)
                            v.tensor_copy(Vb[:, kt * VD:(kt + 1) * VD],
                                          p_gen[kt % 2][:, 0:128]).then_inc(AVGE, 1)
                        for qc in range(2):
                            lo = qc * 512
                            v.wait_ge(AQRE, hb2 + qc + 1)
                            v.tensor_scalar(rt1[0:32, lo:lo + 512], qrr[32:64, lo:lo + 512], -1.0, None, ALU.mult)
                            vfence(v, v.tensor_copy(rt1[32:64, lo:lo + 512], qrr[0:32, lo:lo + 512]))
                            vfence(v, v.tensor_mul(qrr[:, lo:lo + 512], qrr[:, lo:lo + 512], cqT[:, lo:lo + 512]))
                            vfence(v, v.tensor_mul(rt1[:, lo:lo + 512], rt1[:, lo:lo + 512], sqT[:, lo:lo + 512]))
                            v.tensor_add(qT_s2[0:64, lo:lo + 512], qrr[:, lo:lo + 512],
                                         rt1[:, lo:lo + 512]).then_inc(AROP, 1)
                            v.wait_ge(AQGE, hb2 + qc + 1)
                            v.wait_ge(ANKM, hb16 + 16)
                            if qc >= 1:
                                v.wait_ge(ANQM, hb2 + 1)
                            v.tensor_mul(sqb[0][:], qT_nope[:, lo:lo + 512],
                                         qT_nope[:, lo:lo + 512]).then_inc(ASQQ, 1)
                            v.wait_ge(AROP, hb2 + qc + 1)
                            v.tensor_mul(sqb[1][0:64, :], qT_s2[0:64, lo:lo + 512],
                                         qT_s2[0:64, lo:lo + 512]).then_inc(ASQQ, 1)
                        v.wait_ge(ABC, h)
                        vfence(v, v.memset(lacc[:], 0.0))
                        for kt in range(64):
                            v.wait_ge(AEXP, hb64 + kt + 1)
                            if kt >= 1:
                                v.wait_ge(ALC, hb64 + kt)
                            v.tensor_add(lacc[:], lacc[:], PT[:, (kt % 4) * 1024:(kt % 4) * 1024 + 1024]).then_inc(ALC, 1)
                        v.wait_ge(ABC, h + 1)
                        vfence(v, v.reciprocal(rlb[:], p_sp[0][:]))
                        v.wait_ge(APV, hb64 + 64)
                        if h >= 1:
                            v.wait_ge(SA, 16 * h)
                        v.tensor_mul(OT_h[:], p_pvT[:], rlb[:]).then_inc(AOTE, 1)

                @block.sync
                def _(sync):
                    for h in range(nheads):
                        sync.wait_ge(AOTE, h + 1)
                        sync.dma_start(out=d_OT[h, :, :], in_=OT_h[:]).then_inc(SA, 16)

                if taps:
                    @block.sync
                    def _(sync):
                        sync.wait_ge(AKGE, 16)
                        sync.dma_start(out=tap_aps["t_kT0"], in_=kT_nope[:]).then_inc(SG, 16)
                        sync.wait_ge(AVGE, 64)
                        sync.dma_start(out=tap_aps["t_V0"], in_=Vb[:]).then_inc(SG, 16)
                        sync.wait_ge(AOTE, 1)
                        sync.dma_start(out=tap_aps["t_OT0"], in_=OT_h[:]).then_inc(SG, 16)

                barrier()
                if max_phase <= 2:
                    return nc

        with ExitStack() as ctxLF:
            ELF = ctxLF.enter_context
            ln2T = ELF(nc.sbuf_tensor("ln2T", [128, 32 * R], BF16))
            sums = ELF(nc.sbuf_tensor("sums", [128, 8], F32))
            ssqs = ELF(nc.sbuf_tensor("ssqs", [128, 8], F32))

            with ExitStack() as ctxP:
                EP = ctxP.enter_context
                ctxPP = ExitStack()
                EPP = ctxPP.enter_context
                OT_all = EPP(nc.sbuf_tensor("OT_all", [128, 32 * R], BF16))
                wo_sb = [EPP(nc.sbuf_tensor(f"wo{i}", [128, 32 * 512], BF16)) for i in range(2)]
                atile = [EPP(nc.sbuf_tensor(f"atile{i}", [128, 512], F32)) for i in range(2)]
                tstat = EPP(nc.sbuf_tensor("tstat", [128, 4], F32))
                p_pr = [EPP(nc.psum_tensor(f"p_pr{i}", [128, 512], F32)) for i in range(4)]

                PMM = sem("p_mm")
                PEV = sem("p_ev")
                PSQ = sem("p_sq")
                PST = sem("p_stat")

                @block.sync
                def _(sync):
                    sync.wait_ge(SA, 16 * nheads)
                    for s2 in range(2):
                        sync.dma_start(
                            out=OT_all[:].rearrange("p (s n) -> p s n", s=32)[:, s2 * 16:(s2 + 1) * 16, :],
                            in_=d_OT.rearrange("h p q -> p h q")[:, s2 * 16:(s2 + 1) * 16, :],
                        ).then_inc(CLD, 16)
                cld_n[0] += 32

                @block.gpsimd
                def _(gp):
                    gp.wait_ge(SA, 16 * nheads)
                    for dc in range(8):
                        par = dc % 2
                        sl = [LE, LF][par]
                        if dc >= 2:
                            gp.wait_ge(PMM, (dc - 2) * 8 + 8)
                        for s2 in range(2):
                            gp.dma_start(
                                out=wo_sb[par][:].rearrange("p (s n) -> p s n", s=32)[:, s2 * 16:(s2 + 1) * 16, :],
                                in_=w_o[:, dc * 512:(dc + 1) * 512].rearrange("(s p) n -> p s n", p=128)[:, s2 * 16:(s2 + 1) * 16, :],
                            ).then_inc(sl, 16)

                WOB = [32 * ((nheads + 1) // 2), 32 * (nheads // 2)]

                @block.tensor
                def _(tn):
                    tn.wait_ge(CLD, cld_n[0])
                    tn.wait_ge(WLD, wld_n[0])
                    for dc in range(8):
                        par = dc % 2
                        sl = [LE, LF][par]
                        tn.wait_ge(sl, WOB[par] + 32 * (dc // 2 + 1))
                        for qt in range(8):
                            idx = dc * 8 + qt
                            if idx >= 4:
                                tn.wait_ge(PEV, idx - 3)
                            for s32 in range(32):
                                mm = tn.matmul(p_pr[qt % 4][:],
                                               OT_all[:, s32 * R + qt * 128:s32 * R + (qt + 1) * 128],
                                               wo_sb[par][:, s32 * 512:(s32 + 1) * 512],
                                               start=(s32 == 0), stop=(s32 == 31))
                            mm.then_inc(PMM, 1)

                @block.scalar
                def _(s):
                    for dc in range(8):
                        for qt in range(8):
                            idx = dc * 8 + qt
                            par = idx % 2
                            s.wait_ge(PMM, idx + 1)
                            if idx >= 2:
                                s.wait_ge([ST0, ST1][par], 16 * (idx // 2))
                            if idx >= 2:
                                s.wait_ge(PST, idx - 1)
                            s.activation(atile[par][:], p_pr[qt % 4][:], AF.Copy,
                                         accum_out=tstat[:, 2 * par:2 * par + 1]).then_inc(PEV, 1)
                            s.wait_ge(PEV, idx + 1)
                            s.activation(junk[:, par * 512:(par + 1) * 512], atile[par][:], AF.Square,
                                         accum_out=tstat[:, 2 * par + 1:2 * par + 2]).then_inc(PSQ, 1)

                @block.vector
                def _(v):
                    v.memset(sums[:], 0.0)
                    vfence(v, v.memset(ssqs[:], 0.0))
                    for dc in range(8):
                        for qt in range(8):
                            idx = dc * 8 + qt
                            par = idx % 2
                            v.wait_ge(PSQ, idx + 1)
                            vfence(v, v.tensor_add(sums[:, qt:qt + 1], sums[:, qt:qt + 1], tstat[:, 2 * par:2 * par + 1]))
                            v.tensor_add(ssqs[:, qt:qt + 1], ssqs[:, qt:qt + 1],
                                         tstat[:, 2 * par + 1:2 * par + 2]).then_inc(PST, 1)

                @block.sync
                def _(sync):
                    for dc in range(8):
                        for qt in range(8):
                            idx = dc * 8 + qt
                            par = idx % 2
                            sync.wait_ge(PSQ, idx + 1)
                            sync.dma_start(out=d_attn[qt * 128:(qt + 1) * 128, dc * 512:(dc + 1) * 512],
                                           in_=atile[par][:]).then_inc([ST0, ST1][par], 16)

                barrier()

                if taps:
                    @block.sync
                    def _(sync):
                        sync.wait_ge(ST0, 16 * 32)
                        sync.wait_ge(ST1, 16 * 32)
                        sync.dma_start(out=tap_aps["t_attn"], in_=d_attn).then_inc(SG, 16)
                    barrier()
                if max_phase <= 2.5:
                    ctxPP.close()
                    return nc

                ctxPP.close()
                xrt = [EP(nc.sbuf_tensor(f"xrt{i}", [128, D], F32)) for i in range(2)]
                art = [EP(nc.sbuf_tensor(f"art{i}", [128, D], F32)) for i in range(2)]
                x2t = EP(nc.sbuf_tensor("x2t", [128, D], F32))
                l2f = EP(nc.sbuf_tensor("l2f", [128, D], F32))
                l2bf = EP(nc.sbuf_tensor("l2bf", [128, D], BF16))
                p_t3 = [EP(nc.psum_tensor(f"p_t3{i}", [128, 1024], F32)) for i in range(2)]

                RVAR = sem("r_var")
                RRS = sem("r_rs")
                RX2 = sem("r_x2")
                RSQ = sem("r_sq")
                RVB = sem("r_vb")
                RRS2 = sem("r_rs2")
                RL2 = sem("r_l2")
                RMM = sem("r_mm")
                REV = sem("r_ev")

                XRB = [40 * 16, 32 * 16]
                X2SB = 16 * 32

                @block.sync
                def _(sync):
                    sync.wait_ge(ST0, 16 * 32)
                    sync.wait_ge(ST1, 16 * 32)
                    for qt in range(8):
                        par = qt % 2
                        sl = [LA, LB][par]
                        if qt >= 2:
                            sync.wait_ge(RL2, qt - 1)
                        sync.dma_start(out=xrt[par][:], in_=x_loc[qt * 128:(qt + 1) * 128, :]).then_inc(sl, 16)
                        sync.dma_start(out=art[par][:], in_=d_attn[qt * 128:(qt + 1) * 128, :]).then_inc(sl, 16)
                        if qt >= 1:
                            sync.wait_ge(RX2, qt)
                            sync.dma_start(out=d_x2[(qt - 1) * 128:qt * 128, :], in_=x2t[:]).then_inc(ST0, 16)
                    sync.wait_ge(RX2, 8)
                    sync.dma_start(out=d_x2[7 * 128:8 * 128, :], in_=x2t[:]).then_inc(ST0, 16)
                    sync.wait_ge(ST0, X2SB + 16 * 8)

                @block.vector
                def _(v):
                    for qt in range(8):
                        par = qt % 2
                        sl = [LA, LB][par]
                        v.wait_ge(sl, XRB[par] + 32 * (qt // 2 + 1))
                        v.tensor_scalar(stats[:, 0:1], sums[:, qt:qt + 1], -1.0 / D, None, ALU.mult)
                        vfence(v, v.tensor_scalar(stats[:, 1:2], ssqs[:, qt:qt + 1], 1.0 / D, None, ALU.mult))
                        vfence(v, v.tensor_mul(stats[:, 2:3], stats[:, 0:1], stats[:, 0:1]))
                        v.tensor_sub(stats[:, 3:4], stats[:, 1:2], stats[:, 2:3]).then_inc(RVAR, 1)
                        v.wait_ge(RRS, qt + 1)
                        vfence(v, v.tensor_scalar(art[par][:], art[par][:], stats[:, 0:1], None, ALU.add))
                        vfence(v, v.tensor_scalar(art[par][:], art[par][:], stats[:, 4:5], None, ALU.mult))
                        if qt >= 1:
                            v.wait_ge(RSQ, qt)
                            v.wait_ge(ST0, X2SB + 16 * qt)
                        v.tensor_add(x2t[:], xrt[par][:], art[par][:]).then_inc(RX2, 1)
                        v.wait_ge(RX2, qt + 1)
                        vfence(v, v.reduce_sum(stats[:, 5:6], x2t[:], axis=AX.X, negate=True))
                        vfence(v, v.tensor_scalar(stats[:, 6:7], stats[:, 5:6], 1.0 / D, None, ALU.mult))
                        v.tensor_mul(stats[:, 7:8], stats[:, 6:7], stats[:, 6:7])
                        v.wait_ge(RSQ, qt + 1)
                        vfence(v, v.tensor_add(stats[:, 23:24], stats[:, 9:10], stats[:, 22:23]))
                        vfence(v, v.tensor_scalar(stats[:, 8:9], stats[:, 23:24], 1.0 / D, None, ALU.mult))
                        v.tensor_sub(stats[:, 10:11], stats[:, 8:9], stats[:, 7:8]).then_inc(RVB, 1)
                        v.wait_ge(RRS2, qt + 1)
                        vfence(v, v.tensor_scalar(l2f[:], x2t[:], stats[:, 6:7], None, ALU.add))
                        if qt >= 1:
                            v.wait_ge(RMM, 4 * qt)
                        v.tensor_scalar(l2bf[:], l2f[:], stats[:, 11:12], None, ALU.mult).then_inc(RL2, 1)

                @block.scalar
                def _(s):
                    for qt in range(8):
                        s.wait_ge(RVAR, qt + 1)
                        sfence(s, s.activation(stats[:, 20:21], stats[:, 3:4], AF.Ln, bias=eps_sb[:]))
                        s.activation(stats[:, 4:5], stats[:, 20:21], AF.Exp, scale=-0.5).then_inc(RRS, 1)
                        s.wait_ge(RX2, qt + 1)
                        sfence(s, s.activation(junk[:], x2t[:, 0:2048], AF.Square, accum_out=stats[:, 9:10]))
                        s.activation(junk[:], x2t[:, 2048:4096], AF.Square, accum_out=stats[:, 22:23]).then_inc(RSQ, 1)
                        s.wait_ge(RVB, qt + 1)
                        sfence(s, s.activation(stats[:, 21:22], stats[:, 10:11], AF.Ln, bias=eps_sb[:]))
                        s.activation(stats[:, 11:12], stats[:, 21:22], AF.Exp, scale=-0.5).then_inc(RRS2, 1)
                        for g in range(4):
                            s.wait_ge(RMM, qt * 4 + g + 1)
                            for j8 in range(8):
                                j = g * 8 + j8
                                last = s.copy(ln2T[:, j * R + qt * 128:j * R + (qt + 1) * 128],
                                              p_t3[g % 2][:, j8 * 128:(j8 + 1) * 128])
                            last.then_inc(REV, 1)

                @block.tensor
                def _(tn):
                    for qt in range(8):
                        tn.wait_ge(RL2, qt + 1)
                        for g in range(4):
                            if g >= 2:
                                tn.wait_ge(REV, qt * 4 + g - 1)
                            elif qt >= 1:
                                tn.wait_ge(REV, (qt - 1) * 4 + g + 3)
                            for j8 in range(8):
                                j = g * 8 + j8
                                mm = tn.matmul(p_t3[g % 2][:, j8 * 128:(j8 + 1) * 128],
                                               l2bf[:, j * 128:(j + 1) * 128], ident_sb[:],
                                               start=True, stop=True)
                            mm.then_inc(RMM, 1)

                if taps:
                    @block.sync
                    def _(sync):
                        sync.wait_ge(ST0, X2SB + 16 * 8)
                        sync.dma_start(out=tap_aps["t_x2"], in_=d_x2).then_inc(SG, 16)

                barrier()
                if max_phase <= 3:
                    return nc

            with ExitStack() as ctxF1:
                EF1 = ctxF1.enter_context
                w1_sb = [EF1(nc.sbuf_tensor(f"w1sb{i}", [128, 32 * 512], BF16)) for i in range(2)]
                b1_sb = EF1(nc.sbuf_tensor("b1_sb", [128, 128], F32))
                hT_sb = [EF1(nc.sbuf_tensor(f"hTsb{i}", [128, 1024], BF16)) for i in range(2)]
                p_h = [EF1(nc.psum_tensor(f"p_h{i}", [128, 1024], F32)) for i in range(2)]

                FMM = sem("f_mm")
                FEV = sem("f_ev")
                FST = sem("f_st")

                @block.sync
                def _(sync):
                    sync.dma_start(out=b1_sb[:], in_=b1r).then_inc(CLD, 16)
                cld_n[0] += 16

                WOB2 = [WOB[0] + 32 * 4, WOB[1] + 32 * 4]

                @block.gpsimd
                def _(gp):
                    gp.wait_ge(ST0, 16 * 40)
                    gp.wait_ge(ST1, 16 * 32)
                    for fs in range(32):
                        par = fs % 2
                        sl = [LE, LF][par]
                        if fs >= 2:
                            gp.wait_ge(FMM, (fs - 2) * 4 + 4)
                        for s2 in range(2):
                            gp.dma_start(
                                out=w1_sb[par][:].rearrange("p (s n) -> p s n", s=32)[:, s2 * 16:(s2 + 1) * 16, :],
                                in_=w1[:, fs * 512:(fs + 1) * 512].rearrange("(s p) n -> p s n", p=128)[:, s2 * 16:(s2 + 1) * 16, :],
                            ).then_inc(sl, 16)

                @block.tensor
                def _(tn):
                    tn.wait_ge(CLD, cld_n[0])
                    tn.wait_ge(WLD, wld_n[0])
                    for fs in range(32):
                        par = fs % 2
                        sl = [LE, LF][par]
                        tn.wait_ge(sl, WOB2[par] + 32 * (fs // 2 + 1))
                        for f4 in range(4):
                            key = fs * 4 + f4
                            if key >= 2:
                                tn.wait_ge(FEV, key - 1)
                            for qc in range(2):
                                for j in range(32):
                                    mm = tn.matmul(p_h[f4 % 2][:, qc * 512:qc * 512 + 512],
                                                   w1_sb[par][:, j * 512 + f4 * 128:j * 512 + (f4 + 1) * 128],
                                                   ln2T[:, j * R + qc * 512:j * R + qc * 512 + 512],
                                                   start=(j == 0), stop=(j == 31))
                            mm.then_inc(FMM, 1)

                HSB = [16 * 40, 16 * 32]

                @block.scalar
                def _(s):
                    for key in range(128):
                        par = key % 2
                        s.wait_ge(FMM, key + 1)
                        if key >= 2:
                            s.wait_ge([ST0, ST1][par], HSB[par] + 16 * (key // 2))
                        s.activation(hT_sb[par][:], p_h[key % 2][:], AF.Gelu_apprx_tanh,
                                     bias=b1_sb[:, key:key + 1]).then_inc(FEV, 1)

                @block.sync
                def _(sync):
                    for key in range(128):
                        par = key % 2
                        sync.wait_ge(FEV, key + 1)
                        sync.dma_start(out=d_hT[key * 128:(key + 1) * 128, :],
                                       in_=hT_sb[par][:]).then_inc([ST0, ST1][par], 16)

                if taps:
                    @block.sync
                    def _(sync):
                        sync.wait_ge(ST0, HSB[0] + 16 * 64)
                        sync.wait_ge(ST1, HSB[1] + 16 * 64)
                        sync.dma_start(out=tap_aps["t_hT0"], in_=d_hT[0:128, :]).then_inc(SG, 16)

                barrier()
                if max_phase <= 4:
                    return nc

            with ExitStack() as ctxF2:
                EF2 = ctxF2.enter_context
                w2_sb = [EF2(nc.sbuf_tensor(f"w2sb{i}", [128, 16 * 512], BF16)) for i in range(2)]
                hTp = [EF2(nc.sbuf_tensor(f"hTp{i}", [128, 16 * 1024], BF16)) for i in range(2)]
                b2_sb = EF2(nc.sbuf_tensor("b2_sb", [1, D], F32))
                o2t = [EF2(nc.sbuf_tensor(f"o2t{i}", [128, 512], F32)) for i in range(2)]
                tstat2 = EF2(nc.sbuf_tensor("tstat2", [128, 4], F32))
                p_s = [EF2(nc.psum_tensor(f"p_s{i}", [128, 512], F32)) for i in range(8)]

                GFB = sem("g_fb")
                GMM = sem("g_mm")
                GEV = sem("g_ev")
                GSQ = sem("g_sq")
                GST = sem("g_stat")

                W2B = [WOB2[0] + 32 * 16, WOB2[1] + 32 * 16]
                HLB = [32 * 32, 32 * 32]

                F2SB = [16 * (32 + 8 + 64), 16 * (32 + 64)]

                @block.sync
                def _(sync):
                    sync.dma_start(out=b2_sb[:], in_=b2r).then_inc(CLD, 16)
                    sync.wait_ge(ST0, F2SB[0])
                    sync.wait_ge(ST1, F2SB[1])
                    for dc in range(8):
                        if dc >= 1:
                            for qt in range(8):
                                idx = (dc - 1) * 8 + qt
                                par = idx % 2
                                sync.wait_ge(GSQ, idx + 1)
                                sync.dma_start(out=d_out2[qt * 128:(qt + 1) * 128, (dc - 1) * 512:dc * 512],
                                               in_=o2t[par][:]).then_inc([ST0, ST1][par], 16)
                        for fb in range(8):
                            key = dc * 8 + fb
                            par = key % 2
                            sl = [LC, LD2][par]
                            if key >= 2:
                                sync.wait_ge(GFB, key - 1)
                            sync.dma_start(
                                out=hTp[par][:].rearrange("p (a q) -> p a q", a=16),
                                in_=d_hT[fb * 2048:(fb + 1) * 2048, :].rearrange("(a p) q -> p a q", p=128),
                            ).then_inc(sl, 16)
                    for qt in range(8):
                        idx = 7 * 8 + qt
                        par = idx % 2
                        sync.wait_ge(GSQ, idx + 1)
                        sync.dma_start(out=d_out2[qt * 128:(qt + 1) * 128, 7 * 512:8 * 512],
                                       in_=o2t[par][:]).then_inc([ST0, ST1][par], 16)
                    sync.wait_ge(ST0, F2SB[0] + 16 * 32)
                    sync.wait_ge(ST1, F2SB[1] + 16 * 32)
                    if taps:
                        sync.dma_start(out=tap_aps["t_out2"], in_=d_out2).then_inc(SG, 16)
                cld_n[0] += 16

                @block.gpsimd
                def _(gp):
                    gp.wait_ge(ST0, F2SB[0])
                    gp.wait_ge(ST1, F2SB[1])
                    for dc in range(8):
                        for fb in range(8):
                            key = dc * 8 + fb
                            par = key % 2
                            sl = [LE, LF][par]
                            if key >= 2:
                                gp.wait_ge(GFB, key - 1)
                            gp.dma_start(
                                out=w2_sb[par][:].rearrange("p (s n) -> p s n", s=16),
                                in_=w2[fb * 2048:(fb + 1) * 2048, dc * 512:(dc + 1) * 512].rearrange("(s p) n -> p s n", p=128),
                            ).then_inc(sl, 16)

                @block.tensor
                def _(tn):
                    tn.wait_ge(CLD, cld_n[0])
                    tn.wait_ge(WLD, wld_n[0])
                    for dc in range(8):
                        for fb in range(8):
                            key = dc * 8 + fb
                            par = key % 2
                            tn.wait_ge([LE, LF][par], W2B[par] + 16 * (key // 2 + 1))
                            tn.wait_ge([LC, LD2][par], HLB[par] + 16 * (key // 2 + 1))
                            if fb == 0 and dc >= 1:
                                tn.wait_ge(GSQ, dc * 8)
                            for qt in range(8):
                                for j in range(16):
                                    mm = tn.matmul(p_s[qt][:],
                                                   hTp[par][:, j * 1024 + qt * 128:j * 1024 + (qt + 1) * 128],
                                                   w2_sb[par][:, j * 512:(j + 1) * 512],
                                                   start=(fb == 0 and j == 0), stop=False)
                            mm.then_inc(GFB, 1)
                        for qt in range(8):
                            mm = tn.matmul(p_s[qt][:], onesf_sb[:], b2_sb[:, dc * 512:(dc + 1) * 512],
                                           start=False, stop=True)
                            mm.then_inc(GMM, 1)

                @block.scalar
                def _(s):
                    for dc in range(8):
                        for qt in range(8):
                            idx = dc * 8 + qt
                            par = idx % 2
                            s.wait_ge(GMM, idx + 1)
                            if idx >= 2:
                                s.wait_ge([ST0, ST1][par], F2SB[par] + 16 * (idx // 2))
                            if idx >= 2:
                                s.wait_ge(GST, idx - 1)
                            s.activation(o2t[par][:], p_s[qt][:], AF.Copy,
                                         accum_out=tstat2[:, 2 * par:2 * par + 1]).then_inc(GEV, 1)
                            s.wait_ge(GEV, idx + 1)
                            s.activation(junk[:, par * 512:(par + 1) * 512], o2t[par][:], AF.Square,
                                         accum_out=tstat2[:, 2 * par + 1:2 * par + 2]).then_inc(GSQ, 1)

                @block.vector
                def _(v):
                    v.memset(sums[:], 0.0)
                    vfence(v, v.memset(ssqs[:], 0.0))
                    for dc in range(8):
                        for qt in range(8):
                            idx = dc * 8 + qt
                            par = idx % 2
                            v.wait_ge(GSQ, idx + 1)
                            vfence(v, v.tensor_add(sums[:, qt:qt + 1], sums[:, qt:qt + 1], tstat2[:, 2 * par:2 * par + 1]))
                            v.tensor_add(ssqs[:, qt:qt + 1], ssqs[:, qt:qt + 1],
                                         tstat2[:, 2 * par + 1:2 * par + 2]).then_inc(GST, 1)

                barrier()

            with ExitStack() as ctxF3:
                EF3 = ctxF3.enter_context
                x2r = [EF3(nc.sbuf_tensor(f"x2r{i}", [128, D], F32)) for i in range(2)]
                o2r = [EF3(nc.sbuf_tensor(f"o2r{i}", [128, D], F32)) for i in range(2)]
                xo = [EF3(nc.sbuf_tensor(f"xo{i}", [128, D], F32)) for i in range(2)]

                ZVAR = sem("z_var")
                ZRS = sem("z_rs")
                ZXO = sem("z_xo")

                ZB = [(8 + 32 + 4 * 2) * 16, (32 + 4 * 2) * 16]

                @block.sync
                def _(sync):
                    for qt in range(8):
                        par = qt % 2
                        sl = [LA, LB][par]
                        if qt >= 2:
                            sync.wait_ge(ZXO, qt - 1)
                        sync.dma_start(out=x2r[par][:], in_=d_x2[qt * 128:(qt + 1) * 128, :]).then_inc(sl, 16)
                        sync.dma_start(out=o2r[par][:], in_=d_out2[qt * 128:(qt + 1) * 128, :]).then_inc(sl, 16)
                        if qt >= 2:
                            sync.dma_start(out=out_rows[(qt - 2) * 128:(qt - 1) * 128, :],
                                           in_=xo[(qt - 2) % 2][:]).then_inc([OST, OST1][qt % 2], 16)

                @block.vector
                def _(v):
                    for qt in range(8):
                        par = qt % 2
                        sl = [LA, LB][par]
                        v.wait_ge(sl, ZB[par] + 32 * (qt // 2 + 1))
                        v.tensor_scalar(stats[:, 0:1], sums[:, qt:qt + 1], -1.0 / D, None, ALU.mult)
                        vfence(v, v.tensor_scalar(stats[:, 1:2], ssqs[:, qt:qt + 1], 1.0 / D, None, ALU.mult))
                        vfence(v, v.tensor_mul(stats[:, 2:3], stats[:, 0:1], stats[:, 0:1]))
                        v.tensor_sub(stats[:, 3:4], stats[:, 1:2], stats[:, 2:3]).then_inc(ZVAR, 1)
                        v.wait_ge(ZRS, qt + 1)
                        vfence(v, v.tensor_scalar(o2r[par][:], o2r[par][:], stats[:, 0:1], None, ALU.add))
                        vfence(v, v.tensor_scalar(o2r[par][:], o2r[par][:], stats[:, 4:5], None, ALU.mult))
                        if qt >= 2:
                            v.wait_ge([OST, OST1][qt % 2], 16 * (qt // 2))
                        v.tensor_add(xo[par][:], x2r[par][:], o2r[par][:]).then_inc(ZXO, 1)

                @block.scalar
                def _(s):
                    for qt in range(8):
                        s.wait_ge(ZVAR, qt + 1)
                        sfence(s, s.activation(stats[:, 20:21], stats[:, 3:4], AF.Ln, bias=eps_sb[:]))
                        s.activation(stats[:, 4:5], stats[:, 20:21], AF.Exp, scale=-0.5).then_inc(ZRS, 1)

                @block.sync
                def _(sync):
                    for qt in range(6, 8):
                        par = qt % 2
                        sync.wait_ge(ZXO, qt + 1)
                        sync.dma_start(out=out_rows[qt * 128:(qt + 1) * 128, :],
                                       in_=xo[par][:]).then_inc([OST, OST1][qt % 2], 16)
                    sync.wait_ge(OST, 16 * 4)
                    sync.wait_ge(OST1, 16 * 4)
                    if taps:
                        sync.wait_ge(SG, 16 * 9)

    return nc


def _host_constants():
    inv_freq = 1.0 / (10000.0 ** (np.arange(0, RD, 2, dtype=np.float64) / RD))
    t = np.arange(S, dtype=np.float64)
    freqs = np.outer(t, inv_freq)
    emb = np.concatenate([freqs, freqs], axis=-1)
    cosk = np.cos(emb).astype(np.float32)
    sink = np.sin(emb).astype(np.float32)
    ident = np.eye(128, dtype=ml_dtypes.bfloat16)
    ones_c = np.ones((128, 128), dtype=ml_dtypes.bfloat16)
    return cosk, sink, ident, ones_c


def make_in_maps(inputs):
    x = np.ascontiguousarray(np.asarray(inputs["L0_attn_hidden_states"], dtype=np.float32).reshape(S, D))
    cosk, sink, ident, ones_c = _host_constants()
    common = {
        "x_full": x,
        "w_qa": np.ascontiguousarray(inputs["L0_attn_w_qa"], dtype=np.float32),
        "g_qa": np.ascontiguousarray(inputs["L0_attn_g_qa"], dtype=np.float32).reshape(1, QR),
        "w_qb": np.ascontiguousarray(inputs["L0_attn_w_qb"], dtype=np.float32),
        "w_kva": np.ascontiguousarray(inputs["L0_attn_w_kva"], dtype=np.float32),
        "g_kva": np.ascontiguousarray(inputs["L0_attn_g_kva"], dtype=np.float32).reshape(1, KVR),
        "w_kvb": np.ascontiguousarray(inputs["L0_attn_w_kvb"], dtype=np.float32),
        "w_o": np.ascontiguousarray(inputs["L0_attn_w_o"], dtype=np.float32),
        "w1": np.ascontiguousarray(inputs["L0_ffn_inter_w"], dtype=np.float32),
        "b1r": np.ascontiguousarray(np.asarray(inputs["L0_ffn_inter_b"], dtype=np.float32).reshape(128, 128).T),
        "w2": np.ascontiguousarray(inputs["L0_ffn_output_w"], dtype=np.float32),
        "b2r": np.ascontiguousarray(inputs["L0_ffn_output_b"], dtype=np.float32).reshape(1, D),
        "cosk": cosk,
        "sink": sink,
        "identc": ident,
        "epsc": np.full((128, 1), LN_EPS, np.float32),
        "onesfc": np.ones((1, 128), np.float32),
        "onesf2c": np.ones((128, 128), np.float32),
        "onesc": ones_c,
    }
    in_maps = []
    for c in range(NCORES):
        m = dict(common)
        m["x_loc"] = np.ascontiguousarray(x[c * R:(c + 1) * R, :])
        m["cosqT"] = np.ascontiguousarray(cosk[c * R:(c + 1) * R, :].T)
        m["sinqT"] = np.ascontiguousarray(sink[c * R:(c + 1) * R, :].T)
        in_maps.append(m)
    return in_maps


def kernel(**inputs):
    in_maps = make_in_maps(inputs)
    nc = build_nc(taps=False)
    res = run_bass_kernel_spmd(nc, in_maps, core_ids=list(range(NCORES)))
    out = np.concatenate([res.results[c]["out_rows"] for c in range(NCORES)], axis=0)
    return out.reshape(1, S, D).astype(np.float32)


# version 04
# speedup vs baseline: 1.2298x; 1.2298x over previous
import numpy as np
import ml_dtypes
from contextlib import ExitStack
import concourse.bass as bass
import concourse.mybir as mybir
from concourse.bass_utils import run_bass_kernel_spmd

F32 = mybir.dt.float32
BF16 = mybir.dt.bfloat16
AF = mybir.ActivationFunctionType
ALU = mybir.AluOpType
AX = mybir.AxisListType

S, D, R = 8192, 4096, 1024
NH, QR, KVR, RD, ND, VD = 32, 1536, 512, 64, 128, 128
QK = ND + RD
FF = 16384
SC = float(QK) ** -0.5
NCORES = 8
LN_EPS = 1e-5
RMS_EPS = 1e-6


def build_nc(taps=False, nheads=NH, max_phase=99):
    nc = bass.Bass("TRN2", target_bir_lowering=False)

    x_full = nc.dram_tensor("x_full", [S, D], F32, kind="ExternalInput").ap()
    x_loc = nc.dram_tensor("x_loc", [R, D], F32, kind="ExternalInput").ap()
    w_qa = nc.dram_tensor("w_qa", [D, QR], F32, kind="ExternalInput").ap()
    g_qa = nc.dram_tensor("g_qa", [1, QR], F32, kind="ExternalInput").ap()
    w_qb = nc.dram_tensor("w_qb", [QR, NH * QK], F32, kind="ExternalInput").ap()
    w_kva = nc.dram_tensor("w_kva", [D, KVR + RD], F32, kind="ExternalInput").ap()
    g_kva = nc.dram_tensor("g_kva", [1, KVR], F32, kind="ExternalInput").ap()
    w_kvb = nc.dram_tensor("w_kvb", [KVR, NH * (ND + VD)], F32, kind="ExternalInput").ap()
    w_o = nc.dram_tensor("w_o", [D, D], F32, kind="ExternalInput").ap()
    w1 = nc.dram_tensor("w1", [D, FF], F32, kind="ExternalInput").ap()
    b1r = nc.dram_tensor("b1r", [128, 128], F32, kind="ExternalInput").ap()
    w2 = nc.dram_tensor("w2", [FF, D], F32, kind="ExternalInput").ap()
    b2r = nc.dram_tensor("b2r", [1, D], F32, kind="ExternalInput").ap()
    cosk = nc.dram_tensor("cosk", [S, RD], F32, kind="ExternalInput").ap()
    sink = nc.dram_tensor("sink", [S, RD], F32, kind="ExternalInput").ap()
    cosqT = nc.dram_tensor("cosqT", [RD, R], F32, kind="ExternalInput").ap()
    sinqT = nc.dram_tensor("sinqT", [RD, R], F32, kind="ExternalInput").ap()
    identc = nc.dram_tensor("identc", [128, 128], BF16, kind="ExternalInput").ap()
    epsc = nc.dram_tensor("epsc", [128, 1], F32, kind="ExternalInput").ap()
    onesfc = nc.dram_tensor("onesfc", [1, 128], F32, kind="ExternalInput").ap()
    onesf2c = nc.dram_tensor("onesf2c", [128, 128], F32, kind="ExternalInput").ap()
    onesc = nc.dram_tensor("onesc", [128, 128], BF16, kind="ExternalInput").ap()

    out_rows = nc.dram_tensor("out_rows", [R, D], F32, kind="ExternalOutput").ap()

    d_OT = nc.dram_tensor("d_OT", [NH, 128, R], BF16, kind="Internal").ap()
    d_attn = nc.dram_tensor("d_attn", [R, D], F32, kind="Internal").ap()
    d_x2 = nc.dram_tensor("d_x2", [R, D], F32, kind="Internal").ap()
    d_hT = nc.dram_tensor("d_hT", [FF, R], BF16, kind="Internal").ap()
    d_out2 = nc.dram_tensor("d_out2", [R, D], F32, kind="Internal").ap()

    tap_aps = {}
    if taps:
        for name, shape, dt in [
            ("t_qcT", [128, 12 * R], BF16),
            ("t_kcT", [128, 4 * S], BF16),
            ("t_krT", [65, S], BF16),
            ("t_kT0", [128, S], BF16),
            ("t_V0", [128, 64 * VD], BF16),
            ("t_OT0", [128, R], BF16),
            ("t_x2", [R, D], F32),
            ("t_hT0", [128, R], BF16),
            ("t_attn", [R, D], F32),
            ("t_out2", [R, D], F32),
        ]:
            tap_aps[name] = nc.dram_tensor(name, shape, dt, kind="ExternalOutput").ap()

    with ExitStack() as ctx0:
        E0 = ctx0.enter_context

        sems = {}

        def sem(name):
            if name not in sems:
                sems[name] = E0(nc.semaphore(name))
            return sems[name]

        PH = sem("ph")
        CLD = sem("c_ld")
        OST = sem("o_st")
        OST1 = sem("o_st1")
        LA, LB = sem("la"), sem("lb")
        LC, LD2 = sem("lc"), sem("ld2")
        LE, LF = sem("le"), sem("lf")
        ST0, ST1 = sem("st0"), sem("st1")
        SA = sem("sa")
        SG = sem("sg")

        WLD = sem("w_ld")
        cld_n = [0]
        wld_n = [0]
        ph_n = [0]

        ident_sb = E0(nc.sbuf_tensor("ident_sb", [128, 128], BF16))
        eps_sb = E0(nc.sbuf_tensor("eps_sb", [128, 1], F32))
        onesf_sb = E0(nc.sbuf_tensor("onesf_sb", [1, 128], F32))
        onesf2_sb = E0(nc.sbuf_tensor("onesf2_sb", [128, 128], F32))
        ones_sb = E0(nc.sbuf_tensor("ones_sb", [128, 128], BF16))
        junk = E0(nc.sbuf_tensor("junk", [128, 2048], BF16))
        stats = E0(nc.sbuf_tensor("stats", [128, 32], F32))

        block = E0(nc.Block())
        ENG = ["tensor", "scalar", "vector", "sync", "gpsimd"]

        VFS = sem("v_fence")
        SFS = sem("s_fence")
        fence_n = {"v": 0, "s": 0}

        def vfence(v, inst):
            fence_n["v"] += 1
            inst.then_inc(VFS, 1)
            v.wait_ge(VFS, fence_n["v"])

        def sfence(s, inst):
            fence_n["s"] += 1
            inst.then_inc(SFS, 1)
            s.wait_ge(SFS, fence_n["s"])

        def barrier():
            target = ph_n[0] + len(ENG)

            @block.scalar
            def _(s):
                s.activation(stats[:, 30:31], eps_sb[:], AF.Copy).then_inc(PH, 1)
                s.wait_ge(PH, target)

            @block.vector
            def _(v):
                v.memset(stats[:, 31:32], 0.0).then_inc(PH, 1)
                v.wait_ge(PH, target)

            for name in ["tensor", "sync", "gpsimd"]:
                def mk():
                    def _(eng):
                        eng.sem_inc(PH, 1)
                        eng.wait_ge(PH, target)
                    return _
                getattr(block, name)(mk())
            ph_n[0] = target

        @block.sync
        def _(sync):
            sync.dma_start(out=ident_sb[:], in_=identc).then_inc(CLD, 16)
            sync.dma_start(out=ones_sb[:], in_=onesc).then_inc(CLD, 16)
            sync.dma_start(out=eps_sb[:], in_=epsc).then_inc(CLD, 16)
            sync.dma_start(out=onesf_sb[:], in_=onesfc).then_inc(CLD, 16)
            sync.dma_start(out=onesf2_sb[:], in_=onesf2c).then_inc(CLD, 16)
        cld_n[0] += 80

        d_qcT = nc.dram_tensor("d_qcT", [128, 12 * R], BF16, kind="Internal").ap()
        QSP = sem("q_spill")
        with ExitStack() as ctxA:
            EA = ctxA.enter_context
            if True:
                with ExitStack() as ctxQ:
                    EQ = ctxQ.enter_context
                    qcT = EQ(nc.sbuf_tensor("qcT", [128, 12 * R], BF16))
                    xt = [EQ(nc.sbuf_tensor("xt0", [128, D], F32))] * 2
                    xcb = [EQ(nc.sbuf_tensor("xcb0", [128, D], BF16))] * 2
                    xcT = [EQ(nc.sbuf_tensor("xcT0", [128, D], BF16))] * 2
                    wqa_sb = EQ(nc.sbuf_tensor("wqa_sb", [128, 32 * QR], BF16))
                    gqa_b = EQ(nc.sbuf_tensor("gqa_b", [128, QR], F32))
                    qln = EQ(nc.sbuf_tensor("qln", [128, QR], F32))
                    qcb = [EQ(nc.sbuf_tensor("qcb0", [128, QR], BF16))] * 2
                    p_aq = EQ(nc.psum_tensor("p_aq", [128, QR], F32))
                    p_tr = [EQ(nc.psum_tensor(f"p_tr{i}", [128, 1024], F32)) for i in range(2)]

                    NT = 8
                    SQ = sem("q_sq")
                    VXC = sem("q_xc")
                    VVAR = sem("q_var")
                    MTR = sem("q_mtr")
                    EXT = sem("q_ext")
                    MAQ = sem("q_maq")
                    EQL = sem("q_eql")
                    SUQ = sem("q_suq")
                    VQC = sem("q_vqc")
                    MQT = sem("q_mqt")
                    EQT = sem("q_eqt")
                    VSS = sem("q_vss")
                    SUQ2 = sem("q_suq2")

                    @block.gpsimd
                    def _(gp):
                        for s4 in range(4):
                            gp.dma_start(
                                out=wqa_sb[:].rearrange("p (s n) -> p s n", s=32)[:, s4 * 8:(s4 + 1) * 8, :],
                                in_=w_qa.rearrange("(s p) n -> p s n", p=128)[:, s4 * 8:(s4 + 1) * 8, :],
                            ).then_inc(WLD, 16)
                    wld_n[0] += 64

                    @block.sync
                    def _(sync):
                        sync.dma_start(out=gqa_b[:], in_=g_qa.broadcast_to((128, QR))).then_inc(CLD, 16)
                        for t in range(NT):
                            if t >= 1:
                                sync.wait_ge(VXC, t)
                                sync.wait_ge(SQ, t)
                            sync.dma_start(out=xt[0][:], in_=x_loc[t * 128:(t + 1) * 128, :]).then_inc(LA, 16)
                        sync.wait_ge(EQT, 16)
                        sync.dma_start(out=d_qcT, in_=qcT[:]).then_inc(QSP, 16)
                        sync.wait_ge(QSP, 16)
                    cld_n[0] += 16

                    @block.scalar
                    def _(s):
                        s.wait_ge(CLD, cld_n[0])
                        s.wait_ge(WLD, wld_n[0])
                        for t in range(NT):
                            par = t % 2
                            par2 = (t % 2) * 22
                            s.wait_ge(LA, 16 * (t + 1))
                            sfence(s, s.activation(junk[:], xt[0][:, 0:2048], AF.Square,
                                         accum_out=stats[:, 6 + par2:7 + par2]))
                            s.activation(junk[:], xt[0][:, 2048:4096], AF.Square,
                                         accum_out=stats[:, 7 + par2:8 + par2]).then_inc(SQ, 1)
                            s.wait_ge(VVAR, t + 1)
                            sfence(s, s.activation(stats[:, 12:13], stats[:, 10 + par:11 + par], AF.Ln, bias=eps_sb[:]))
                            s.activation(stats[:, 14 + par:15 + par], stats[:, 12:13], AF.Exp, scale=-0.5)
                            if t >= 1:
                                s.wait_ge(MAQ, t)
                            for g in range(4):
                                s.wait_ge(MTR, t * 4 + g + 1)
                                s.copy(xcT[par][:, g * 1024:(g + 1) * 1024], p_tr[g % 2][:]).then_inc(EXT, 1)
                            s.wait_ge(MAQ, t + 1)
                            if t >= 1:
                                s.wait_ge(VQC, t)
                            s.activation(qln[:], p_aq[:], AF.Copy,
                                         scale=stats[:, 14 + par:15 + par]).then_inc(EQL, 1)
                            s.wait_ge(EQL, t + 1)
                            s.activation(junk[:, 0:QR // 2], qln[:, 0:QR // 2], AF.Square, accum_out=stats[:, 22:23])
                            s.activation(junk[:, QR // 2:QR], qln[:, QR // 2:QR], AF.Square,
                                         accum_out=stats[:, 23:24]).then_inc(SUQ, 1)
                            s.wait_ge(VSS, t + 1)
                            sfence(s, s.activation(stats[:, 24:25], stats[:, 27:28], AF.Ln))
                            s.activation(stats[:, 16:17], stats[:, 24:25], AF.Exp, scale=0.5).then_inc(SUQ2, 1)
                            for g in range(2):
                                s.wait_ge(MQT, t * 2 + g + 1)
                                nj = 8 if g == 0 else 4
                                for j8 in range(nj):
                                    j = g * 8 + j8
                                    last = s.copy(qcT[:, j * R + t * 128:j * R + (t + 1) * 128],
                                                  p_tr[g % 2][:, j8 * 128:(j8 + 1) * 128])
                                last.then_inc(EQT, 1)

                    @block.vector
                    def _(v):
                        for t in range(NT):
                            par = t % 2
                            v.wait_ge(LA, 16 * (t + 1))
                            vfence(v, v.reduce_sum(stats[:, 0:1], xt[0][:], axis=AX.X, negate=True))
                            vfence(v, v.tensor_scalar(stats[:, 2 + par:3 + par], stats[:, 0:1], 1.0 / D, None, ALU.mult))
                            if t >= 1:
                                v.wait_ge(MTR, (t - 1) * 4 + 4)
                            v.tensor_scalar(xcb[0][:], xt[0][:], stats[:, 2 + par:3 + par],
                                            None, ALU.add).then_inc(VXC, 1)
                            v.tensor_mul(stats[:, 4:5], stats[:, 2 + par:3 + par], stats[:, 2 + par:3 + par])
                            par2 = (t % 2) * 22
                            v.wait_ge(SQ, t + 1)
                            vfence(v, v.tensor_add(stats[:, 5:6], stats[:, 6 + par2:7 + par2], stats[:, 7 + par2:8 + par2]))
                            vfence(v, v.tensor_scalar(stats[:, 8:9], stats[:, 5:6], 1.0 / D, None, ALU.mult))
                            v.tensor_sub(stats[:, 10 + par:11 + par], stats[:, 8:9], stats[:, 4:5]).then_inc(VVAR, 1)
                            v.wait_ge(SUQ, t + 1)
                            vfence(v, v.tensor_add(stats[:, 26:27], stats[:, 22:23], stats[:, 23:24]))
                            v.tensor_scalar(stats[:, 27:28], stats[:, 26:27], 1.0 / QR, None, ALU.mult).then_inc(VSS, 1)
                            v.wait_ge(SUQ2, t + 1)
                            vfence(v, v.tensor_scalar(stats[:, 20:21], stats[:, 16:17], RMS_EPS, None, ALU.add))
                            vfence(v, v.reciprocal(stats[:, 18:19], stats[:, 20:21]))
                            if t >= 1:
                                v.wait_ge(MQT, (t - 1) * 2 + 2)
                            v.scalar_tensor_tensor(qcb[0][:], qln[:], stats[:, 18:19], gqa_b[:],
                                                   ALU.mult, ALU.mult).then_inc(VQC, 1)

                    @block.tensor
                    def _(tn):
                        tn.wait_ge(CLD, cld_n[0])
                        tn.wait_ge(WLD, wld_n[0])
                        for t in range(NT):
                            par = 0
                            tn.wait_ge(VXC, t + 1)
                            for g in range(4):
                                if g >= 2:
                                    tn.wait_ge(EXT, t * 4 + g - 1)
                                elif t >= 1:
                                    if g == 0:
                                        tn.wait_ge(EQT, (t - 1) * 2 + 1)
                                    else:
                                        tn.wait_ge(EQT, (t - 1) * 2 + 2)
                                for j8 in range(8):
                                    j = g * 8 + j8
                                    mm = tn.matmul(p_tr[g % 2][:, j8 * 128:(j8 + 1) * 128],
                                                   xcb[par][:, j * 128:(j + 1) * 128], ident_sb[:],
                                                   start=True, stop=True)
                                mm.then_inc(MTR, 1)
                            tn.wait_ge(EXT, t * 4 + 4)
                            if t >= 1:
                                tn.wait_ge(EQL, t)
                            for nch in range(3):
                                for j in range(32):
                                    mm = tn.matmul(p_aq[:, nch * 512:(nch + 1) * 512],
                                                   xcT[par][:, j * 128:(j + 1) * 128],
                                                   wqa_sb[:, j * QR + nch * 512:j * QR + nch * 512 + 512],
                                                   start=(j == 0), stop=(j == 31))
                            mm.then_inc(MAQ, 1)
                            tn.wait_ge(VQC, t + 1)
                            for g in range(2):
                                tn.wait_ge(EXT, t * 4 + 3 + g)
                                nj = 8 if g == 0 else 4
                                for j8 in range(nj):
                                    j = g * 8 + j8
                                    mm = tn.matmul(p_tr[g % 2][:, j8 * 128:(j8 + 1) * 128],
                                                   qcb[par][:, j * 128:(j + 1) * 128], ident_sb[:],
                                                   start=True, stop=True)
                                mm.then_inc(MQT, 1)

                    barrier()

                kcT = EA(nc.sbuf_tensor("kcT", [128, 4 * S], BF16))
                krT = EA(nc.sbuf_tensor("krT", [65, S], BF16))
                with ExitStack() as ctxKV:
                    EKV = ctxKV.enter_context
                    xt = [EKV(nc.sbuf_tensor(f"kxt{i}", [128, D], F32)) for i in range(2)]
                    xcb = [EKV(nc.sbuf_tensor(f"kxcb{i}", [128, D], BF16)) for i in range(2)]
                    xcT = [EKV(nc.sbuf_tensor(f"kxcT{i}", [128, D], BF16)) for i in range(2)]
                    wkva_sb = EKV(nc.sbuf_tensor("wkva_sb", [128, 32 * (KVR + RD)], BF16))
                    gkva_b = EKV(nc.sbuf_tensor("gkva_b", [128, KVR], F32))
                    kvln = EKV(nc.sbuf_tensor("kvln", [128, KVR + RD], F32))
                    kcb = [EKV(nc.sbuf_tensor(f"kcb{i}", [128, KVR], BF16)) for i in range(2)]
                    ckb = [EKV(nc.sbuf_tensor(f"ckb{i}", [128, RD], F32)) for i in range(2)]
                    skb = [EKV(nc.sbuf_tensor(f"skb{i}", [128, RD], F32)) for i in range(2)]
                    krf = EKV(nc.sbuf_tensor("krf", [128, RD], BF16))
                    krt1 = EKV(nc.sbuf_tensor("krt1", [128, RD], F32))
                    krt2 = EKV(nc.sbuf_tensor("krt2", [128, RD], F32))
                    p_akv = EKV(nc.psum_tensor("p_akv", [128, 1024], F32))
                    p_t2 = [EKV(nc.psum_tensor(f"p_t2{i}", [128, 1024], F32)) for i in range(2)]

                    NT2 = 64
                    SQ2 = sem("k_sq")
                    VXC2 = sem("k_xc")
                    VVAR2 = sem("k_var")
                    MTR2 = sem("k_mtr")
                    EXT2 = sem("k_ext")
                    MKV = sem("k_mkv")
                    EKL = sem("k_ekl")
                    SUK = sem("k_suk")
                    VKC = sem("k_vkc")
                    VKR = sem("k_vkr")
                    MKT = sem("k_mkt")
                    EKT = sem("k_ekt")

                    @block.gpsimd
                    def _(gp):
                        for s2 in range(2):
                            gp.dma_start(
                                out=wkva_sb[:].rearrange("p (s n) -> p s n", s=32)[:, s2 * 16:(s2 + 1) * 16, :],
                                in_=w_kva.rearrange("(s p) n -> p s n", p=128)[:, s2 * 16:(s2 + 1) * 16, :],
                            ).then_inc(WLD, 16)
                    wld_n[0] += 32

                    @block.vector
                    def _(v):
                        v.memset(krT[64:65, :], -1.0).then_inc(VKR, 1)

                    @block.sync
                    def _(sync):
                        sync.dma_start(out=gkva_b[:], in_=g_kva.broadcast_to((128, KVR))).then_inc(CLD, 16)
                        for t in range(NT2):
                            par = t % 2
                            sl = [LA, LB][par]
                            cl = [LC, LD2][par]
                            if t >= 2:
                                sync.wait_ge(VXC2, t - 1)
                                sync.wait_ge(SQ2, t - 1)
                            sync.dma_start(out=xt[par][:], in_=x_full[t * 128:(t + 1) * 128, :]).then_inc(sl, 16)
                            if t >= 2:
                                sync.wait_ge(VKR, t)
                            sync.dma_start(out=ckb[par][:], in_=cosk[t * 128:(t + 1) * 128, :]).then_inc(cl, 16)
                            sync.dma_start(out=skb[par][:], in_=sink[t * 128:(t + 1) * 128, :]).then_inc(cl, 16)
                    cld_n[0] += 16

                    XB1 = 16 * 8
                    XB2 = 0

                    @block.scalar
                    def _(s):
                        s.wait_ge(CLD, cld_n[0])
                        s.wait_ge(WLD, wld_n[0])
                        for t in range(NT2):
                            par = t % 2
                            sl = [LA, LB][par]
                            base = [XB1, XB2][par]
                            par2 = par * 22
                            s.wait_ge(sl, base + 16 * (t // 2 + 1))
                            sfence(s, s.activation(junk[:], xt[par][:, 0:2048], AF.Square,
                                         accum_out=stats[:, 6 + par2:7 + par2]))
                            s.activation(junk[:], xt[par][:, 2048:4096], AF.Square,
                                         accum_out=stats[:, 7 + par2:8 + par2]).then_inc(SQ2, 1)
                            s.wait_ge(VVAR2, t + 1)
                            sfence(s, s.activation(stats[:, 12:13], stats[:, 10 + par:11 + par], AF.Ln, bias=eps_sb[:]))
                            s.activation(stats[:, 14 + par:15 + par], stats[:, 12:13], AF.Exp, scale=-0.5)
                            if t >= 2:
                                s.wait_ge(MKV, t - 1)
                            for g in range(4):
                                s.wait_ge(MTR2, t * 4 + g + 1)
                                s.copy(xcT[par][:, g * 1024:(g + 1) * 1024], p_t2[g % 2][:]).then_inc(EXT2, 1)
                            s.wait_ge(MKV, t + 1)
                            if t >= 1:
                                s.wait_ge(VKC, t)
                                s.wait_ge(VKR, t + 1)
                            s.activation(kvln[:], p_akv[:, 0:KVR + RD], AF.Copy,
                                         scale=stats[:, 14 + par:15 + par]).then_inc(EKL, 1)
                            s.wait_ge(EKL, t + 1)
                            sfence(s, s.activation(junk[:, 0:KVR], kvln[:, 0:KVR], AF.Square, accum_out=stats[:, 22:23]))
                            sfence(s, s.activation(stats[:, 24:25], stats[:, 22:23], AF.Ln, scale=1.0 / KVR))
                            s.activation(stats[:, 16:17], stats[:, 24:25], AF.Exp, scale=0.5).then_inc(SUK, 1)
                            s.wait_ge(MKT, t + 1)
                            for j4 in range(4):
                                s.copy(kcT[:, j4 * S + t * 128:j4 * S + (t + 1) * 128],
                                       p_t2[0][:, j4 * 128:(j4 + 1) * 128])
                            s.copy(krT[0:64, t * 128:(t + 1) * 128],
                                   p_t2[0][0:64, 512:640]).then_inc(EKT, 1)

                    @block.vector
                    def _(v):
                        for t in range(NT2):
                            par = t % 2
                            sl = [LA, LB][par]
                            cl = [LC, LD2][par]
                            base = [XB1, XB2][par]
                            v.wait_ge(sl, base + 16 * (t // 2 + 1))
                            vfence(v, v.reduce_sum(stats[:, 0:1], xt[par][:], axis=AX.X, negate=True))
                            vfence(v, v.tensor_scalar(stats[:, 2 + par:3 + par], stats[:, 0:1], 1.0 / D, None, ALU.mult))
                            if t >= 2:
                                v.wait_ge(MTR2, (t - 2) * 4 + 4)
                            v.tensor_scalar(xcb[par][:], xt[par][:], stats[:, 2 + par:3 + par],
                                            None, ALU.add).then_inc(VXC2, 1)
                            v.tensor_mul(stats[:, 4:5], stats[:, 2 + par:3 + par], stats[:, 2 + par:3 + par])
                            par2 = par * 22
                            v.wait_ge(SQ2, t + 1)
                            vfence(v, v.tensor_add(stats[:, 5:6], stats[:, 6 + par2:7 + par2], stats[:, 7 + par2:8 + par2]))
                            vfence(v, v.tensor_scalar(stats[:, 8:9], stats[:, 5:6], 1.0 / D, None, ALU.mult))
                            v.tensor_sub(stats[:, 10 + par:11 + par], stats[:, 8:9], stats[:, 4:5]).then_inc(VVAR2, 1)
                            v.wait_ge(SUK, t + 1)
                            vfence(v, v.tensor_scalar(stats[:, 20:21], stats[:, 16:17], RMS_EPS, None, ALU.add))
                            vfence(v, v.reciprocal(stats[:, 18:19], stats[:, 20:21]))
                            if t >= 2:
                                v.wait_ge(MKT, t - 1)
                            v.scalar_tensor_tensor(kcb[par][:], kvln[:, 0:KVR], stats[:, 18:19], gkva_b[:],
                                                   ALU.mult, ALU.mult).then_inc(VKC, 1)
                            v.wait_ge(cl, 32 * (t // 2 + 1))
                            if t >= 1:
                                v.wait_ge(MKT, t)
                            v.tensor_scalar(krt1[:, 0:32], kvln[:, KVR + 32:KVR + 64], -1.0, None, ALU.mult)
                            vfence(v, v.tensor_copy(krt1[:, 32:64], kvln[:, KVR:KVR + 32]))
                            v.tensor_mul(krt2[:], kvln[:, KVR:KVR + RD], ckb[par][:])
                            vfence(v, v.tensor_mul(krt1[:], krt1[:], skb[par][:]))
                            v.tensor_add(krf[:], krt2[:], krt1[:]).then_inc(VKR, 1)

                    @block.tensor
                    def _(tn):
                        tn.wait_ge(CLD, cld_n[0])
                        tn.wait_ge(WLD, wld_n[0])
                        for t in range(NT2):
                            par = t % 2
                            tn.wait_ge(VXC2, t + 1)
                            for g in range(4):
                                if g >= 2:
                                    tn.wait_ge(EXT2, t * 4 + g - 1)
                                elif t >= 1:
                                    if g == 0:
                                        tn.wait_ge(EKT, t)
                                    else:
                                        tn.wait_ge(EXT2, (t - 1) * 4 + 4)
                                for j8 in range(8):
                                    j = g * 8 + j8
                                    mm = tn.matmul(p_t2[g % 2][:, j8 * 128:(j8 + 1) * 128],
                                                   xcb[par][:, j * 128:(j + 1) * 128], ident_sb[:],
                                                   start=True, stop=True)
                                mm.then_inc(MTR2, 1)
                            tn.wait_ge(EXT2, t * 4 + 4)
                            if t >= 1:
                                tn.wait_ge(EKL, t)
                            for nch in range(2):
                                lo = nch * 512
                                wdt = 512 if nch == 0 else 64
                                for j in range(32):
                                    mm = tn.matmul(p_akv[:, lo:lo + wdt],
                                                   xcT[par][:, j * 128:(j + 1) * 128],
                                                   wkva_sb[:, j * (KVR + RD) + lo:j * (KVR + RD) + lo + wdt],
                                                   start=(j == 0), stop=(j == 31))
                            mm.then_inc(MKV, 1)
                            tn.wait_ge(VKC, t + 1)
                            tn.wait_ge(VKR, t + 2)
                            tn.wait_ge(EXT2, t * 4 + 3)
                            tn.wait_ge(EKT, t)
                            for j4 in range(4):
                                tn.matmul(p_t2[0][:, j4 * 128:(j4 + 1) * 128],
                                          kcb[par][:, j4 * 128:(j4 + 1) * 128], ident_sb[:],
                                          start=True, stop=True)
                            mm = tn.matmul(p_t2[0][0:64, 512:640], krf[:], ident_sb[:],
                                           start=True, stop=True)
                            mm.then_inc(MKT, 1)

                    barrier()

            if taps:
                @block.sync
                def _(sync):
                    sync.wait_ge(QSP, 16)
                    sync.dma_start(out=tap_aps["t_qcT"], in_=d_qcT).then_inc(SG, 16)
                    sync.dma_start(out=tap_aps["t_kcT"], in_=kcT[:]).then_inc(SG, 16)
                    sync.dma_start(out=tap_aps["t_krT"], in_=krT[:]).then_inc(SG, 16)
                    sync.wait_ge(SG, 48)
                barrier()
            if max_phase <= 1:
                return nc

            with ExitStack() as ctxT:
                ET = ctxT.enter_context
                qcT2 = ET(nc.sbuf_tensor("qcT2", [128, 12 * R], BF16))
                wqb_sb = [ET(nc.sbuf_tensor(f"wqb{i}", [128, 12 * QK], BF16)) for i in range(2)]
                wkvb_sb = [ET(nc.sbuf_tensor(f"wkvb{i}", [128, 4 * (ND + VD)], BF16)) for i in range(2)]
                cqT = ET(nc.sbuf_tensor("cqT", [RD, R], F32))
                sqT = ET(nc.sbuf_tensor("sqT", [RD, R], F32))
                qT_nope = ET(nc.sbuf_tensor("qT_nope", [128, R], BF16))
                qT_s2 = ET(nc.sbuf_tensor("qT_s2", [65, R], BF16))
                qrr = ET(nc.sbuf_tensor("qrr", [RD, R], F32))
                rt1 = ET(nc.sbuf_tensor("rt1", [RD, R], F32))
                kT_nope = ET(nc.sbuf_tensor("kT_nope", [128, S], BF16))
                Vb = ET(nc.sbuf_tensor("Vb", [128, 64 * VD], BF16))
                sqb = [ET(nc.sbuf_tensor(f"sqb{i}", [128, 512], BF16)) for i in range(2)]
                sqb2 = [ET(nc.sbuf_tensor(f"sqb2{i}", [65, 512], BF16)) for i in range(2)]
                sqq0 = ET(nc.sbuf_tensor("sqq0", [128, 512], BF16))
                sqq1 = ET(nc.sbuf_tensor("sqq1", [64, 512], BF16))
                PT = ET(nc.sbuf_tensor("PT", [128, 4 * 1024], BF16))
                maxb = ET(nc.sbuf_tensor("maxb", [128, 18], F32))
                lkv = ET(nc.sbuf_tensor("lkv", [1, 4], F32))
                nqln = ET(nc.sbuf_tensor("nqln", [1, 1024], F32))
                lacc = ET(nc.sbuf_tensor("lacc", [128, 2048], F32))
                rlb = ET(nc.sbuf_tensor("rlb", [128, 1024], F32))
                OT_h = ET(nc.sbuf_tensor("OT_h", [128, R], BF16))
                p_sp = [ET(nc.psum_tensor(f"p_sp{i}", [128, 1024], F32)) for i in range(2)]
                p_pvT = ET(nc.psum_tensor("p_pvT", [128, 1024], F32))
                p_gen = [ET(nc.psum_tensor(f"p_gen{i}", [128, 512], F32)) for i in range(2)]

                AKGM = sem("a_kgm")
                AKGE = sem("a_kge")
                ASQK = sem("a_sqk")
                ANKM = sem("a_nkm")
                ANKR = sem("a_nkr")
                AMK = sem("a_mk")
                ALK = sem("a_lk")
                ALKH = sem("a_lkh")
                AVGM = sem("a_vgm")
                AVGE = sem("a_vge")
                AQGM = sem("a_qgm")
                AQGE = sem("a_qge")
                AQRM = sem("a_qrm")
                AQRE = sem("a_qre")
                AROP = sem("a_rop")
                ASQQ = sem("a_sqq")
                ANQM = sem("a_nqm")
                AMRW = sem("a_mrw")
                AEVQ = sem("a_evq")
                ASP = sem("a_sp")
                AEXP = sem("a_exp")
                APV = sem("a_pv")
                ALC = sem("a_lc")
                ABC = sem("a_bc")
                AOTE = sem("a_ote")

                @block.sync
                def _(sync):
                    sync.dma_start(out=cqT[:], in_=cosqT).then_inc(CLD, 16)
                    sync.dma_start(out=sqT[:], in_=sinqT).then_inc(CLD, 16)
                    sync.wait_ge(QSP, 16)
                    sync.dma_start(out=qcT2[:], in_=d_qcT).then_inc(CLD, 16)
                cld_n[0] += 48

                @block.gpsimd
                def _(gp):
                    for h in range(nheads):
                        par = h % 2
                        sl = [LE, LF][par]
                        if h >= 2:
                            gp.wait_ge(AQRM, (h - 2) * 2 + 2)
                            gp.wait_ge(AVGM, (h - 2) * 64 + 64)
                        gp.dma_start(
                            out=wqb_sb[par][:].rearrange("p (s n) -> p s n", s=12),
                            in_=w_qb[:, h * QK:(h + 1) * QK].rearrange("(s p) n -> p s n", p=128),
                        ).then_inc(sl, 16)
                        gp.dma_start(
                            out=wkvb_sb[par][:].rearrange("p (s n) -> p s n", s=4),
                            in_=w_kvb[:, h * (ND + VD):(h + 1) * (ND + VD)].rearrange("(s p) n -> p s n", p=128),
                        ).then_inc(sl, 16)

                @block.tensor
                def _(tn):
                    tn.wait_ge(CLD, cld_n[0])
                    tn.wait_ge(WLD, wld_n[0])
                    for h in range(nheads):
                        par = h % 2
                        sl = [LE, LF][par]
                        tn.wait_ge(sl, 32 * (h // 2 + 1))
                        hb16 = h * 16
                        hb64 = h * 64
                        hb2 = h * 2
                        for c in range(16):
                            gb = p_gen[c % 2]
                            if c >= 2:
                                tn.wait_ge(AKGE, hb16 + c - 1)
                            elif h >= 1:
                                tn.wait_ge(AVGE, (h - 1) * 64 + 63 + c)
                            for j in range(4):
                                mm = tn.matmul(gb[:],
                                               wkvb_sb[par][:, j * (ND + VD):j * (ND + VD) + ND],
                                               kcT[:, j * S + c * 512:j * S + c * 512 + 512],
                                               start=(j == 0), stop=(j == 3))
                            mm.then_inc(AKGM, 1)
                        for qc in range(2):
                            gb = p_gen[qc % 2]
                            tn.wait_ge(AKGE, hb16 + 15 + qc)
                            for j in range(12):
                                mm = tn.matmul(gb[:],
                                               wqb_sb[par][:, j * QK:j * QK + ND],
                                               qcT2[:, j * R + qc * 512:j * R + qc * 512 + 512],
                                               start=(j == 0), stop=(j == 11))
                            mm.then_inc(AQGM, 1)
                        for qc in range(2):
                            gb = p_gen[qc % 2]
                            tn.wait_ge(AQGE, hb2 + qc + 1)
                            for j in range(12):
                                mm = tn.matmul(gb[0:64, :],
                                               wqb_sb[par][:, j * QK + ND:j * QK + QK],
                                               qcT2[:, j * R + qc * 512:j * R + qc * 512 + 512],
                                               start=(j == 0), stop=(j == 11))
                            mm.then_inc(AQRM, 1)
                        for qc in range(2):
                            gb = p_gen[qc % 2]
                            tn.wait_ge(ASQQ, h * 4 + qc * 2 + 2)
                            tn.wait_ge(AQRE, hb2 + qc + 1)
                            mm = tn.matmul(gb[:], ones_sb[:], sqq0[:], start=True, stop=False)
                            mm = tn.matmul(gb[:], ones_sb[0:64, :], sqq1[0:64, :], start=False, stop=True)
                            mm.then_inc(ANQM, 1)
                        for c in range(16):
                            gb = p_gen[c % 2]
                            tn.wait_ge(ASQK, hb16 + c + 1)
                            if c >= 2:
                                tn.wait_ge(ANKR, hb16 + c - 1)
                            else:
                                tn.wait_ge(AEVQ, hb2 + c + 1)
                            mm = tn.matmul(gb[:], ones_sb[:], sqb[c % 2][:], start=True, stop=False)
                            mm = tn.matmul(gb[:], ones_sb[0:65, :], sqb2[c % 2][:],
                                           start=False, stop=True)
                            mm.then_inc(ANKM, 1)
                        for kt in range(64):
                            gb = p_gen[kt % 2]
                            if kt >= 2:
                                tn.wait_ge(AVGE, hb64 + kt - 1)
                            else:
                                tn.wait_ge(ANKR, hb16 + 15 + kt)
                            for j in range(4):
                                mm = tn.matmul(gb[:, 0:128],
                                               kcT[:, j * S + kt * 128:j * S + (kt + 1) * 128],
                                               wkvb_sb[par][:, j * (ND + VD) + ND:j * (ND + VD) + ND + VD],
                                               start=(j == 0), stop=(j == 3))
                            mm.then_inc(AVGM, 1)
                        def emit_pv(kt):
                            tn.wait_ge(AEXP, hb64 + kt + 1)
                            vs = Vb[:, kt * VD:(kt + 1) * VD]
                            slot = (kt % 4) * 1024
                            tn.matmul(p_pvT[:, 0:512], vs, PT[:, slot:slot + 512],
                                      start=(kt == 0), stop=(kt == 63))
                            mm = tn.matmul(p_pvT[:, 512:1024], vs, PT[:, slot + 512:slot + 1024],
                                           start=(kt == 0), stop=(kt == 63))
                            mm.then_inc(APV, 1)

                        for kt in range(64):
                            spb = p_sp[kt % 2]
                            if kt >= 2:
                                tn.wait_ge(AEXP, hb64 + kt - 1)
                            elif h >= 1:
                                tn.wait_ge(AOTE, h)
                            if kt == 0:
                                tn.wait_ge(AMRW, hb2 + 2)
                                tn.wait_ge(AROP, hb2 + 2)
                                tn.wait_ge(AKGE, hb16 + 16)
                                tn.wait_ge(AVGE, hb64 + 64)
                            ks = kT_nope[:, kt * 128:(kt + 1) * 128]
                            rs = krT[:, kt * 128:(kt + 1) * 128]
                            tn.matmul(spb[:, 0:512], ks, qT_nope[:, 0:512], start=True, stop=False)
                            tn.matmul(spb[:, 512:1024], ks, qT_nope[:, 512:1024], start=True, stop=False)
                            tn.matmul(spb[:, 0:512], rs, qT_s2[:, 0:512], start=False, stop=True)
                            mm = tn.matmul(spb[:, 512:1024], rs, qT_s2[:, 512:1024], start=False, stop=True)
                            mm.then_inc(ASP, 1)
                            if kt >= 1:
                                emit_pv(kt - 1)
                        emit_pv(63)
                        tn.wait_ge(ALC, hb64 + 64)
                        tn.wait_ge(AEXP, hb64 + 64)
                        tn.matmul(p_sp[0][:, 0:512], onesf2_sb[:], lacc[:, 0:512], start=True, stop=False)
                        tn.matmul(p_sp[0][:, 0:512], onesf2_sb[:], lacc[:, 1024:1536], start=False, stop=True)
                        tn.matmul(p_sp[0][:, 512:1024], onesf2_sb[:], lacc[:, 512:1024], start=True, stop=False)
                        mm = tn.matmul(p_sp[0][:, 512:1024], onesf2_sb[:], lacc[:, 1536:2048], start=False, stop=True)
                        mm.then_inc(ABC, 1)

                @block.scalar
                def _(s):
                    for h in range(nheads):
                        hb16, hb64, hb2 = h * 16, h * 64, h * 2
                        for c in range(16):
                            s.wait_ge(AKGM, hb16 + c + 1)
                            s.copy(kT_nope[:, c * 512:(c + 1) * 512], p_gen[c % 2][:]).then_inc(AKGE, 1)
                        for qc in range(2):
                            s.wait_ge(AQGM, hb2 + qc + 1)
                            s.activation(qT_nope[:, qc * 512:qc * 512 + 512], p_gen[qc % 2][:],
                                         AF.Copy, scale=SC).then_inc(AQGE, 1)
                        for qc in range(2):
                            s.wait_ge(AQRM, hb2 + qc + 1)
                            s.activation(qrr[:, qc * 512:qc * 512 + 512], p_gen[qc % 2][0:64, :],
                                         AF.Copy, scale=SC).then_inc(AQRE, 1)
                        for qc in range(2):
                            s.wait_ge(ANQM, hb2 + qc + 1)
                            s.activation(nqln[0:1, qc * 512:qc * 512 + 512],
                                         p_gen[qc % 2][0:1, :], AF.Ln).then_inc(AEVQ, 1)
                        s.wait_ge(AMK, h + 1)
                        s.activation(lkv[0:1, 0:1], maxb[0:1, 16:17], AF.Ln).then_inc(ALK, 1)
                        for qc in range(2):
                            s.wait_ge(ALKH, h + 1)
                            s.wait_ge(AEVQ, hb2 + qc + 1)
                            s.activation(qT_s2[64:65, qc * 512:qc * 512 + 512],
                                         nqln[0:1, qc * 512:qc * 512 + 512], AF.Exp,
                                         scale=0.5, bias=lkv[0:1, 1:2]).then_inc(AMRW, 1)
                        for kt in range(64):
                            s.wait_ge(ASP, hb64 + kt + 1)
                            pvw = hb64 + kt - 3
                            if pvw >= 1:
                                s.wait_ge(APV, pvw)
                                s.wait_ge(ALC, pvw)
                            s.activation(PT[:, (kt % 4) * 1024:(kt % 4) * 1024 + 1024],
                                         p_sp[kt % 2][:], AF.Exp).then_inc(AEXP, 1)

                @block.vector
                def _(v):
                    for h in range(nheads):
                        hb16, hb64, hb2 = h * 16, h * 64, h * 2
                        for qc in range(2):
                            lo = qc * 512
                            v.wait_ge(AQRE, hb2 + qc + 1)
                            v.tensor_scalar(rt1[0:32, lo:lo + 512], qrr[32:64, lo:lo + 512], -1.0, None, ALU.mult)
                            vfence(v, v.tensor_copy(rt1[32:64, lo:lo + 512], qrr[0:32, lo:lo + 512]))
                            vfence(v, v.tensor_mul(qrr[:, lo:lo + 512], qrr[:, lo:lo + 512], cqT[:, lo:lo + 512]))
                            vfence(v, v.tensor_mul(rt1[:, lo:lo + 512], rt1[:, lo:lo + 512], sqT[:, lo:lo + 512]))
                            v.tensor_add(qT_s2[0:64, lo:lo + 512], qrr[:, lo:lo + 512],
                                         rt1[:, lo:lo + 512]).then_inc(AROP, 1)
                            v.wait_ge(AQGE, hb2 + qc + 1)
                            if h >= 1 or qc >= 1:
                                v.wait_ge(ANQM, hb2 + qc)
                            v.tensor_mul(sqq0[:], qT_nope[:, lo:lo + 512],
                                         qT_nope[:, lo:lo + 512]).then_inc(ASQQ, 1)
                            v.wait_ge(AROP, hb2 + qc + 1)
                            v.tensor_mul(sqq1[:], qT_s2[0:64, lo:lo + 512],
                                         qT_s2[0:64, lo:lo + 512]).then_inc(ASQQ, 1)
                        for c in range(16):
                            v.wait_ge(AKGE, hb16 + c + 1)
                            if c >= 2:
                                v.wait_ge(ANKM, hb16 + c - 1)
                            elif h >= 1:
                                v.wait_ge(ANKM, (h - 1) * 16 + 15 + c)
                            v.tensor_mul(sqb[c % 2][:], kT_nope[:, c * 512:(c + 1) * 512],
                                         kT_nope[:, c * 512:(c + 1) * 512])
                            v.tensor_mul(sqb2[c % 2][:], krT[:, c * 512:(c + 1) * 512],
                                         krT[:, c * 512:(c + 1) * 512]).then_inc(ASQK, 1)
                            if c >= 2:
                                v.wait_ge(ANKM, hb16 + c - 1)
                                v.reduce_max(maxb[:, c - 2:c - 1], p_gen[c % 2][:], axis=AX.X).then_inc(ANKR, 1)
                        for c in range(14, 16):
                            v.wait_ge(ANKM, hb16 + c + 1)
                            v.reduce_max(maxb[:, c:c + 1], p_gen[c % 2][:], axis=AX.X).then_inc(ANKR, 1)
                        v.wait_ge(ANKR, hb16 + 16)
                        v.reduce_max(maxb[:, 16:17], maxb[:, 0:16], axis=AX.X).then_inc(AMK, 1)
                        v.wait_ge(ALK, h + 1)
                        v.tensor_scalar(lkv[0:1, 1:2], lkv[0:1, 0:1], 0.5, None, ALU.mult).then_inc(ALKH, 1)
                        for kt in range(64):
                            v.wait_ge(AVGM, hb64 + kt + 1)
                            v.tensor_copy(Vb[:, kt * VD:(kt + 1) * VD],
                                          p_gen[kt % 2][:, 0:128]).then_inc(AVGE, 1)
                        v.wait_ge(ABC, h)
                        v.memset(lacc[:, 0:1024], 0.0)
                        vfence(v, v.memset(lacc[:, 1024:2048], 0.0))
                        for kt in range(64):
                            v.wait_ge(AEXP, hb64 + kt + 1)
                            if kt >= 2:
                                v.wait_ge(ALC, hb64 + kt - 1)
                            lo2 = (kt % 2) * 1024
                            v.tensor_add(lacc[:, lo2:lo2 + 1024], lacc[:, lo2:lo2 + 1024],
                                         PT[:, (kt % 4) * 1024:(kt % 4) * 1024 + 1024]).then_inc(ALC, 1)
                        v.wait_ge(ABC, h + 1)
                        vfence(v, v.reciprocal(rlb[:], p_sp[0][:]))
                        v.wait_ge(APV, hb64 + 64)
                        if h >= 1:
                            v.wait_ge(SA, 16 * h)
                        v.tensor_mul(OT_h[:], p_pvT[:], rlb[:]).then_inc(AOTE, 1)

                @block.sync
                def _(sync):
                    for h in range(nheads):
                        sync.wait_ge(AOTE, h + 1)
                        sync.dma_start(out=d_OT[h, :, :], in_=OT_h[:]).then_inc(SA, 16)

                if taps:
                    @block.sync
                    def _(sync):
                        sync.wait_ge(AKGE, 16)
                        sync.dma_start(out=tap_aps["t_kT0"], in_=kT_nope[:]).then_inc(SG, 16)
                        sync.wait_ge(AVGE, 64)
                        sync.dma_start(out=tap_aps["t_V0"], in_=Vb[:]).then_inc(SG, 16)
                        sync.wait_ge(AOTE, 1)
                        sync.dma_start(out=tap_aps["t_OT0"], in_=OT_h[:]).then_inc(SG, 16)

                barrier()
                if max_phase <= 2:
                    return nc

        with ExitStack() as ctxLF:
            ELF = ctxLF.enter_context
            ln2T = ELF(nc.sbuf_tensor("ln2T", [128, 32 * R], BF16))
            sums = ELF(nc.sbuf_tensor("sums", [128, 8], F32))
            ssqs = ELF(nc.sbuf_tensor("ssqs", [128, 8], F32))

            with ExitStack() as ctxP:
                EP = ctxP.enter_context
                ctxPP = ExitStack()
                EPP = ctxPP.enter_context
                OT_all = EPP(nc.sbuf_tensor("OT_all", [128, 32 * R], BF16))
                wo_sb = [EPP(nc.sbuf_tensor(f"wo{i}", [128, 32 * 512], BF16)) for i in range(2)]
                atile = [EPP(nc.sbuf_tensor(f"atile{i}", [128, 512], F32)) for i in range(2)]
                tstat = EPP(nc.sbuf_tensor("tstat", [128, 4], F32))
                p_pr = [EPP(nc.psum_tensor(f"p_pr{i}", [128, 512], F32)) for i in range(4)]

                PMM = sem("p_mm")
                PEV = sem("p_ev")
                PSQ = sem("p_sq")
                PST = sem("p_stat")

                @block.sync
                def _(sync):
                    sync.wait_ge(SA, 16 * nheads)
                    for s2 in range(2):
                        sync.dma_start(
                            out=OT_all[:].rearrange("p (s n) -> p s n", s=32)[:, s2 * 16:(s2 + 1) * 16, :],
                            in_=d_OT.rearrange("h p q -> p h q")[:, s2 * 16:(s2 + 1) * 16, :],
                        ).then_inc(CLD, 16)
                cld_n[0] += 32

                @block.gpsimd
                def _(gp):
                    gp.wait_ge(SA, 16 * nheads)
                    for dc in range(8):
                        par = dc % 2
                        sl = [LE, LF][par]
                        if dc >= 2:
                            gp.wait_ge(PMM, (dc - 2) * 8 + 8)
                        for s2 in range(2):
                            gp.dma_start(
                                out=wo_sb[par][:].rearrange("p (s n) -> p s n", s=32)[:, s2 * 16:(s2 + 1) * 16, :],
                                in_=w_o[:, dc * 512:(dc + 1) * 512].rearrange("(s p) n -> p s n", p=128)[:, s2 * 16:(s2 + 1) * 16, :],
                            ).then_inc(sl, 16)

                WOB = [32 * ((nheads + 1) // 2), 32 * (nheads // 2)]

                @block.tensor
                def _(tn):
                    tn.wait_ge(CLD, cld_n[0])
                    tn.wait_ge(WLD, wld_n[0])
                    for dc in range(8):
                        par = dc % 2
                        sl = [LE, LF][par]
                        tn.wait_ge(sl, WOB[par] + 32 * (dc // 2 + 1))
                        for qt in range(8):
                            idx = dc * 8 + qt
                            if idx >= 4:
                                tn.wait_ge(PEV, idx - 3)
                            for s32 in range(32):
                                mm = tn.matmul(p_pr[qt % 4][:],
                                               OT_all[:, s32 * R + qt * 128:s32 * R + (qt + 1) * 128],
                                               wo_sb[par][:, s32 * 512:(s32 + 1) * 512],
                                               start=(s32 == 0), stop=(s32 == 31))
                            mm.then_inc(PMM, 1)

                @block.scalar
                def _(s):
                    for dc in range(8):
                        for qt in range(8):
                            idx = dc * 8 + qt
                            par = idx % 2
                            s.wait_ge(PMM, idx + 1)
                            if idx >= 2:
                                s.wait_ge([ST0, ST1][par], 16 * (idx // 2))
                            if idx >= 2:
                                s.wait_ge(PST, idx - 1)
                            s.activation(atile[par][:], p_pr[qt % 4][:], AF.Copy,
                                         accum_out=tstat[:, 2 * par:2 * par + 1]).then_inc(PEV, 1)
                            s.wait_ge(PEV, idx + 1)
                            s.activation(junk[:, par * 512:(par + 1) * 512], atile[par][:], AF.Square,
                                         accum_out=tstat[:, 2 * par + 1:2 * par + 2]).then_inc(PSQ, 1)

                @block.vector
                def _(v):
                    v.memset(sums[:], 0.0)
                    vfence(v, v.memset(ssqs[:], 0.0))
                    for dc in range(8):
                        for qt in range(8):
                            idx = dc * 8 + qt
                            par = idx % 2
                            v.wait_ge(PSQ, idx + 1)
                            vfence(v, v.tensor_add(sums[:, qt:qt + 1], sums[:, qt:qt + 1], tstat[:, 2 * par:2 * par + 1]))
                            v.tensor_add(ssqs[:, qt:qt + 1], ssqs[:, qt:qt + 1],
                                         tstat[:, 2 * par + 1:2 * par + 2]).then_inc(PST, 1)

                @block.sync
                def _(sync):
                    for dc in range(8):
                        for qt in range(8):
                            idx = dc * 8 + qt
                            par = idx % 2
                            sync.wait_ge(PSQ, idx + 1)
                            sync.dma_start(out=d_attn[qt * 128:(qt + 1) * 128, dc * 512:(dc + 1) * 512],
                                           in_=atile[par][:]).then_inc([ST0, ST1][par], 16)

                barrier()

                if taps:
                    @block.sync
                    def _(sync):
                        sync.wait_ge(ST0, 16 * 32)
                        sync.wait_ge(ST1, 16 * 32)
                        sync.dma_start(out=tap_aps["t_attn"], in_=d_attn).then_inc(SG, 16)
                    barrier()
                if max_phase <= 2.5:
                    ctxPP.close()
                    return nc

                ctxPP.close()
                xrt = [EP(nc.sbuf_tensor(f"xrt{i}", [128, D], F32)) for i in range(2)]
                art = [EP(nc.sbuf_tensor(f"art{i}", [128, D], F32)) for i in range(2)]
                x2t = EP(nc.sbuf_tensor("x2t", [128, D], F32))
                l2f = EP(nc.sbuf_tensor("l2f", [128, D], F32))
                l2bf = EP(nc.sbuf_tensor("l2bf", [128, D], BF16))
                p_t3 = [EP(nc.psum_tensor(f"p_t3{i}", [128, 1024], F32)) for i in range(2)]

                RVAR = sem("r_var")
                RRS = sem("r_rs")
                RX2 = sem("r_x2")
                RSQ = sem("r_sq")
                RVB = sem("r_vb")
                RRS2 = sem("r_rs2")
                RL2 = sem("r_l2")
                RMM = sem("r_mm")
                REV = sem("r_ev")

                XRB = [40 * 16, 32 * 16]
                X2SB = 16 * 32

                @block.sync
                def _(sync):
                    sync.wait_ge(ST0, 16 * 32)
                    sync.wait_ge(ST1, 16 * 32)
                    for qt in range(8):
                        par = qt % 2
                        sl = [LA, LB][par]
                        if qt >= 2:
                            sync.wait_ge(RL2, qt - 1)
                        sync.dma_start(out=xrt[par][:], in_=x_loc[qt * 128:(qt + 1) * 128, :]).then_inc(sl, 16)
                        sync.dma_start(out=art[par][:], in_=d_attn[qt * 128:(qt + 1) * 128, :]).then_inc(sl, 16)
                        if qt >= 1:
                            sync.wait_ge(RX2, qt)
                            sync.dma_start(out=d_x2[(qt - 1) * 128:qt * 128, :], in_=x2t[:]).then_inc(ST0, 16)
                    sync.wait_ge(RX2, 8)
                    sync.dma_start(out=d_x2[7 * 128:8 * 128, :], in_=x2t[:]).then_inc(ST0, 16)
                    sync.wait_ge(ST0, X2SB + 16 * 8)

                @block.vector
                def _(v):
                    for qt in range(8):
                        par = qt % 2
                        sl = [LA, LB][par]
                        v.wait_ge(sl, XRB[par] + 32 * (qt // 2 + 1))
                        v.tensor_scalar(stats[:, 0:1], sums[:, qt:qt + 1], -1.0 / D, None, ALU.mult)
                        vfence(v, v.tensor_scalar(stats[:, 1:2], ssqs[:, qt:qt + 1], 1.0 / D, None, ALU.mult))
                        vfence(v, v.tensor_mul(stats[:, 2:3], stats[:, 0:1], stats[:, 0:1]))
                        v.tensor_sub(stats[:, 3:4], stats[:, 1:2], stats[:, 2:3]).then_inc(RVAR, 1)
                        v.wait_ge(RRS, qt + 1)
                        vfence(v, v.tensor_scalar(art[par][:], art[par][:], stats[:, 0:1], None, ALU.add))
                        vfence(v, v.tensor_scalar(art[par][:], art[par][:], stats[:, 4:5], None, ALU.mult))
                        if qt >= 1:
                            v.wait_ge(RSQ, qt)
                            v.wait_ge(ST0, X2SB + 16 * qt)
                        v.tensor_add(x2t[:], xrt[par][:], art[par][:]).then_inc(RX2, 1)
                        v.wait_ge(RX2, qt + 1)
                        vfence(v, v.reduce_sum(stats[:, 5:6], x2t[:], axis=AX.X, negate=True))
                        vfence(v, v.tensor_scalar(stats[:, 6:7], stats[:, 5:6], 1.0 / D, None, ALU.mult))
                        v.tensor_mul(stats[:, 7:8], stats[:, 6:7], stats[:, 6:7])
                        v.wait_ge(RSQ, qt + 1)
                        vfence(v, v.tensor_add(stats[:, 23:24], stats[:, 9:10], stats[:, 22:23]))
                        vfence(v, v.tensor_scalar(stats[:, 8:9], stats[:, 23:24], 1.0 / D, None, ALU.mult))
                        v.tensor_sub(stats[:, 10:11], stats[:, 8:9], stats[:, 7:8]).then_inc(RVB, 1)
                        v.wait_ge(RRS2, qt + 1)
                        vfence(v, v.tensor_scalar(l2f[:], x2t[:], stats[:, 6:7], None, ALU.add))
                        if qt >= 1:
                            v.wait_ge(RMM, 4 * qt)
                        v.tensor_scalar(l2bf[:], l2f[:], stats[:, 11:12], None, ALU.mult).then_inc(RL2, 1)

                @block.scalar
                def _(s):
                    for qt in range(8):
                        s.wait_ge(RVAR, qt + 1)
                        sfence(s, s.activation(stats[:, 20:21], stats[:, 3:4], AF.Ln, bias=eps_sb[:]))
                        s.activation(stats[:, 4:5], stats[:, 20:21], AF.Exp, scale=-0.5).then_inc(RRS, 1)
                        s.wait_ge(RX2, qt + 1)
                        sfence(s, s.activation(junk[:], x2t[:, 0:2048], AF.Square, accum_out=stats[:, 9:10]))
                        s.activation(junk[:], x2t[:, 2048:4096], AF.Square, accum_out=stats[:, 22:23]).then_inc(RSQ, 1)
                        s.wait_ge(RVB, qt + 1)
                        sfence(s, s.activation(stats[:, 21:22], stats[:, 10:11], AF.Ln, bias=eps_sb[:]))
                        s.activation(stats[:, 11:12], stats[:, 21:22], AF.Exp, scale=-0.5).then_inc(RRS2, 1)
                        for g in range(4):
                            s.wait_ge(RMM, qt * 4 + g + 1)
                            for j8 in range(8):
                                j = g * 8 + j8
                                last = s.copy(ln2T[:, j * R + qt * 128:j * R + (qt + 1) * 128],
                                              p_t3[g % 2][:, j8 * 128:(j8 + 1) * 128])
                            last.then_inc(REV, 1)

                @block.tensor
                def _(tn):
                    for qt in range(8):
                        tn.wait_ge(RL2, qt + 1)
                        for g in range(4):
                            if g >= 2:
                                tn.wait_ge(REV, qt * 4 + g - 1)
                            elif qt >= 1:
                                tn.wait_ge(REV, (qt - 1) * 4 + g + 3)
                            for j8 in range(8):
                                j = g * 8 + j8
                                mm = tn.matmul(p_t3[g % 2][:, j8 * 128:(j8 + 1) * 128],
                                               l2bf[:, j * 128:(j + 1) * 128], ident_sb[:],
                                               start=True, stop=True)
                            mm.then_inc(RMM, 1)

                if taps:
                    @block.sync
                    def _(sync):
                        sync.wait_ge(ST0, X2SB + 16 * 8)
                        sync.dma_start(out=tap_aps["t_x2"], in_=d_x2).then_inc(SG, 16)

                barrier()
                if max_phase <= 3:
                    return nc

            with ExitStack() as ctxF1:
                EF1 = ctxF1.enter_context
                w1_sb = [EF1(nc.sbuf_tensor(f"w1sb{i}", [128, 32 * 512], BF16)) for i in range(2)]
                b1_sb = EF1(nc.sbuf_tensor("b1_sb", [128, 128], F32))
                hT_sb = [EF1(nc.sbuf_tensor(f"hTsb{i}", [128, 1024], BF16)) for i in range(2)]
                p_h = [EF1(nc.psum_tensor(f"p_h{i}", [128, 1024], F32)) for i in range(2)]

                FMM = sem("f_mm")
                FEV = sem("f_ev")
                FST = sem("f_st")

                @block.sync
                def _(sync):
                    sync.dma_start(out=b1_sb[:], in_=b1r).then_inc(CLD, 16)
                cld_n[0] += 16

                WOB2 = [WOB[0] + 32 * 4, WOB[1] + 32 * 4]

                @block.gpsimd
                def _(gp):
                    gp.wait_ge(ST0, 16 * 40)
                    gp.wait_ge(ST1, 16 * 32)
                    for fs in range(32):
                        par = fs % 2
                        sl = [LE, LF][par]
                        if fs >= 2:
                            gp.wait_ge(FMM, (fs - 2) * 4 + 4)
                        for s2 in range(2):
                            gp.dma_start(
                                out=w1_sb[par][:].rearrange("p (s n) -> p s n", s=32)[:, s2 * 16:(s2 + 1) * 16, :],
                                in_=w1[:, fs * 512:(fs + 1) * 512].rearrange("(s p) n -> p s n", p=128)[:, s2 * 16:(s2 + 1) * 16, :],
                            ).then_inc(sl, 16)

                @block.tensor
                def _(tn):
                    tn.wait_ge(CLD, cld_n[0])
                    tn.wait_ge(WLD, wld_n[0])
                    for fs in range(32):
                        par = fs % 2
                        sl = [LE, LF][par]
                        tn.wait_ge(sl, WOB2[par] + 32 * (fs // 2 + 1))
                        for f4 in range(4):
                            key = fs * 4 + f4
                            if key >= 2:
                                tn.wait_ge(FEV, key - 1)
                            for qc in range(2):
                                for j in range(32):
                                    mm = tn.matmul(p_h[f4 % 2][:, qc * 512:qc * 512 + 512],
                                                   w1_sb[par][:, j * 512 + f4 * 128:j * 512 + (f4 + 1) * 128],
                                                   ln2T[:, j * R + qc * 512:j * R + qc * 512 + 512],
                                                   start=(j == 0), stop=(j == 31))
                            mm.then_inc(FMM, 1)

                HSB = [16 * 40, 16 * 32]

                @block.scalar
                def _(s):
                    for key in range(128):
                        par = key % 2
                        s.wait_ge(FMM, key + 1)
                        if key >= 2:
                            s.wait_ge([ST0, ST1][par], HSB[par] + 16 * (key // 2))
                        s.activation(hT_sb[par][:], p_h[key % 2][:], AF.Gelu_apprx_tanh,
                                     bias=b1_sb[:, key:key + 1]).then_inc(FEV, 1)

                @block.sync
                def _(sync):
                    for key in range(128):
                        par = key % 2
                        sync.wait_ge(FEV, key + 1)
                        sync.dma_start(out=d_hT[key * 128:(key + 1) * 128, :],
                                       in_=hT_sb[par][:]).then_inc([ST0, ST1][par], 16)

                if taps:
                    @block.sync
                    def _(sync):
                        sync.wait_ge(ST0, HSB[0] + 16 * 64)
                        sync.wait_ge(ST1, HSB[1] + 16 * 64)
                        sync.dma_start(out=tap_aps["t_hT0"], in_=d_hT[0:128, :]).then_inc(SG, 16)

                barrier()
                if max_phase <= 4:
                    return nc

            with ExitStack() as ctxF2:
                EF2 = ctxF2.enter_context
                w2_sb = [EF2(nc.sbuf_tensor(f"w2sb{i}", [128, 16 * 512], BF16)) for i in range(2)]
                hTp = [EF2(nc.sbuf_tensor(f"hTp{i}", [128, 16 * 1024], BF16)) for i in range(2)]
                b2_sb = EF2(nc.sbuf_tensor("b2_sb", [1, D], F32))
                o2t = [EF2(nc.sbuf_tensor(f"o2t{i}", [128, 512], F32)) for i in range(2)]
                tstat2 = EF2(nc.sbuf_tensor("tstat2", [128, 4], F32))
                p_s = [EF2(nc.psum_tensor(f"p_s{i}", [128, 512], F32)) for i in range(8)]

                GFB = sem("g_fb")
                GMM = sem("g_mm")
                GEV = sem("g_ev")
                GSQ = sem("g_sq")
                GST = sem("g_stat")

                W2B = [WOB2[0] + 32 * 16, WOB2[1] + 32 * 16]
                HLB = [32 * 32, 32 * 32]

                F2SB = [16 * (32 + 8 + 64), 16 * (32 + 64)]

                @block.sync
                def _(sync):
                    sync.dma_start(out=b2_sb[:], in_=b2r).then_inc(CLD, 16)
                    sync.wait_ge(ST0, F2SB[0])
                    sync.wait_ge(ST1, F2SB[1])
                    for dc in range(8):
                        if dc >= 1:
                            for qt in range(8):
                                idx = (dc - 1) * 8 + qt
                                par = idx % 2
                                sync.wait_ge(GSQ, idx + 1)
                                sync.dma_start(out=d_out2[qt * 128:(qt + 1) * 128, (dc - 1) * 512:dc * 512],
                                               in_=o2t[par][:]).then_inc([ST0, ST1][par], 16)
                        for fb in range(8):
                            key = dc * 8 + fb
                            par = key % 2
                            sl = [LC, LD2][par]
                            if key >= 2:
                                sync.wait_ge(GFB, key - 1)
                            sync.dma_start(
                                out=hTp[par][:].rearrange("p (a q) -> p a q", a=16),
                                in_=d_hT[fb * 2048:(fb + 1) * 2048, :].rearrange("(a p) q -> p a q", p=128),
                            ).then_inc(sl, 16)
                    for qt in range(8):
                        idx = 7 * 8 + qt
                        par = idx % 2
                        sync.wait_ge(GSQ, idx + 1)
                        sync.dma_start(out=d_out2[qt * 128:(qt + 1) * 128, 7 * 512:8 * 512],
                                       in_=o2t[par][:]).then_inc([ST0, ST1][par], 16)
                    sync.wait_ge(ST0, F2SB[0] + 16 * 32)
                    sync.wait_ge(ST1, F2SB[1] + 16 * 32)
                    if taps:
                        sync.dma_start(out=tap_aps["t_out2"], in_=d_out2).then_inc(SG, 16)
                cld_n[0] += 16

                @block.gpsimd
                def _(gp):
                    gp.wait_ge(ST0, F2SB[0])
                    gp.wait_ge(ST1, F2SB[1])
                    for dc in range(8):
                        for fb in range(8):
                            key = dc * 8 + fb
                            par = key % 2
                            sl = [LE, LF][par]
                            if key >= 2:
                                gp.wait_ge(GFB, key - 1)
                            gp.dma_start(
                                out=w2_sb[par][:].rearrange("p (s n) -> p s n", s=16),
                                in_=w2[fb * 2048:(fb + 1) * 2048, dc * 512:(dc + 1) * 512].rearrange("(s p) n -> p s n", p=128),
                            ).then_inc(sl, 16)

                @block.tensor
                def _(tn):
                    tn.wait_ge(CLD, cld_n[0])
                    tn.wait_ge(WLD, wld_n[0])
                    for dc in range(8):
                        for fb in range(8):
                            key = dc * 8 + fb
                            par = key % 2
                            tn.wait_ge([LE, LF][par], W2B[par] + 16 * (key // 2 + 1))
                            tn.wait_ge([LC, LD2][par], HLB[par] + 16 * (key // 2 + 1))
                            if fb == 0 and dc >= 1:
                                tn.wait_ge(GSQ, dc * 8)
                            for qt in range(8):
                                for j in range(16):
                                    mm = tn.matmul(p_s[qt][:],
                                                   hTp[par][:, j * 1024 + qt * 128:j * 1024 + (qt + 1) * 128],
                                                   w2_sb[par][:, j * 512:(j + 1) * 512],
                                                   start=(fb == 0 and j == 0), stop=False)
                            mm.then_inc(GFB, 1)
                        for qt in range(8):
                            mm = tn.matmul(p_s[qt][:], onesf_sb[:], b2_sb[:, dc * 512:(dc + 1) * 512],
                                           start=False, stop=True)
                            mm.then_inc(GMM, 1)

                @block.scalar
                def _(s):
                    for dc in range(8):
                        for qt in range(8):
                            idx = dc * 8 + qt
                            par = idx % 2
                            s.wait_ge(GMM, idx + 1)
                            if idx >= 2:
                                s.wait_ge([ST0, ST1][par], F2SB[par] + 16 * (idx // 2))
                            if idx >= 2:
                                s.wait_ge(GST, idx - 1)
                            s.activation(o2t[par][:], p_s[qt][:], AF.Copy,
                                         accum_out=tstat2[:, 2 * par:2 * par + 1]).then_inc(GEV, 1)
                            s.wait_ge(GEV, idx + 1)
                            s.activation(junk[:, par * 512:(par + 1) * 512], o2t[par][:], AF.Square,
                                         accum_out=tstat2[:, 2 * par + 1:2 * par + 2]).then_inc(GSQ, 1)

                @block.vector
                def _(v):
                    v.memset(sums[:], 0.0)
                    vfence(v, v.memset(ssqs[:], 0.0))
                    for dc in range(8):
                        for qt in range(8):
                            idx = dc * 8 + qt
                            par = idx % 2
                            v.wait_ge(GSQ, idx + 1)
                            vfence(v, v.tensor_add(sums[:, qt:qt + 1], sums[:, qt:qt + 1], tstat2[:, 2 * par:2 * par + 1]))
                            v.tensor_add(ssqs[:, qt:qt + 1], ssqs[:, qt:qt + 1],
                                         tstat2[:, 2 * par + 1:2 * par + 2]).then_inc(GST, 1)

                barrier()

            with ExitStack() as ctxF3:
                EF3 = ctxF3.enter_context
                x2r = [EF3(nc.sbuf_tensor(f"x2r{i}", [128, D], F32)) for i in range(2)]
                o2r = [EF3(nc.sbuf_tensor(f"o2r{i}", [128, D], F32)) for i in range(2)]
                xo = [EF3(nc.sbuf_tensor(f"xo{i}", [128, D], F32)) for i in range(2)]

                ZVAR = sem("z_var")
                ZRS = sem("z_rs")
                ZXO = sem("z_xo")

                ZB = [(8 + 32 + 4 * 2) * 16, (32 + 4 * 2) * 16]

                @block.sync
                def _(sync):
                    for qt in range(8):
                        par = qt % 2
                        sl = [LA, LB][par]
                        if qt >= 2:
                            sync.wait_ge(ZXO, qt - 1)
                        sync.dma_start(out=x2r[par][:], in_=d_x2[qt * 128:(qt + 1) * 128, :]).then_inc(sl, 16)
                        sync.dma_start(out=o2r[par][:], in_=d_out2[qt * 128:(qt + 1) * 128, :]).then_inc(sl, 16)
                        if qt >= 2:
                            sync.dma_start(out=out_rows[(qt - 2) * 128:(qt - 1) * 128, :],
                                           in_=xo[(qt - 2) % 2][:]).then_inc([OST, OST1][qt % 2], 16)

                @block.vector
                def _(v):
                    for qt in range(8):
                        par = qt % 2
                        sl = [LA, LB][par]
                        v.wait_ge(sl, ZB[par] + 32 * (qt // 2 + 1))
                        v.tensor_scalar(stats[:, 0:1], sums[:, qt:qt + 1], -1.0 / D, None, ALU.mult)
                        vfence(v, v.tensor_scalar(stats[:, 1:2], ssqs[:, qt:qt + 1], 1.0 / D, None, ALU.mult))
                        vfence(v, v.tensor_mul(stats[:, 2:3], stats[:, 0:1], stats[:, 0:1]))
                        v.tensor_sub(stats[:, 3:4], stats[:, 1:2], stats[:, 2:3]).then_inc(ZVAR, 1)
                        v.wait_ge(ZRS, qt + 1)
                        vfence(v, v.tensor_scalar(o2r[par][:], o2r[par][:], stats[:, 0:1], None, ALU.add))
                        vfence(v, v.tensor_scalar(o2r[par][:], o2r[par][:], stats[:, 4:5], None, ALU.mult))
                        if qt >= 2:
                            v.wait_ge([OST, OST1][qt % 2], 16 * (qt // 2))
                        v.tensor_add(xo[par][:], x2r[par][:], o2r[par][:]).then_inc(ZXO, 1)

                @block.scalar
                def _(s):
                    for qt in range(8):
                        s.wait_ge(ZVAR, qt + 1)
                        sfence(s, s.activation(stats[:, 20:21], stats[:, 3:4], AF.Ln, bias=eps_sb[:]))
                        s.activation(stats[:, 4:5], stats[:, 20:21], AF.Exp, scale=-0.5).then_inc(ZRS, 1)

                @block.sync
                def _(sync):
                    for qt in range(6, 8):
                        par = qt % 2
                        sync.wait_ge(ZXO, qt + 1)
                        sync.dma_start(out=out_rows[qt * 128:(qt + 1) * 128, :],
                                       in_=xo[par][:]).then_inc([OST, OST1][qt % 2], 16)
                    sync.wait_ge(OST, 16 * 4)
                    sync.wait_ge(OST1, 16 * 4)
                    if taps:
                        sync.wait_ge(SG, 16 * 9)

    return nc


def _host_constants():
    inv_freq = 1.0 / (10000.0 ** (np.arange(0, RD, 2, dtype=np.float64) / RD))
    t = np.arange(S, dtype=np.float64)
    freqs = np.outer(t, inv_freq)
    emb = np.concatenate([freqs, freqs], axis=-1)
    cosk = np.cos(emb).astype(np.float32)
    sink = np.sin(emb).astype(np.float32)
    ident = np.eye(128, dtype=ml_dtypes.bfloat16)
    ones_c = np.ones((128, 128), dtype=ml_dtypes.bfloat16)
    return cosk, sink, ident, ones_c


def make_in_maps(inputs):
    x = np.ascontiguousarray(np.asarray(inputs["L0_attn_hidden_states"], dtype=np.float32).reshape(S, D))
    cosk, sink, ident, ones_c = _host_constants()
    common = {
        "x_full": x,
        "w_qa": np.ascontiguousarray(inputs["L0_attn_w_qa"], dtype=np.float32),
        "g_qa": np.ascontiguousarray(inputs["L0_attn_g_qa"], dtype=np.float32).reshape(1, QR),
        "w_qb": np.ascontiguousarray(inputs["L0_attn_w_qb"], dtype=np.float32),
        "w_kva": np.ascontiguousarray(inputs["L0_attn_w_kva"], dtype=np.float32),
        "g_kva": np.ascontiguousarray(inputs["L0_attn_g_kva"], dtype=np.float32).reshape(1, KVR),
        "w_kvb": np.ascontiguousarray(inputs["L0_attn_w_kvb"], dtype=np.float32),
        "w_o": np.ascontiguousarray(inputs["L0_attn_w_o"], dtype=np.float32),
        "w1": np.ascontiguousarray(inputs["L0_ffn_inter_w"], dtype=np.float32),
        "b1r": np.ascontiguousarray(np.asarray(inputs["L0_ffn_inter_b"], dtype=np.float32).reshape(128, 128).T),
        "w2": np.ascontiguousarray(inputs["L0_ffn_output_w"], dtype=np.float32),
        "b2r": np.ascontiguousarray(inputs["L0_ffn_output_b"], dtype=np.float32).reshape(1, D),
        "cosk": cosk,
        "sink": sink,
        "identc": ident,
        "epsc": np.full((128, 1), LN_EPS, np.float32),
        "onesfc": np.ones((1, 128), np.float32),
        "onesf2c": np.ones((128, 128), np.float32),
        "onesc": ones_c,
    }
    in_maps = []
    for c in range(NCORES):
        m = dict(common)
        m["x_loc"] = np.ascontiguousarray(x[c * R:(c + 1) * R, :])
        m["cosqT"] = np.ascontiguousarray(cosk[c * R:(c + 1) * R, :].T)
        m["sinqT"] = np.ascontiguousarray(sink[c * R:(c + 1) * R, :].T)
        in_maps.append(m)
    return in_maps


def kernel(**inputs):
    in_maps = make_in_maps(inputs)
    nc = build_nc(taps=False)
    res = run_bass_kernel_spmd(nc, in_maps, core_ids=list(range(NCORES)))
    out = np.concatenate([res.results[c]["out_rows"] for c in range(NCORES)], axis=0)
    return out.reshape(1, S, D).astype(np.float32)


# version 05
# speedup vs baseline: 1.3857x; 1.1268x over previous
import numpy as np
import ml_dtypes
from contextlib import ExitStack
import concourse.bass as bass
import concourse.mybir as mybir
from concourse.bass_utils import run_bass_kernel_spmd

F32 = mybir.dt.float32
BF16 = mybir.dt.bfloat16
AF = mybir.ActivationFunctionType
ALU = mybir.AluOpType
AX = mybir.AxisListType

S, D, R = 8192, 4096, 1024
NH, QR, KVR, RD, ND, VD = 32, 1536, 512, 64, 128, 128
QK = ND + RD
FF = 16384
SC = float(QK) ** -0.5
NCORES = 8
LN_EPS = 1e-5
RMS_EPS = 1e-6


def build_nc(taps=False, nheads=NH, max_phase=99):
    nc = bass.Bass("TRN2", target_bir_lowering=False)

    x_full = nc.dram_tensor("x_full", [S, D], F32, kind="ExternalInput").ap()
    x_loc = nc.dram_tensor("x_loc", [R, D], F32, kind="ExternalInput").ap()
    w_qa = nc.dram_tensor("w_qa", [D, QR], F32, kind="ExternalInput").ap()
    g_qa = nc.dram_tensor("g_qa", [1, QR], F32, kind="ExternalInput").ap()
    w_qb = nc.dram_tensor("w_qb", [QR, NH * QK], F32, kind="ExternalInput").ap()
    w_kva = nc.dram_tensor("w_kva", [D, KVR + RD], F32, kind="ExternalInput").ap()
    g_kva = nc.dram_tensor("g_kva", [1, KVR], F32, kind="ExternalInput").ap()
    w_kvb = nc.dram_tensor("w_kvb", [KVR, NH * (ND + VD)], F32, kind="ExternalInput").ap()
    w_o = nc.dram_tensor("w_o", [D, D], F32, kind="ExternalInput").ap()
    w1 = nc.dram_tensor("w1", [D, FF], F32, kind="ExternalInput").ap()
    b1r = nc.dram_tensor("b1r", [128, 128], F32, kind="ExternalInput").ap()
    w2 = nc.dram_tensor("w2", [FF, D], F32, kind="ExternalInput").ap()
    b2r = nc.dram_tensor("b2r", [1, D], F32, kind="ExternalInput").ap()
    cosk = nc.dram_tensor("cosk", [S, RD], F32, kind="ExternalInput").ap()
    sink = nc.dram_tensor("sink", [S, RD], F32, kind="ExternalInput").ap()
    cosqT = nc.dram_tensor("cosqT", [RD, R], F32, kind="ExternalInput").ap()
    sinqT = nc.dram_tensor("sinqT", [RD, R], F32, kind="ExternalInput").ap()
    identc = nc.dram_tensor("identc", [128, 128], BF16, kind="ExternalInput").ap()
    epsc = nc.dram_tensor("epsc", [128, 1], F32, kind="ExternalInput").ap()
    onesfc = nc.dram_tensor("onesfc", [1, 128], F32, kind="ExternalInput").ap()
    onesf2c = nc.dram_tensor("onesf2c", [128, 128], F32, kind="ExternalInput").ap()
    onesc = nc.dram_tensor("onesc", [128, 128], BF16, kind="ExternalInput").ap()

    out_rows = nc.dram_tensor("out_rows", [R, D], F32, kind="ExternalOutput").ap()

    d_OT = nc.dram_tensor("d_OT", [NH, 128, R], BF16, kind="Internal").ap()
    d_attn = nc.dram_tensor("d_attn", [R, D], F32, kind="Internal").ap()
    d_x2 = nc.dram_tensor("d_x2", [R, D], F32, kind="Internal").ap()
    d_hT = nc.dram_tensor("d_hT", [FF, R], BF16, kind="Internal").ap()
    d_out2 = nc.dram_tensor("d_out2", [R, D], F32, kind="Internal").ap()

    tap_aps = {}
    if taps:
        for name, shape, dt in [
            ("t_qcT", [128, 12 * R], BF16),
            ("t_kcT", [128, 4 * S], BF16),
            ("t_krT", [65, S], BF16),
            ("t_kT0", [128, S], BF16),
            ("t_V0", [128, 64 * VD], BF16),
            ("t_OT0", [128, R], BF16),
            ("t_x2", [R, D], F32),
            ("t_hT0", [128, R], BF16),
            ("t_attn", [R, D], F32),
            ("t_out2", [R, D], F32),
        ]:
            tap_aps[name] = nc.dram_tensor(name, shape, dt, kind="ExternalOutput").ap()

    with ExitStack() as ctx0:
        E0 = ctx0.enter_context

        sems = {}

        def sem(name):
            if name not in sems:
                sems[name] = E0(nc.semaphore(name))
            return sems[name]

        PH = sem("ph")
        CLD = sem("c_ld")
        OST = sem("o_st")
        OST1 = sem("o_st1")
        LA, LB = sem("la"), sem("lb")
        LC, LD2 = sem("lc"), sem("ld2")
        LE, LF = sem("le"), sem("lf")
        ST0, ST1 = sem("st0"), sem("st1")
        SA = sem("sa")
        SG = sem("sg")

        WLD = sem("w_ld")
        cld_n = [0]
        wld_n = [0]
        ph_n = [0]

        ident_sb = E0(nc.sbuf_tensor("ident_sb", [128, 128], BF16))
        eps_sb = E0(nc.sbuf_tensor("eps_sb", [128, 1], F32))
        onesf_sb = E0(nc.sbuf_tensor("onesf_sb", [1, 128], F32))
        onesf2_sb = E0(nc.sbuf_tensor("onesf2_sb", [128, 128], F32))
        ones_sb = E0(nc.sbuf_tensor("ones_sb", [128, 128], BF16))
        junk = E0(nc.sbuf_tensor("junk", [128, 2048], BF16))
        stats = E0(nc.sbuf_tensor("stats", [128, 32], F32))

        block = E0(nc.Block())
        ENG = ["tensor", "scalar", "vector", "sync", "gpsimd"]

        VFS = sem("v_fence")
        SFS = sem("s_fence")
        fence_n = {"v": 0, "s": 0}

        def vfence(v, inst):
            fence_n["v"] += 1
            inst.then_inc(VFS, 1)
            v.wait_ge(VFS, fence_n["v"])

        def sfence(s, inst):
            fence_n["s"] += 1
            inst.then_inc(SFS, 1)
            s.wait_ge(SFS, fence_n["s"])

        def barrier():
            target = ph_n[0] + len(ENG)

            @block.scalar
            def _(s):
                s.activation(stats[:, 30:31], eps_sb[:], AF.Copy).then_inc(PH, 1)
                s.wait_ge(PH, target)

            @block.vector
            def _(v):
                v.memset(stats[:, 31:32], 0.0).then_inc(PH, 1)
                v.wait_ge(PH, target)

            for name in ["tensor", "sync", "gpsimd"]:
                def mk():
                    def _(eng):
                        eng.sem_inc(PH, 1)
                        eng.wait_ge(PH, target)
                    return _
                getattr(block, name)(mk())
            ph_n[0] = target

        @block.sync
        def _(sync):
            sync.dma_start(out=ident_sb[:], in_=identc).then_inc(CLD, 16)
            sync.dma_start(out=ones_sb[:], in_=onesc).then_inc(CLD, 16)
            sync.dma_start(out=eps_sb[:], in_=epsc).then_inc(CLD, 16)
            sync.dma_start(out=onesf_sb[:], in_=onesfc).then_inc(CLD, 16)
            sync.dma_start(out=onesf2_sb[:], in_=onesf2c).then_inc(CLD, 16)
        cld_n[0] += 80

        d_qcT = nc.dram_tensor("d_qcT", [128, 12 * R], BF16, kind="Internal").ap()
        QSP = sem("q_spill")
        with ExitStack() as ctxA:
            EA = ctxA.enter_context
            if True:
                with ExitStack() as ctxQ:
                    EQ = ctxQ.enter_context
                    qcT = EQ(nc.sbuf_tensor("qcT", [128, 12 * R], BF16))
                    xt = [EQ(nc.sbuf_tensor("xt0", [128, D], F32))] * 2
                    xcb = [EQ(nc.sbuf_tensor("xcb0", [128, D], BF16))] * 2
                    xcT = [EQ(nc.sbuf_tensor("xcT0", [128, D], BF16))] * 2
                    wqa_sb = EQ(nc.sbuf_tensor("wqa_sb", [128, 32 * QR], BF16))
                    gqa_b = EQ(nc.sbuf_tensor("gqa_b", [128, QR], F32))
                    qln = EQ(nc.sbuf_tensor("qln", [128, QR], F32))
                    qcb = [EQ(nc.sbuf_tensor("qcb0", [128, QR], BF16))] * 2
                    p_aq = EQ(nc.psum_tensor("p_aq", [128, QR], F32))
                    p_tr = [EQ(nc.psum_tensor(f"p_tr{i}", [128, 1024], F32)) for i in range(2)]

                    NT = 8
                    SQ = sem("q_sq")
                    VXC = sem("q_xc")
                    VVAR = sem("q_var")
                    MTR = sem("q_mtr")
                    EXT = sem("q_ext")
                    MAQ = sem("q_maq")
                    EQL = sem("q_eql")
                    SUQ = sem("q_suq")
                    VQC = sem("q_vqc")
                    MQT = sem("q_mqt")
                    EQT = sem("q_eqt")
                    VSS = sem("q_vss")
                    SUQ2 = sem("q_suq2")

                    @block.gpsimd
                    def _(gp):
                        for s4 in range(4):
                            gp.dma_start(
                                out=wqa_sb[:].rearrange("p (s n) -> p s n", s=32)[:, s4 * 8:(s4 + 1) * 8, :],
                                in_=w_qa.rearrange("(s p) n -> p s n", p=128)[:, s4 * 8:(s4 + 1) * 8, :],
                            ).then_inc(WLD, 16)
                    wld_n[0] += 64

                    @block.sync
                    def _(sync):
                        sync.dma_start(out=gqa_b[:], in_=g_qa.broadcast_to((128, QR))).then_inc(CLD, 16)
                        for t in range(NT):
                            if t >= 1:
                                sync.wait_ge(VXC, t)
                                sync.wait_ge(SQ, t)
                            sync.dma_start(out=xt[0][:], in_=x_loc[t * 128:(t + 1) * 128, :]).then_inc(LA, 16)
                        sync.wait_ge(EQT, 16)
                        sync.dma_start(out=d_qcT, in_=qcT[:]).then_inc(QSP, 16)
                        sync.wait_ge(QSP, 16)
                    cld_n[0] += 16

                    @block.scalar
                    def _(s):
                        s.wait_ge(CLD, cld_n[0])
                        s.wait_ge(WLD, wld_n[0])
                        for t in range(NT):
                            par = t % 2
                            par2 = (t % 2) * 22
                            s.wait_ge(LA, 16 * (t + 1))
                            sfence(s, s.activation(junk[:], xt[0][:, 0:2048], AF.Square,
                                         accum_out=stats[:, 6 + par2:7 + par2]))
                            s.activation(junk[:], xt[0][:, 2048:4096], AF.Square,
                                         accum_out=stats[:, 7 + par2:8 + par2]).then_inc(SQ, 1)
                            s.wait_ge(VVAR, t + 1)
                            sfence(s, s.activation(stats[:, 12:13], stats[:, 10 + par:11 + par], AF.Ln, bias=eps_sb[:]))
                            s.activation(stats[:, 14 + par:15 + par], stats[:, 12:13], AF.Exp, scale=-0.5)
                            if t >= 1:
                                s.wait_ge(MAQ, t)
                            for g in range(4):
                                s.wait_ge(MTR, t * 4 + g + 1)
                                s.copy(xcT[par][:, g * 1024:(g + 1) * 1024], p_tr[g % 2][:]).then_inc(EXT, 1)
                            s.wait_ge(MAQ, t + 1)
                            if t >= 1:
                                s.wait_ge(VQC, t)
                            s.activation(qln[:], p_aq[:], AF.Copy,
                                         scale=stats[:, 14 + par:15 + par]).then_inc(EQL, 1)
                            s.wait_ge(EQL, t + 1)
                            s.activation(junk[:, 0:QR // 2], qln[:, 0:QR // 2], AF.Square, accum_out=stats[:, 22:23])
                            s.activation(junk[:, QR // 2:QR], qln[:, QR // 2:QR], AF.Square,
                                         accum_out=stats[:, 23:24]).then_inc(SUQ, 1)
                            s.wait_ge(VSS, t + 1)
                            sfence(s, s.activation(stats[:, 24:25], stats[:, 27:28], AF.Ln))
                            s.activation(stats[:, 16:17], stats[:, 24:25], AF.Exp, scale=0.5).then_inc(SUQ2, 1)
                            for g in range(2):
                                s.wait_ge(MQT, t * 2 + g + 1)
                                nj = 8 if g == 0 else 4
                                for j8 in range(nj):
                                    j = g * 8 + j8
                                    last = s.copy(qcT[:, j * R + t * 128:j * R + (t + 1) * 128],
                                                  p_tr[g % 2][:, j8 * 128:(j8 + 1) * 128])
                                last.then_inc(EQT, 1)

                    @block.vector
                    def _(v):
                        for t in range(NT):
                            par = t % 2
                            v.wait_ge(LA, 16 * (t + 1))
                            vfence(v, v.reduce_sum(stats[:, 0:1], xt[0][:], axis=AX.X, negate=True))
                            vfence(v, v.tensor_scalar(stats[:, 2 + par:3 + par], stats[:, 0:1], 1.0 / D, None, ALU.mult))
                            if t >= 1:
                                v.wait_ge(MTR, (t - 1) * 4 + 4)
                            v.tensor_scalar(xcb[0][:], xt[0][:], stats[:, 2 + par:3 + par],
                                            None, ALU.add).then_inc(VXC, 1)
                            v.tensor_mul(stats[:, 4:5], stats[:, 2 + par:3 + par], stats[:, 2 + par:3 + par])
                            par2 = (t % 2) * 22
                            v.wait_ge(SQ, t + 1)
                            vfence(v, v.tensor_add(stats[:, 5:6], stats[:, 6 + par2:7 + par2], stats[:, 7 + par2:8 + par2]))
                            vfence(v, v.tensor_scalar(stats[:, 8:9], stats[:, 5:6], 1.0 / D, None, ALU.mult))
                            v.tensor_sub(stats[:, 10 + par:11 + par], stats[:, 8:9], stats[:, 4:5]).then_inc(VVAR, 1)
                            v.wait_ge(SUQ, t + 1)
                            vfence(v, v.tensor_add(stats[:, 26:27], stats[:, 22:23], stats[:, 23:24]))
                            v.tensor_scalar(stats[:, 27:28], stats[:, 26:27], 1.0 / QR, None, ALU.mult).then_inc(VSS, 1)
                            v.wait_ge(SUQ2, t + 1)
                            vfence(v, v.tensor_scalar(stats[:, 20:21], stats[:, 16:17], RMS_EPS, None, ALU.add))
                            vfence(v, v.reciprocal(stats[:, 18:19], stats[:, 20:21]))
                            if t >= 1:
                                v.wait_ge(MQT, (t - 1) * 2 + 2)
                            v.scalar_tensor_tensor(qcb[0][:], qln[:], stats[:, 18:19], gqa_b[:],
                                                   ALU.mult, ALU.mult).then_inc(VQC, 1)

                    @block.tensor
                    def _(tn):
                        tn.wait_ge(CLD, cld_n[0])
                        tn.wait_ge(WLD, wld_n[0])
                        for t in range(NT):
                            par = 0
                            tn.wait_ge(VXC, t + 1)
                            for g in range(4):
                                if g >= 2:
                                    tn.wait_ge(EXT, t * 4 + g - 1)
                                elif t >= 1:
                                    if g == 0:
                                        tn.wait_ge(EQT, (t - 1) * 2 + 1)
                                    else:
                                        tn.wait_ge(EQT, (t - 1) * 2 + 2)
                                for j8 in range(8):
                                    j = g * 8 + j8
                                    mm = tn.matmul(p_tr[g % 2][:, j8 * 128:(j8 + 1) * 128],
                                                   xcb[par][:, j * 128:(j + 1) * 128], ident_sb[:],
                                                   start=True, stop=True)
                                mm.then_inc(MTR, 1)
                            tn.wait_ge(EXT, t * 4 + 4)
                            if t >= 1:
                                tn.wait_ge(EQL, t)
                            for j in range(32):
                                for nch in range(3):
                                    mm = tn.matmul(p_aq[:, nch * 512:(nch + 1) * 512],
                                                   xcT[par][:, j * 128:(j + 1) * 128],
                                                   wqa_sb[:, j * QR + nch * 512:j * QR + nch * 512 + 512],
                                                   start=(j == 0), stop=(j == 31))
                            mm.then_inc(MAQ, 1)
                            tn.wait_ge(VQC, t + 1)
                            for g in range(2):
                                tn.wait_ge(EXT, t * 4 + 3 + g)
                                nj = 8 if g == 0 else 4
                                for j8 in range(nj):
                                    j = g * 8 + j8
                                    mm = tn.matmul(p_tr[g % 2][:, j8 * 128:(j8 + 1) * 128],
                                                   qcb[par][:, j * 128:(j + 1) * 128], ident_sb[:],
                                                   start=True, stop=True)
                                mm.then_inc(MQT, 1)

                    barrier()

                kcT = EA(nc.sbuf_tensor("kcT", [128, 4 * S], BF16))
                krT = EA(nc.sbuf_tensor("krT", [65, S], BF16))
                with ExitStack() as ctxKV:
                    EKV = ctxKV.enter_context
                    xt = [EKV(nc.sbuf_tensor(f"kxt{i}", [128, D], F32)) for i in range(2)]
                    xcb = [EKV(nc.sbuf_tensor(f"kxcb{i}", [128, D], BF16)) for i in range(2)]
                    xcT = [EKV(nc.sbuf_tensor(f"kxcT{i}", [128, D], BF16)) for i in range(2)]
                    wkva_sb = EKV(nc.sbuf_tensor("wkva_sb", [128, 32 * (KVR + RD)], BF16))
                    gkva_b = EKV(nc.sbuf_tensor("gkva_b", [128, KVR], F32))
                    kvln = EKV(nc.sbuf_tensor("kvln", [128, KVR + RD], F32))
                    kcb = [EKV(nc.sbuf_tensor(f"kcb{i}", [128, KVR], BF16)) for i in range(2)]
                    ckb = [EKV(nc.sbuf_tensor(f"ckb{i}", [128, RD], F32)) for i in range(2)]
                    skb = [EKV(nc.sbuf_tensor(f"skb{i}", [128, RD], F32)) for i in range(2)]
                    krf = EKV(nc.sbuf_tensor("krf", [128, RD], BF16))
                    krt1 = EKV(nc.sbuf_tensor("krt1", [128, RD], F32))
                    krt2 = EKV(nc.sbuf_tensor("krt2", [128, RD], F32))
                    p_akv = EKV(nc.psum_tensor("p_akv", [128, 1024], F32))
                    p_t2 = [EKV(nc.psum_tensor(f"p_t2{i}", [128, 1024], F32)) for i in range(2)]

                    NT2 = 64
                    SQ2 = sem("k_sq")
                    VXC2 = sem("k_xc")
                    VVAR2 = sem("k_var")
                    MTR2 = sem("k_mtr")
                    EXT2 = sem("k_ext")
                    MKV = sem("k_mkv")
                    EKL = sem("k_ekl")
                    SUK = sem("k_suk")
                    VKC = sem("k_vkc")
                    VKR = sem("k_vkr")
                    MKT = sem("k_mkt")
                    EKT = sem("k_ekt")

                    @block.gpsimd
                    def _(gp):
                        for s2 in range(2):
                            gp.dma_start(
                                out=wkva_sb[:].rearrange("p (s n) -> p s n", s=32)[:, s2 * 16:(s2 + 1) * 16, :],
                                in_=w_kva.rearrange("(s p) n -> p s n", p=128)[:, s2 * 16:(s2 + 1) * 16, :],
                            ).then_inc(WLD, 16)
                    wld_n[0] += 32

                    @block.vector
                    def _(v):
                        v.memset(krT[64:65, :], -1.0).then_inc(VKR, 1)

                    @block.sync
                    def _(sync):
                        sync.dma_start(out=gkva_b[:], in_=g_kva.broadcast_to((128, KVR))).then_inc(CLD, 16)
                        for t in range(NT2):
                            par = t % 2
                            sl = [LA, LB][par]
                            cl = [LC, LD2][par]
                            if t >= 2:
                                sync.wait_ge(VXC2, t - 1)
                                sync.wait_ge(SQ2, t - 1)
                            sync.dma_start(out=xt[par][:], in_=x_full[t * 128:(t + 1) * 128, :]).then_inc(sl, 16)
                            if t >= 2:
                                sync.wait_ge(VKR, t)
                            sync.dma_start(out=ckb[par][:], in_=cosk[t * 128:(t + 1) * 128, :]).then_inc(cl, 16)
                            sync.dma_start(out=skb[par][:], in_=sink[t * 128:(t + 1) * 128, :]).then_inc(cl, 16)
                    cld_n[0] += 16

                    XB1 = 16 * 8
                    XB2 = 0

                    @block.scalar
                    def _(s):
                        s.wait_ge(CLD, cld_n[0])
                        s.wait_ge(WLD, wld_n[0])
                        for t in range(NT2):
                            par = t % 2
                            sl = [LA, LB][par]
                            base = [XB1, XB2][par]
                            par2 = par * 22
                            s.wait_ge(sl, base + 16 * (t // 2 + 1))
                            sfence(s, s.activation(junk[:], xt[par][:, 0:2048], AF.Square,
                                         accum_out=stats[:, 6 + par2:7 + par2]))
                            s.activation(junk[:], xt[par][:, 2048:4096], AF.Square,
                                         accum_out=stats[:, 7 + par2:8 + par2]).then_inc(SQ2, 1)
                            s.wait_ge(VVAR2, t + 1)
                            sfence(s, s.activation(stats[:, 12:13], stats[:, 10 + par:11 + par], AF.Ln, bias=eps_sb[:]))
                            s.activation(stats[:, 14 + par:15 + par], stats[:, 12:13], AF.Exp, scale=-0.5)
                            if t >= 2:
                                s.wait_ge(MKV, t - 1)
                            for g in range(4):
                                s.wait_ge(MTR2, t * 4 + g + 1)
                                s.copy(xcT[par][:, g * 1024:(g + 1) * 1024], p_t2[g % 2][:]).then_inc(EXT2, 1)
                            s.wait_ge(MKV, t + 1)
                            if t >= 1:
                                s.wait_ge(VKC, t)
                                s.wait_ge(VKR, t + 1)
                            s.activation(kvln[:], p_akv[:, 0:KVR + RD], AF.Copy,
                                         scale=stats[:, 14 + par:15 + par]).then_inc(EKL, 1)
                            s.wait_ge(EKL, t + 1)
                            sfence(s, s.activation(junk[:, 0:KVR], kvln[:, 0:KVR], AF.Square, accum_out=stats[:, 22:23]))
                            sfence(s, s.activation(stats[:, 24:25], stats[:, 22:23], AF.Ln, scale=1.0 / KVR))
                            s.activation(stats[:, 16:17], stats[:, 24:25], AF.Exp, scale=0.5).then_inc(SUK, 1)
                            s.wait_ge(MKT, t + 1)
                            for j4 in range(4):
                                s.copy(kcT[:, j4 * S + t * 128:j4 * S + (t + 1) * 128],
                                       p_t2[0][:, j4 * 128:(j4 + 1) * 128])
                            s.copy(krT[0:64, t * 128:(t + 1) * 128],
                                   p_t2[0][0:64, 512:640]).then_inc(EKT, 1)

                    @block.vector
                    def _(v):
                        for t in range(NT2):
                            par = t % 2
                            sl = [LA, LB][par]
                            cl = [LC, LD2][par]
                            base = [XB1, XB2][par]
                            v.wait_ge(sl, base + 16 * (t // 2 + 1))
                            vfence(v, v.reduce_sum(stats[:, 0:1], xt[par][:], axis=AX.X, negate=True))
                            vfence(v, v.tensor_scalar(stats[:, 2 + par:3 + par], stats[:, 0:1], 1.0 / D, None, ALU.mult))
                            if t >= 2:
                                v.wait_ge(MTR2, (t - 2) * 4 + 4)
                            v.tensor_scalar(xcb[par][:], xt[par][:], stats[:, 2 + par:3 + par],
                                            None, ALU.add).then_inc(VXC2, 1)
                            v.tensor_mul(stats[:, 4:5], stats[:, 2 + par:3 + par], stats[:, 2 + par:3 + par])
                            par2 = par * 22
                            v.wait_ge(SQ2, t + 1)
                            vfence(v, v.tensor_add(stats[:, 5:6], stats[:, 6 + par2:7 + par2], stats[:, 7 + par2:8 + par2]))
                            vfence(v, v.tensor_scalar(stats[:, 8:9], stats[:, 5:6], 1.0 / D, None, ALU.mult))
                            v.tensor_sub(stats[:, 10 + par:11 + par], stats[:, 8:9], stats[:, 4:5]).then_inc(VVAR2, 1)
                            v.wait_ge(SUK, t + 1)
                            vfence(v, v.tensor_scalar(stats[:, 20:21], stats[:, 16:17], RMS_EPS, None, ALU.add))
                            vfence(v, v.reciprocal(stats[:, 18:19], stats[:, 20:21]))
                            if t >= 2:
                                v.wait_ge(MKT, t - 1)
                            v.scalar_tensor_tensor(kcb[par][:], kvln[:, 0:KVR], stats[:, 18:19], gkva_b[:],
                                                   ALU.mult, ALU.mult).then_inc(VKC, 1)
                            v.wait_ge(cl, 32 * (t // 2 + 1))
                            if t >= 1:
                                v.wait_ge(MKT, t)
                            v.tensor_scalar(krt1[:, 0:32], kvln[:, KVR + 32:KVR + 64], -1.0, None, ALU.mult)
                            vfence(v, v.tensor_copy(krt1[:, 32:64], kvln[:, KVR:KVR + 32]))
                            v.tensor_mul(krt2[:], kvln[:, KVR:KVR + RD], ckb[par][:])
                            vfence(v, v.tensor_mul(krt1[:], krt1[:], skb[par][:]))
                            v.tensor_add(krf[:], krt2[:], krt1[:]).then_inc(VKR, 1)

                    @block.tensor
                    def _(tn):
                        tn.wait_ge(CLD, cld_n[0])
                        tn.wait_ge(WLD, wld_n[0])
                        for t in range(NT2):
                            par = t % 2
                            tn.wait_ge(VXC2, t + 1)
                            for g in range(4):
                                if g >= 2:
                                    tn.wait_ge(EXT2, t * 4 + g - 1)
                                elif t >= 1:
                                    if g == 0:
                                        tn.wait_ge(EKT, t)
                                    else:
                                        tn.wait_ge(EXT2, (t - 1) * 4 + 4)
                                for j8 in range(8):
                                    j = g * 8 + j8
                                    mm = tn.matmul(p_t2[g % 2][:, j8 * 128:(j8 + 1) * 128],
                                                   xcb[par][:, j * 128:(j + 1) * 128], ident_sb[:],
                                                   start=True, stop=True)
                                mm.then_inc(MTR2, 1)
                            tn.wait_ge(EXT2, t * 4 + 4)
                            if t >= 1:
                                tn.wait_ge(EKL, t)
                            for j in range(32):
                                for nch in range(2):
                                    lo = nch * 512
                                    wdt = 512 if nch == 0 else 64
                                    mm = tn.matmul(p_akv[:, lo:lo + wdt],
                                                   xcT[par][:, j * 128:(j + 1) * 128],
                                                   wkva_sb[:, j * (KVR + RD) + lo:j * (KVR + RD) + lo + wdt],
                                                   start=(j == 0), stop=(j == 31))
                            mm.then_inc(MKV, 1)
                            tn.wait_ge(VKC, t + 1)
                            tn.wait_ge(VKR, t + 2)
                            tn.wait_ge(EXT2, t * 4 + 3)
                            tn.wait_ge(EKT, t)
                            for j4 in range(4):
                                tn.matmul(p_t2[0][:, j4 * 128:(j4 + 1) * 128],
                                          kcb[par][:, j4 * 128:(j4 + 1) * 128], ident_sb[:],
                                          start=True, stop=True)
                            mm = tn.matmul(p_t2[0][0:64, 512:640], krf[:], ident_sb[:],
                                           start=True, stop=True)
                            mm.then_inc(MKT, 1)

                    barrier()

            if taps:
                @block.sync
                def _(sync):
                    sync.wait_ge(QSP, 16)
                    sync.dma_start(out=tap_aps["t_qcT"], in_=d_qcT).then_inc(SG, 16)
                    sync.dma_start(out=tap_aps["t_kcT"], in_=kcT[:]).then_inc(SG, 16)
                    sync.dma_start(out=tap_aps["t_krT"], in_=krT[:]).then_inc(SG, 16)
                    sync.wait_ge(SG, 48)
                barrier()
            if max_phase <= 1:
                return nc

            with ExitStack() as ctxT:
                ET = ctxT.enter_context
                qcT2 = ET(nc.sbuf_tensor("qcT2", [128, 12 * R], BF16))
                wqb_sb = [ET(nc.sbuf_tensor(f"wqb{i}", [128, 12 * QK], BF16)) for i in range(2)]
                wkvb_sb = [ET(nc.sbuf_tensor(f"wkvb{i}", [128, 4 * (ND + VD)], BF16)) for i in range(2)]
                cqT = ET(nc.sbuf_tensor("cqT", [RD, R], F32))
                sqT = ET(nc.sbuf_tensor("sqT", [RD, R], F32))
                qT_nope = ET(nc.sbuf_tensor("qT_nope", [128, R], BF16))
                qT_s2 = ET(nc.sbuf_tensor("qT_s2", [65, R], BF16))
                qrr = ET(nc.sbuf_tensor("qrr", [RD, R], F32))
                rt1 = ET(nc.sbuf_tensor("rt1", [RD, R], F32))
                kT_nope = ET(nc.sbuf_tensor("kT_nope", [128, S], BF16))
                Vb = ET(nc.sbuf_tensor("Vb", [128, 64 * VD], BF16))
                sqb = [ET(nc.sbuf_tensor(f"sqb{i}", [128, 512], BF16)) for i in range(2)]
                sqb2 = [ET(nc.sbuf_tensor(f"sqb2{i}", [65, 512], BF16)) for i in range(2)]
                sqq0 = ET(nc.sbuf_tensor("sqq0", [128, 512], BF16))
                sqq1 = ET(nc.sbuf_tensor("sqq1", [64, 512], BF16))
                PT = ET(nc.sbuf_tensor("PT", [128, 4 * 1024], BF16))
                maxb = ET(nc.sbuf_tensor("maxb", [128, 18], F32))
                lkv = ET(nc.sbuf_tensor("lkv", [1, 4], F32))
                nqln = ET(nc.sbuf_tensor("nqln", [1, 1024], F32))
                lacc = ET(nc.sbuf_tensor("lacc", [128, 1024], F32))
                ptp = ET(nc.sbuf_tensor("ptp", [128, 1024], BF16))
                rlb = ET(nc.sbuf_tensor("rlb", [128, 1024], F32))
                OT_h = ET(nc.sbuf_tensor("OT_h", [128, R], BF16))
                p_sp = [ET(nc.psum_tensor(f"p_sp{i}", [128, 1024], F32)) for i in range(2)]
                p_pvT = ET(nc.psum_tensor("p_pvT", [128, 1024], F32))
                p_gen = [ET(nc.psum_tensor(f"p_gen{i}", [128, 512], F32)) for i in range(2)]

                AKGM = sem("a_kgm")
                AKGE = sem("a_kge")
                ASQK = sem("a_sqk")
                ANKM = sem("a_nkm")
                ANKR = sem("a_nkr")
                AMK = sem("a_mk")
                ALK = sem("a_lk")
                ALKH = sem("a_lkh")
                AVGM = sem("a_vgm")
                AVGE = sem("a_vge")
                AQGM = sem("a_qgm")
                AQGE = sem("a_qge")
                AQRM = sem("a_qrm")
                AQRE = sem("a_qre")
                AROP = sem("a_rop")
                ASQQ = sem("a_sqq")
                ANQM = sem("a_nqm")
                AMRW = sem("a_mrw")
                AEVQ = sem("a_evq")
                ASP = sem("a_sp")
                AEXP = sem("a_exp")
                APV = sem("a_pv")
                ALC = sem("a_lc")
                ABC = sem("a_bc")
                AOTE = sem("a_ote")

                @block.sync
                def _(sync):
                    sync.dma_start(out=cqT[:], in_=cosqT).then_inc(CLD, 16)
                    sync.dma_start(out=sqT[:], in_=sinqT).then_inc(CLD, 16)
                    sync.wait_ge(QSP, 16)
                    sync.dma_start(out=qcT2[:], in_=d_qcT).then_inc(CLD, 16)
                cld_n[0] += 48

                @block.gpsimd
                def _(gp):
                    for h in range(nheads):
                        par = h % 2
                        sl = [LE, LF][par]
                        if h >= 2:
                            gp.wait_ge(AQRM, (h - 2) * 2 + 2)
                            gp.wait_ge(AVGM, (h - 2) * 64 + 64)
                        gp.dma_start(
                            out=wqb_sb[par][:].rearrange("p (s n) -> p s n", s=12),
                            in_=w_qb[:, h * QK:(h + 1) * QK].rearrange("(s p) n -> p s n", p=128),
                        ).then_inc(sl, 16)
                        gp.dma_start(
                            out=wkvb_sb[par][:].rearrange("p (s n) -> p s n", s=4),
                            in_=w_kvb[:, h * (ND + VD):(h + 1) * (ND + VD)].rearrange("(s p) n -> p s n", p=128),
                        ).then_inc(sl, 16)

                @block.tensor
                def _(tn):
                    tn.wait_ge(CLD, cld_n[0])
                    tn.wait_ge(WLD, wld_n[0])
                    for h in range(nheads):
                        par = h % 2
                        sl = [LE, LF][par]
                        tn.wait_ge(sl, 32 * (h // 2 + 1))
                        hb16 = h * 16
                        hb64 = h * 64
                        hb2 = h * 2
                        for c in range(16):
                            gb = p_gen[c % 2]
                            if c >= 2:
                                tn.wait_ge(AKGE, hb16 + c - 1)
                            elif h >= 1:
                                tn.wait_ge(AVGE, (h - 1) * 64 + 63 + c)
                            for j in range(4):
                                mm = tn.matmul(gb[:],
                                               wkvb_sb[par][:, j * (ND + VD):j * (ND + VD) + ND],
                                               kcT[:, j * S + c * 512:j * S + c * 512 + 512],
                                               start=(j == 0), stop=(j == 3))
                            mm.then_inc(AKGM, 1)
                        tn.wait_ge(AKGE, hb16 + 16)
                        for j in range(12):
                            for qc in range(2):
                                mm = tn.matmul(p_gen[qc][:],
                                               wqb_sb[par][:, j * QK:j * QK + ND],
                                               qcT2[:, j * R + qc * 512:j * R + qc * 512 + 512],
                                               start=(j == 0), stop=(j == 11))
                                if j == 11:
                                    mm.then_inc(AQGM, 1)
                        tn.wait_ge(AQGE, hb2 + 2)
                        for j in range(12):
                            for qc in range(2):
                                mm = tn.matmul(p_gen[qc][0:64, :],
                                               wqb_sb[par][:, j * QK + ND:j * QK + QK],
                                               qcT2[:, j * R + qc * 512:j * R + qc * 512 + 512],
                                               start=(j == 0), stop=(j == 11))
                                if j == 11:
                                    mm.then_inc(AQRM, 1)
                        for qc in range(2):
                            gb = p_gen[qc % 2]
                            tn.wait_ge(ASQQ, h * 4 + qc * 2 + 2)
                            tn.wait_ge(AQRE, hb2 + qc + 1)
                            mm = tn.matmul(gb[:], ones_sb[:], sqq0[:], start=True, stop=False)
                            mm = tn.matmul(gb[:], ones_sb[0:64, :], sqq1[0:64, :], start=False, stop=True)
                            mm.then_inc(ANQM, 1)
                        for c in range(16):
                            gb = p_gen[c % 2]
                            tn.wait_ge(ASQK, hb16 + c + 1)
                            if c >= 2:
                                tn.wait_ge(ANKR, hb16 + c - 1)
                            else:
                                tn.wait_ge(AEVQ, hb2 + c + 1)
                            mm = tn.matmul(gb[:], ones_sb[:], sqb[c % 2][:], start=True, stop=False)
                            mm = tn.matmul(gb[:], ones_sb[0:65, :], sqb2[c % 2][:],
                                           start=False, stop=True)
                            mm.then_inc(ANKM, 1)
                        for kt in range(64):
                            gb = p_gen[kt % 2]
                            if kt >= 2:
                                tn.wait_ge(AVGE, hb64 + kt - 1)
                            else:
                                tn.wait_ge(ANKR, hb16 + 15 + kt)
                            for j in range(4):
                                mm = tn.matmul(gb[:, 0:128],
                                               kcT[:, j * S + kt * 128:j * S + (kt + 1) * 128],
                                               wkvb_sb[par][:, j * (ND + VD) + ND:j * (ND + VD) + ND + VD],
                                               start=(j == 0), stop=(j == 3))
                            mm.then_inc(AVGM, 1)
                        def emit_pv(kt):
                            tn.wait_ge(AEXP, hb64 + kt + 1)
                            vs = Vb[:, kt * VD:(kt + 1) * VD]
                            slot = (kt % 4) * 1024
                            tn.matmul(p_pvT[:, 0:512], vs, PT[:, slot:slot + 512],
                                      start=(kt == 0), stop=(kt == 63))
                            mm = tn.matmul(p_pvT[:, 512:1024], vs, PT[:, slot + 512:slot + 1024],
                                           start=(kt == 0), stop=(kt == 63))
                            mm.then_inc(APV, 1)

                        for kt in range(64):
                            spb = p_sp[kt % 2]
                            if kt >= 2:
                                tn.wait_ge(AEXP, hb64 + kt - 1)
                            elif h >= 1:
                                tn.wait_ge(AOTE, h)
                            if kt == 0:
                                tn.wait_ge(AMRW, hb2 + 2)
                                tn.wait_ge(AROP, hb2 + 2)
                                tn.wait_ge(AKGE, hb16 + 16)
                                tn.wait_ge(AVGE, hb64 + 64)
                            ks = kT_nope[:, kt * 128:(kt + 1) * 128]
                            rs = krT[:, kt * 128:(kt + 1) * 128]
                            tn.matmul(spb[:, 0:512], ks, qT_nope[:, 0:512], start=True, stop=False)
                            tn.matmul(spb[:, 512:1024], ks, qT_nope[:, 512:1024], start=True, stop=False)
                            tn.matmul(spb[:, 0:512], rs, qT_s2[:, 0:512], start=False, stop=True)
                            mm = tn.matmul(spb[:, 512:1024], rs, qT_s2[:, 512:1024], start=False, stop=True)
                            mm.then_inc(ASP, 1)
                            if kt >= 1:
                                emit_pv(kt - 1)
                        emit_pv(63)
                        tn.wait_ge(ALC, h * 32 + 32)
                        tn.wait_ge(AEXP, hb64 + 64)
                        tn.matmul(p_sp[0][:, 0:512], onesf2_sb[:], lacc[:, 0:512], start=True, stop=True)
                        mm = tn.matmul(p_sp[0][:, 512:1024], onesf2_sb[:], lacc[:, 512:1024], start=True, stop=True)
                        mm.then_inc(ABC, 1)

                @block.scalar
                def _(s):
                    for h in range(nheads):
                        hb16, hb64, hb2 = h * 16, h * 64, h * 2
                        for c in range(16):
                            s.wait_ge(AKGM, hb16 + c + 1)
                            s.copy(kT_nope[:, c * 512:(c + 1) * 512], p_gen[c % 2][:]).then_inc(AKGE, 1)
                        for qc in range(2):
                            s.wait_ge(AQGM, hb2 + qc + 1)
                            s.activation(qT_nope[:, qc * 512:qc * 512 + 512], p_gen[qc % 2][:],
                                         AF.Copy, scale=SC).then_inc(AQGE, 1)
                        for qc in range(2):
                            s.wait_ge(AQRM, hb2 + qc + 1)
                            s.activation(qrr[:, qc * 512:qc * 512 + 512], p_gen[qc % 2][0:64, :],
                                         AF.Copy, scale=SC).then_inc(AQRE, 1)
                        for qc in range(2):
                            s.wait_ge(ANQM, hb2 + qc + 1)
                            s.activation(nqln[0:1, qc * 512:qc * 512 + 512],
                                         p_gen[qc % 2][0:1, :], AF.Ln).then_inc(AEVQ, 1)
                        s.wait_ge(AMK, h + 1)
                        s.activation(lkv[0:1, 0:1], maxb[0:1, 16:17], AF.Ln).then_inc(ALK, 1)
                        for qc in range(2):
                            s.wait_ge(ALKH, h + 1)
                            s.wait_ge(AEVQ, hb2 + qc + 1)
                            s.activation(qT_s2[64:65, qc * 512:qc * 512 + 512],
                                         nqln[0:1, qc * 512:qc * 512 + 512], AF.Exp,
                                         scale=0.5, bias=lkv[0:1, 1:2]).then_inc(AMRW, 1)
                        hb32 = h * 32
                        for kt in range(64):
                            s.wait_ge(ASP, hb64 + kt + 1)
                            pvw = hb64 + kt - 3
                            if pvw >= 1:
                                s.wait_ge(APV, pvw)
                            lcw = hb32 + (kt - 4) // 2 + 1
                            if kt >= 4 or h >= 1:
                                s.wait_ge(ALC, lcw)
                            s.activation(PT[:, (kt % 4) * 1024:(kt % 4) * 1024 + 1024],
                                         p_sp[kt % 2][:], AF.Exp).then_inc(AEXP, 1)

                @block.vector
                def _(v):
                    for h in range(nheads):
                        hb16, hb64, hb2 = h * 16, h * 64, h * 2
                        for qc in range(2):
                            lo = qc * 512
                            v.wait_ge(AQRE, hb2 + qc + 1)
                            v.tensor_scalar(rt1[0:32, lo:lo + 512], qrr[32:64, lo:lo + 512], -1.0, None, ALU.mult)
                            vfence(v, v.tensor_copy(rt1[32:64, lo:lo + 512], qrr[0:32, lo:lo + 512]))
                            vfence(v, v.tensor_mul(qrr[:, lo:lo + 512], qrr[:, lo:lo + 512], cqT[:, lo:lo + 512]))
                            vfence(v, v.tensor_mul(rt1[:, lo:lo + 512], rt1[:, lo:lo + 512], sqT[:, lo:lo + 512]))
                            v.tensor_add(qT_s2[0:64, lo:lo + 512], qrr[:, lo:lo + 512],
                                         rt1[:, lo:lo + 512]).then_inc(AROP, 1)
                            v.wait_ge(AQGE, hb2 + qc + 1)
                            if h >= 1 or qc >= 1:
                                v.wait_ge(ANQM, hb2 + qc)
                            v.tensor_mul(sqq0[:], qT_nope[:, lo:lo + 512],
                                         qT_nope[:, lo:lo + 512]).then_inc(ASQQ, 1)
                            v.wait_ge(AROP, hb2 + qc + 1)
                            v.tensor_mul(sqq1[:], qT_s2[0:64, lo:lo + 512],
                                         qT_s2[0:64, lo:lo + 512]).then_inc(ASQQ, 1)
                        for c in range(16):
                            v.wait_ge(AKGE, hb16 + c + 1)
                            if c >= 2:
                                v.wait_ge(ANKM, hb16 + c - 1)
                            elif h >= 1:
                                v.wait_ge(ANKM, (h - 1) * 16 + 15 + c)
                            v.tensor_mul(sqb[c % 2][:], kT_nope[:, c * 512:(c + 1) * 512],
                                         kT_nope[:, c * 512:(c + 1) * 512])
                            v.tensor_mul(sqb2[c % 2][:], krT[:, c * 512:(c + 1) * 512],
                                         krT[:, c * 512:(c + 1) * 512]).then_inc(ASQK, 1)
                            if c >= 2:
                                v.wait_ge(ANKM, hb16 + c - 1)
                                v.reduce_max(maxb[:, c - 2:c - 1], p_gen[c % 2][:], axis=AX.X).then_inc(ANKR, 1)
                        for c in range(14, 16):
                            v.wait_ge(ANKM, hb16 + c + 1)
                            v.reduce_max(maxb[:, c:c + 1], p_gen[c % 2][:], axis=AX.X).then_inc(ANKR, 1)
                        v.wait_ge(ANKR, hb16 + 16)
                        v.reduce_max(maxb[:, 16:17], maxb[:, 0:16], axis=AX.X).then_inc(AMK, 1)
                        v.wait_ge(ALK, h + 1)
                        v.tensor_scalar(lkv[0:1, 1:2], lkv[0:1, 0:1], 0.5, None, ALU.mult).then_inc(ALKH, 1)
                        for kt in range(64):
                            v.wait_ge(AVGM, hb64 + kt + 1)
                            v.tensor_copy(Vb[:, kt * VD:(kt + 1) * VD],
                                          p_gen[kt % 2][:, 0:128]).then_inc(AVGE, 1)
                        hb32 = h * 32
                        v.wait_ge(ABC, h)
                        vfence(v, v.memset(lacc[:], 0.0))
                        for p2 in range(32):
                            v.wait_ge(AEXP, hb64 + 2 * p2 + 2)
                            if p2 >= 1:
                                v.wait_ge(ALC, hb32 + p2)
                            s0 = (2 * p2 % 4) * 1024
                            s1 = ((2 * p2 + 1) % 4) * 1024
                            vfence(v, v.tensor_add(ptp[:], PT[:, s0:s0 + 1024], PT[:, s1:s1 + 1024]))
                            v.tensor_add(lacc[:], lacc[:], ptp[:]).then_inc(ALC, 1)
                        v.wait_ge(ABC, h + 1)
                        vfence(v, v.reciprocal(rlb[:], p_sp[0][:]))
                        v.wait_ge(APV, hb64 + 64)
                        if h >= 1:
                            v.wait_ge(SA, 16 * h)
                        v.tensor_mul(OT_h[:], p_pvT[:], rlb[:]).then_inc(AOTE, 1)

                @block.sync
                def _(sync):
                    for h in range(nheads):
                        sync.wait_ge(AOTE, h + 1)
                        sync.dma_start(out=d_OT[h, :, :], in_=OT_h[:]).then_inc(SA, 16)

                if taps:
                    @block.sync
                    def _(sync):
                        sync.wait_ge(AKGE, 16)
                        sync.dma_start(out=tap_aps["t_kT0"], in_=kT_nope[:]).then_inc(SG, 16)
                        sync.wait_ge(AVGE, 64)
                        sync.dma_start(out=tap_aps["t_V0"], in_=Vb[:]).then_inc(SG, 16)
                        sync.wait_ge(AOTE, 1)
                        sync.dma_start(out=tap_aps["t_OT0"], in_=OT_h[:]).then_inc(SG, 16)

                barrier()
                if max_phase <= 2:
                    return nc

        with ExitStack() as ctxLF:
            ELF = ctxLF.enter_context
            ln2T = ELF(nc.sbuf_tensor("ln2T", [128, 32 * R], BF16))
            sums = ELF(nc.sbuf_tensor("sums", [128, 8], F32))
            ssqs = ELF(nc.sbuf_tensor("ssqs", [128, 8], F32))

            with ExitStack() as ctxP:
                EP = ctxP.enter_context
                ctxPP = ExitStack()
                EPP = ctxPP.enter_context
                OT_all = EPP(nc.sbuf_tensor("OT_all", [128, 32 * R], BF16))
                wo_sb = [EPP(nc.sbuf_tensor(f"wo{i}", [128, 32 * 512], BF16)) for i in range(2)]
                atile = [EPP(nc.sbuf_tensor(f"atile{i}", [128, 512], F32)) for i in range(2)]
                tstat = EPP(nc.sbuf_tensor("tstat", [128, 4], F32))
                p_pr = [EPP(nc.psum_tensor(f"p_pr{i}", [128, 512], F32)) for i in range(4)]

                PMM = sem("p_mm")
                PEV = sem("p_ev")
                PSQ = sem("p_sq")
                PST = sem("p_stat")

                @block.sync
                def _(sync):
                    sync.wait_ge(SA, 16 * nheads)
                    for s2 in range(2):
                        sync.dma_start(
                            out=OT_all[:].rearrange("p (s n) -> p s n", s=32)[:, s2 * 16:(s2 + 1) * 16, :],
                            in_=d_OT.rearrange("h p q -> p h q")[:, s2 * 16:(s2 + 1) * 16, :],
                        ).then_inc(CLD, 16)
                cld_n[0] += 32

                @block.gpsimd
                def _(gp):
                    gp.wait_ge(SA, 16 * nheads)
                    for dc in range(8):
                        par = dc % 2
                        sl = [LE, LF][par]
                        if dc >= 2:
                            gp.wait_ge(PMM, (dc - 2) * 8 + 8)
                        for s2 in range(2):
                            gp.dma_start(
                                out=wo_sb[par][:].rearrange("p (s n) -> p s n", s=32)[:, s2 * 16:(s2 + 1) * 16, :],
                                in_=w_o[:, dc * 512:(dc + 1) * 512].rearrange("(s p) n -> p s n", p=128)[:, s2 * 16:(s2 + 1) * 16, :],
                            ).then_inc(sl, 16)

                WOB = [32 * ((nheads + 1) // 2), 32 * (nheads // 2)]

                @block.tensor
                def _(tn):
                    tn.wait_ge(CLD, cld_n[0])
                    tn.wait_ge(WLD, wld_n[0])
                    for dc in range(8):
                        par = dc % 2
                        sl = [LE, LF][par]
                        tn.wait_ge(sl, WOB[par] + 32 * (dc // 2 + 1))
                        for qt in range(8):
                            idx = dc * 8 + qt
                            if idx >= 4:
                                tn.wait_ge(PEV, idx - 3)
                            for s32 in range(32):
                                mm = tn.matmul(p_pr[qt % 4][:],
                                               OT_all[:, s32 * R + qt * 128:s32 * R + (qt + 1) * 128],
                                               wo_sb[par][:, s32 * 512:(s32 + 1) * 512],
                                               start=(s32 == 0), stop=(s32 == 31))
                            mm.then_inc(PMM, 1)

                @block.scalar
                def _(s):
                    for dc in range(8):
                        for qt in range(8):
                            idx = dc * 8 + qt
                            par = idx % 2
                            s.wait_ge(PMM, idx + 1)
                            if idx >= 2:
                                s.wait_ge([ST0, ST1][par], 16 * (idx // 2))
                            if idx >= 2:
                                s.wait_ge(PST, idx - 1)
                            s.activation(atile[par][:], p_pr[qt % 4][:], AF.Copy,
                                         accum_out=tstat[:, 2 * par:2 * par + 1]).then_inc(PEV, 1)
                            s.wait_ge(PEV, idx + 1)
                            s.activation(junk[:, par * 512:(par + 1) * 512], atile[par][:], AF.Square,
                                         accum_out=tstat[:, 2 * par + 1:2 * par + 2]).then_inc(PSQ, 1)

                @block.vector
                def _(v):
                    v.memset(sums[:], 0.0)
                    vfence(v, v.memset(ssqs[:], 0.0))
                    for dc in range(8):
                        for qt in range(8):
                            idx = dc * 8 + qt
                            par = idx % 2
                            v.wait_ge(PSQ, idx + 1)
                            vfence(v, v.tensor_add(sums[:, qt:qt + 1], sums[:, qt:qt + 1], tstat[:, 2 * par:2 * par + 1]))
                            v.tensor_add(ssqs[:, qt:qt + 1], ssqs[:, qt:qt + 1],
                                         tstat[:, 2 * par + 1:2 * par + 2]).then_inc(PST, 1)

                @block.sync
                def _(sync):
                    for dc in range(8):
                        for qt in range(8):
                            idx = dc * 8 + qt
                            par = idx % 2
                            sync.wait_ge(PSQ, idx + 1)
                            sync.dma_start(out=d_attn[qt * 128:(qt + 1) * 128, dc * 512:(dc + 1) * 512],
                                           in_=atile[par][:]).then_inc([ST0, ST1][par], 16)

                barrier()

                if taps:
                    @block.sync
                    def _(sync):
                        sync.wait_ge(ST0, 16 * 32)
                        sync.wait_ge(ST1, 16 * 32)
                        sync.dma_start(out=tap_aps["t_attn"], in_=d_attn).then_inc(SG, 16)
                    barrier()
                if max_phase <= 2.5:
                    ctxPP.close()
                    return nc

                ctxPP.close()
                xrt = [EP(nc.sbuf_tensor(f"xrt{i}", [128, D], F32)) for i in range(2)]
                art = [EP(nc.sbuf_tensor(f"art{i}", [128, D], F32)) for i in range(2)]
                x2t = EP(nc.sbuf_tensor("x2t", [128, D], F32))
                l2f = EP(nc.sbuf_tensor("l2f", [128, D], F32))
                l2bf = EP(nc.sbuf_tensor("l2bf", [128, D], BF16))
                p_t3 = [EP(nc.psum_tensor(f"p_t3{i}", [128, 1024], F32)) for i in range(2)]

                RVAR = sem("r_var")
                RRS = sem("r_rs")
                RX2 = sem("r_x2")
                RSQ = sem("r_sq")
                RVB = sem("r_vb")
                RRS2 = sem("r_rs2")
                RL2 = sem("r_l2")
                RMM = sem("r_mm")
                REV = sem("r_ev")

                XRB = [40 * 16, 32 * 16]
                X2SB = 16 * 32

                @block.sync
                def _(sync):
                    sync.wait_ge(ST0, 16 * 32)
                    sync.wait_ge(ST1, 16 * 32)
                    for qt in range(8):
                        par = qt % 2
                        sl = [LA, LB][par]
                        if qt >= 2:
                            sync.wait_ge(RL2, qt - 1)
                        sync.dma_start(out=xrt[par][:], in_=x_loc[qt * 128:(qt + 1) * 128, :]).then_inc(sl, 16)
                        sync.dma_start(out=art[par][:], in_=d_attn[qt * 128:(qt + 1) * 128, :]).then_inc(sl, 16)
                        if qt >= 1:
                            sync.wait_ge(RX2, qt)
                            sync.dma_start(out=d_x2[(qt - 1) * 128:qt * 128, :], in_=x2t[:]).then_inc(ST0, 16)
                    sync.wait_ge(RX2, 8)
                    sync.dma_start(out=d_x2[7 * 128:8 * 128, :], in_=x2t[:]).then_inc(ST0, 16)
                    sync.wait_ge(ST0, X2SB + 16 * 8)

                @block.vector
                def _(v):
                    for qt in range(8):
                        par = qt % 2
                        sl = [LA, LB][par]
                        v.wait_ge(sl, XRB[par] + 32 * (qt // 2 + 1))
                        v.tensor_scalar(stats[:, 0:1], sums[:, qt:qt + 1], -1.0 / D, None, ALU.mult)
                        vfence(v, v.tensor_scalar(stats[:, 1:2], ssqs[:, qt:qt + 1], 1.0 / D, None, ALU.mult))
                        vfence(v, v.tensor_mul(stats[:, 2:3], stats[:, 0:1], stats[:, 0:1]))
                        v.tensor_sub(stats[:, 3:4], stats[:, 1:2], stats[:, 2:3]).then_inc(RVAR, 1)
                        v.wait_ge(RRS, qt + 1)
                        vfence(v, v.tensor_scalar(art[par][:], art[par][:], stats[:, 0:1], None, ALU.add))
                        vfence(v, v.tensor_scalar(art[par][:], art[par][:], stats[:, 4:5], None, ALU.mult))
                        if qt >= 1:
                            v.wait_ge(RSQ, qt)
                            v.wait_ge(ST0, X2SB + 16 * qt)
                        v.tensor_add(x2t[:], xrt[par][:], art[par][:]).then_inc(RX2, 1)
                        v.wait_ge(RX2, qt + 1)
                        vfence(v, v.reduce_sum(stats[:, 5:6], x2t[:], axis=AX.X, negate=True))
                        vfence(v, v.tensor_scalar(stats[:, 6:7], stats[:, 5:6], 1.0 / D, None, ALU.mult))
                        v.tensor_mul(stats[:, 7:8], stats[:, 6:7], stats[:, 6:7])
                        v.wait_ge(RSQ, qt + 1)
                        vfence(v, v.tensor_add(stats[:, 23:24], stats[:, 9:10], stats[:, 22:23]))
                        vfence(v, v.tensor_scalar(stats[:, 8:9], stats[:, 23:24], 1.0 / D, None, ALU.mult))
                        v.tensor_sub(stats[:, 10:11], stats[:, 8:9], stats[:, 7:8]).then_inc(RVB, 1)
                        v.wait_ge(RRS2, qt + 1)
                        vfence(v, v.tensor_scalar(l2f[:], x2t[:], stats[:, 6:7], None, ALU.add))
                        if qt >= 1:
                            v.wait_ge(RMM, 4 * qt)
                        v.tensor_scalar(l2bf[:], l2f[:], stats[:, 11:12], None, ALU.mult).then_inc(RL2, 1)

                @block.scalar
                def _(s):
                    for qt in range(8):
                        s.wait_ge(RVAR, qt + 1)
                        sfence(s, s.activation(stats[:, 20:21], stats[:, 3:4], AF.Ln, bias=eps_sb[:]))
                        s.activation(stats[:, 4:5], stats[:, 20:21], AF.Exp, scale=-0.5).then_inc(RRS, 1)
                        s.wait_ge(RX2, qt + 1)
                        sfence(s, s.activation(junk[:], x2t[:, 0:2048], AF.Square, accum_out=stats[:, 9:10]))
                        s.activation(junk[:], x2t[:, 2048:4096], AF.Square, accum_out=stats[:, 22:23]).then_inc(RSQ, 1)
                        s.wait_ge(RVB, qt + 1)
                        sfence(s, s.activation(stats[:, 21:22], stats[:, 10:11], AF.Ln, bias=eps_sb[:]))
                        s.activation(stats[:, 11:12], stats[:, 21:22], AF.Exp, scale=-0.5).then_inc(RRS2, 1)
                        for g in range(4):
                            s.wait_ge(RMM, qt * 4 + g + 1)
                            for j8 in range(8):
                                j = g * 8 + j8
                                last = s.copy(ln2T[:, j * R + qt * 128:j * R + (qt + 1) * 128],
                                              p_t3[g % 2][:, j8 * 128:(j8 + 1) * 128])
                            last.then_inc(REV, 1)

                @block.tensor
                def _(tn):
                    for qt in range(8):
                        tn.wait_ge(RL2, qt + 1)
                        for g in range(4):
                            if g >= 2:
                                tn.wait_ge(REV, qt * 4 + g - 1)
                            elif qt >= 1:
                                tn.wait_ge(REV, (qt - 1) * 4 + g + 3)
                            for j8 in range(8):
                                j = g * 8 + j8
                                mm = tn.matmul(p_t3[g % 2][:, j8 * 128:(j8 + 1) * 128],
                                               l2bf[:, j * 128:(j + 1) * 128], ident_sb[:],
                                               start=True, stop=True)
                            mm.then_inc(RMM, 1)

                if taps:
                    @block.sync
                    def _(sync):
                        sync.wait_ge(ST0, X2SB + 16 * 8)
                        sync.dma_start(out=tap_aps["t_x2"], in_=d_x2).then_inc(SG, 16)

                barrier()
                if max_phase <= 3:
                    return nc

            with ExitStack() as ctxF1:
                EF1 = ctxF1.enter_context
                w1_sb = [EF1(nc.sbuf_tensor(f"w1sb{i}", [128, 32 * 512], BF16)) for i in range(2)]
                b1_sb = EF1(nc.sbuf_tensor("b1_sb", [128, 128], F32))
                hT_sb = [EF1(nc.sbuf_tensor(f"hTsb{i}", [128, 1024], BF16)) for i in range(2)]
                p_h = [EF1(nc.psum_tensor(f"p_h{i}", [128, 1024], F32)) for i in range(2)]

                FMM = sem("f_mm")
                FEV = sem("f_ev")
                FST = sem("f_st")

                @block.sync
                def _(sync):
                    sync.dma_start(out=b1_sb[:], in_=b1r).then_inc(CLD, 16)
                cld_n[0] += 16

                WOB2 = [WOB[0] + 32 * 4, WOB[1] + 32 * 4]

                @block.gpsimd
                def _(gp):
                    gp.wait_ge(ST0, 16 * 40)
                    gp.wait_ge(ST1, 16 * 32)
                    for fs in range(32):
                        par = fs % 2
                        sl = [LE, LF][par]
                        if fs >= 2:
                            gp.wait_ge(FMM, (fs - 2) * 4 + 4)
                        for s2 in range(2):
                            gp.dma_start(
                                out=w1_sb[par][:].rearrange("p (s n) -> p s n", s=32)[:, s2 * 16:(s2 + 1) * 16, :],
                                in_=w1[:, fs * 512:(fs + 1) * 512].rearrange("(s p) n -> p s n", p=128)[:, s2 * 16:(s2 + 1) * 16, :],
                            ).then_inc(sl, 16)

                @block.tensor
                def _(tn):
                    tn.wait_ge(CLD, cld_n[0])
                    tn.wait_ge(WLD, wld_n[0])
                    for fs in range(32):
                        par = fs % 2
                        sl = [LE, LF][par]
                        tn.wait_ge(sl, WOB2[par] + 32 * (fs // 2 + 1))
                        for f4 in range(4):
                            key = fs * 4 + f4
                            if key >= 2:
                                tn.wait_ge(FEV, key - 1)
                            for j in range(32):
                                for qc in range(2):
                                    mm = tn.matmul(p_h[f4 % 2][:, qc * 512:qc * 512 + 512],
                                                   w1_sb[par][:, j * 512 + f4 * 128:j * 512 + (f4 + 1) * 128],
                                                   ln2T[:, j * R + qc * 512:j * R + qc * 512 + 512],
                                                   start=(j == 0), stop=(j == 31))
                            mm.then_inc(FMM, 1)

                HSB = [16 * 40, 16 * 32]

                @block.scalar
                def _(s):
                    for key in range(128):
                        par = key % 2
                        s.wait_ge(FMM, key + 1)
                        if key >= 2:
                            s.wait_ge([ST0, ST1][par], HSB[par] + 16 * (key // 2))
                        s.activation(hT_sb[par][:], p_h[key % 2][:], AF.Gelu_apprx_tanh,
                                     bias=b1_sb[:, key:key + 1]).then_inc(FEV, 1)

                @block.sync
                def _(sync):
                    for key in range(128):
                        par = key % 2
                        sync.wait_ge(FEV, key + 1)
                        sync.dma_start(out=d_hT[key * 128:(key + 1) * 128, :],
                                       in_=hT_sb[par][:]).then_inc([ST0, ST1][par], 16)

                if taps:
                    @block.sync
                    def _(sync):
                        sync.wait_ge(ST0, HSB[0] + 16 * 64)
                        sync.wait_ge(ST1, HSB[1] + 16 * 64)
                        sync.dma_start(out=tap_aps["t_hT0"], in_=d_hT[0:128, :]).then_inc(SG, 16)

                barrier()
                if max_phase <= 4:
                    return nc

            with ExitStack() as ctxF2:
                EF2 = ctxF2.enter_context
                w2_sb = [EF2(nc.sbuf_tensor(f"w2sb{i}", [128, 16 * 512], BF16)) for i in range(2)]
                hTp = [EF2(nc.sbuf_tensor(f"hTp{i}", [128, 16 * 1024], BF16)) for i in range(2)]
                b2_sb = EF2(nc.sbuf_tensor("b2_sb", [1, D], F32))
                o2t = [EF2(nc.sbuf_tensor(f"o2t{i}", [128, 512], F32)) for i in range(2)]
                tstat2 = EF2(nc.sbuf_tensor("tstat2", [128, 4], F32))
                p_s = [EF2(nc.psum_tensor(f"p_s{i}", [128, 512], F32)) for i in range(8)]

                GFB = sem("g_fb")
                GMM = sem("g_mm")
                GEV = sem("g_ev")
                GSQ = sem("g_sq")
                GST = sem("g_stat")

                W2B = [WOB2[0] + 32 * 16, WOB2[1] + 32 * 16]
                HLB = [32 * 32, 32 * 32]

                F2SB = [16 * (32 + 8 + 64), 16 * (32 + 64)]

                @block.sync
                def _(sync):
                    sync.dma_start(out=b2_sb[:], in_=b2r).then_inc(CLD, 16)
                    sync.wait_ge(ST0, F2SB[0])
                    sync.wait_ge(ST1, F2SB[1])
                    for dc in range(8):
                        if dc >= 1:
                            for qt in range(8):
                                idx = (dc - 1) * 8 + qt
                                par = idx % 2
                                sync.wait_ge(GSQ, idx + 1)
                                sync.dma_start(out=d_out2[qt * 128:(qt + 1) * 128, (dc - 1) * 512:dc * 512],
                                               in_=o2t[par][:]).then_inc([ST0, ST1][par], 16)
                        for fb in range(8):
                            key = dc * 8 + fb
                            par = key % 2
                            sl = [LC, LD2][par]
                            if key >= 2:
                                sync.wait_ge(GFB, key - 1)
                            sync.dma_start(
                                out=hTp[par][:].rearrange("p (a q) -> p a q", a=16),
                                in_=d_hT[fb * 2048:(fb + 1) * 2048, :].rearrange("(a p) q -> p a q", p=128),
                            ).then_inc(sl, 16)
                    for qt in range(8):
                        idx = 7 * 8 + qt
                        par = idx % 2
                        sync.wait_ge(GSQ, idx + 1)
                        sync.dma_start(out=d_out2[qt * 128:(qt + 1) * 128, 7 * 512:8 * 512],
                                       in_=o2t[par][:]).then_inc([ST0, ST1][par], 16)
                    sync.wait_ge(ST0, F2SB[0] + 16 * 32)
                    sync.wait_ge(ST1, F2SB[1] + 16 * 32)
                    if taps:
                        sync.dma_start(out=tap_aps["t_out2"], in_=d_out2).then_inc(SG, 16)
                cld_n[0] += 16

                @block.gpsimd
                def _(gp):
                    gp.wait_ge(ST0, F2SB[0])
                    gp.wait_ge(ST1, F2SB[1])
                    for dc in range(8):
                        for fb in range(8):
                            key = dc * 8 + fb
                            par = key % 2
                            sl = [LE, LF][par]
                            if key >= 2:
                                gp.wait_ge(GFB, key - 1)
                            gp.dma_start(
                                out=w2_sb[par][:].rearrange("p (s n) -> p s n", s=16),
                                in_=w2[fb * 2048:(fb + 1) * 2048, dc * 512:(dc + 1) * 512].rearrange("(s p) n -> p s n", p=128),
                            ).then_inc(sl, 16)

                @block.tensor
                def _(tn):
                    tn.wait_ge(CLD, cld_n[0])
                    tn.wait_ge(WLD, wld_n[0])
                    for dc in range(8):
                        for fb in range(8):
                            key = dc * 8 + fb
                            par = key % 2
                            tn.wait_ge([LE, LF][par], W2B[par] + 16 * (key // 2 + 1))
                            tn.wait_ge([LC, LD2][par], HLB[par] + 16 * (key // 2 + 1))
                            if fb == 0 and dc >= 1:
                                tn.wait_ge(GSQ, dc * 8)
                            for qt in range(8):
                                for j in range(16):
                                    mm = tn.matmul(p_s[qt][:],
                                                   hTp[par][:, j * 1024 + qt * 128:j * 1024 + (qt + 1) * 128],
                                                   w2_sb[par][:, j * 512:(j + 1) * 512],
                                                   start=(fb == 0 and j == 0), stop=False)
                            mm.then_inc(GFB, 1)
                        for qt in range(8):
                            mm = tn.matmul(p_s[qt][:], onesf_sb[:], b2_sb[:, dc * 512:(dc + 1) * 512],
                                           start=False, stop=True)
                            mm.then_inc(GMM, 1)

                @block.scalar
                def _(s):
                    for dc in range(8):
                        for qt in range(8):
                            idx = dc * 8 + qt
                            par = idx % 2
                            s.wait_ge(GMM, idx + 1)
                            if idx >= 2:
                                s.wait_ge([ST0, ST1][par], F2SB[par] + 16 * (idx // 2))
                            if idx >= 2:
                                s.wait_ge(GST, idx - 1)
                            s.activation(o2t[par][:], p_s[qt][:], AF.Copy,
                                         accum_out=tstat2[:, 2 * par:2 * par + 1]).then_inc(GEV, 1)
                            s.wait_ge(GEV, idx + 1)
                            s.activation(junk[:, par * 512:(par + 1) * 512], o2t[par][:], AF.Square,
                                         accum_out=tstat2[:, 2 * par + 1:2 * par + 2]).then_inc(GSQ, 1)

                @block.vector
                def _(v):
                    v.memset(sums[:], 0.0)
                    vfence(v, v.memset(ssqs[:], 0.0))
                    for dc in range(8):
                        for qt in range(8):
                            idx = dc * 8 + qt
                            par = idx % 2
                            v.wait_ge(GSQ, idx + 1)
                            vfence(v, v.tensor_add(sums[:, qt:qt + 1], sums[:, qt:qt + 1], tstat2[:, 2 * par:2 * par + 1]))
                            v.tensor_add(ssqs[:, qt:qt + 1], ssqs[:, qt:qt + 1],
                                         tstat2[:, 2 * par + 1:2 * par + 2]).then_inc(GST, 1)

                barrier()

            with ExitStack() as ctxF3:
                EF3 = ctxF3.enter_context
                x2r = [EF3(nc.sbuf_tensor(f"x2r{i}", [128, D], F32)) for i in range(2)]
                o2r = [EF3(nc.sbuf_tensor(f"o2r{i}", [128, D], F32)) for i in range(2)]
                xo = [EF3(nc.sbuf_tensor(f"xo{i}", [128, D], F32)) for i in range(2)]

                ZVAR = sem("z_var")
                ZRS = sem("z_rs")
                ZXO = sem("z_xo")

                ZB = [(8 + 32 + 4 * 2) * 16, (32 + 4 * 2) * 16]

                @block.sync
                def _(sync):
                    for qt in range(8):
                        par = qt % 2
                        sl = [LA, LB][par]
                        if qt >= 2:
                            sync.wait_ge(ZXO, qt - 1)
                        sync.dma_start(out=x2r[par][:], in_=d_x2[qt * 128:(qt + 1) * 128, :]).then_inc(sl, 16)
                        sync.dma_start(out=o2r[par][:], in_=d_out2[qt * 128:(qt + 1) * 128, :]).then_inc(sl, 16)
                        if qt >= 2:
                            sync.dma_start(out=out_rows[(qt - 2) * 128:(qt - 1) * 128, :],
                                           in_=xo[(qt - 2) % 2][:]).then_inc([OST, OST1][qt % 2], 16)

                @block.vector
                def _(v):
                    for qt in range(8):
                        par = qt % 2
                        sl = [LA, LB][par]
                        v.wait_ge(sl, ZB[par] + 32 * (qt // 2 + 1))
                        v.tensor_scalar(stats[:, 0:1], sums[:, qt:qt + 1], -1.0 / D, None, ALU.mult)
                        vfence(v, v.tensor_scalar(stats[:, 1:2], ssqs[:, qt:qt + 1], 1.0 / D, None, ALU.mult))
                        vfence(v, v.tensor_mul(stats[:, 2:3], stats[:, 0:1], stats[:, 0:1]))
                        v.tensor_sub(stats[:, 3:4], stats[:, 1:2], stats[:, 2:3]).then_inc(ZVAR, 1)
                        v.wait_ge(ZRS, qt + 1)
                        vfence(v, v.tensor_scalar(o2r[par][:], o2r[par][:], stats[:, 0:1], None, ALU.add))
                        vfence(v, v.tensor_scalar(o2r[par][:], o2r[par][:], stats[:, 4:5], None, ALU.mult))
                        if qt >= 2:
                            v.wait_ge([OST, OST1][qt % 2], 16 * (qt // 2))
                        v.tensor_add(xo[par][:], x2r[par][:], o2r[par][:]).then_inc(ZXO, 1)

                @block.scalar
                def _(s):
                    for qt in range(8):
                        s.wait_ge(ZVAR, qt + 1)
                        sfence(s, s.activation(stats[:, 20:21], stats[:, 3:4], AF.Ln, bias=eps_sb[:]))
                        s.activation(stats[:, 4:5], stats[:, 20:21], AF.Exp, scale=-0.5).then_inc(ZRS, 1)

                @block.sync
                def _(sync):
                    for qt in range(6, 8):
                        par = qt % 2
                        sync.wait_ge(ZXO, qt + 1)
                        sync.dma_start(out=out_rows[qt * 128:(qt + 1) * 128, :],
                                       in_=xo[par][:]).then_inc([OST, OST1][qt % 2], 16)
                    sync.wait_ge(OST, 16 * 4)
                    sync.wait_ge(OST1, 16 * 4)
                    if taps:
                        sync.wait_ge(SG, 16 * 9)

    return nc


def _host_constants():
    inv_freq = 1.0 / (10000.0 ** (np.arange(0, RD, 2, dtype=np.float64) / RD))
    t = np.arange(S, dtype=np.float64)
    freqs = np.outer(t, inv_freq)
    emb = np.concatenate([freqs, freqs], axis=-1)
    cosk = np.cos(emb).astype(np.float32)
    sink = np.sin(emb).astype(np.float32)
    ident = np.eye(128, dtype=ml_dtypes.bfloat16)
    ones_c = np.ones((128, 128), dtype=ml_dtypes.bfloat16)
    return cosk, sink, ident, ones_c


def make_in_maps(inputs):
    x = np.ascontiguousarray(np.asarray(inputs["L0_attn_hidden_states"], dtype=np.float32).reshape(S, D))
    cosk, sink, ident, ones_c = _host_constants()
    common = {
        "x_full": x,
        "w_qa": np.ascontiguousarray(inputs["L0_attn_w_qa"], dtype=np.float32),
        "g_qa": np.ascontiguousarray(inputs["L0_attn_g_qa"], dtype=np.float32).reshape(1, QR),
        "w_qb": np.ascontiguousarray(inputs["L0_attn_w_qb"], dtype=np.float32),
        "w_kva": np.ascontiguousarray(inputs["L0_attn_w_kva"], dtype=np.float32),
        "g_kva": np.ascontiguousarray(inputs["L0_attn_g_kva"], dtype=np.float32).reshape(1, KVR),
        "w_kvb": np.ascontiguousarray(inputs["L0_attn_w_kvb"], dtype=np.float32),
        "w_o": np.ascontiguousarray(inputs["L0_attn_w_o"], dtype=np.float32),
        "w1": np.ascontiguousarray(inputs["L0_ffn_inter_w"], dtype=np.float32),
        "b1r": np.ascontiguousarray(np.asarray(inputs["L0_ffn_inter_b"], dtype=np.float32).reshape(128, 128).T),
        "w2": np.ascontiguousarray(inputs["L0_ffn_output_w"], dtype=np.float32),
        "b2r": np.ascontiguousarray(inputs["L0_ffn_output_b"], dtype=np.float32).reshape(1, D),
        "cosk": cosk,
        "sink": sink,
        "identc": ident,
        "epsc": np.full((128, 1), LN_EPS, np.float32),
        "onesfc": np.ones((1, 128), np.float32),
        "onesf2c": np.ones((128, 128), np.float32),
        "onesc": ones_c,
    }
    in_maps = []
    for c in range(NCORES):
        m = dict(common)
        m["x_loc"] = np.ascontiguousarray(x[c * R:(c + 1) * R, :])
        m["cosqT"] = np.ascontiguousarray(cosk[c * R:(c + 1) * R, :].T)
        m["sinqT"] = np.ascontiguousarray(sink[c * R:(c + 1) * R, :].T)
        in_maps.append(m)
    return in_maps


def kernel(**inputs):
    in_maps = make_in_maps(inputs)
    nc = build_nc(taps=False)
    res = run_bass_kernel_spmd(nc, in_maps, core_ids=list(range(NCORES)))
    out = np.concatenate([res.results[c]["out_rows"] for c in range(NCORES)], axis=0)
    return out.reshape(1, S, D).astype(np.float32)


# version 06
# speedup vs baseline: 1.4080x; 1.0161x over previous
import numpy as np
import ml_dtypes
from contextlib import ExitStack
import concourse.bass as bass
import concourse.mybir as mybir
from concourse.bass_utils import run_bass_kernel_spmd

F32 = mybir.dt.float32
BF16 = mybir.dt.bfloat16
AF = mybir.ActivationFunctionType
ALU = mybir.AluOpType
AX = mybir.AxisListType

S, D, R = 8192, 4096, 1024
NH, QR, KVR, RD, ND, VD = 32, 1536, 512, 64, 128, 128
QK = ND + RD
FF = 16384
SC = float(QK) ** -0.5
NCORES = 8
LN_EPS = 1e-5
RMS_EPS = 1e-6


def build_nc(taps=False, nheads=NH, max_phase=99):
    nc = bass.Bass("TRN2", target_bir_lowering=False)

    x_full = nc.dram_tensor("x_full", [S, D], F32, kind="ExternalInput").ap()
    x_loc = nc.dram_tensor("x_loc", [R, D], F32, kind="ExternalInput").ap()
    w_qa = nc.dram_tensor("w_qa", [D, QR], F32, kind="ExternalInput").ap()
    g_qa = nc.dram_tensor("g_qa", [1, QR], F32, kind="ExternalInput").ap()
    w_qb = nc.dram_tensor("w_qb", [QR, NH * QK], F32, kind="ExternalInput").ap()
    w_kva = nc.dram_tensor("w_kva", [D, KVR + RD], F32, kind="ExternalInput").ap()
    g_kva = nc.dram_tensor("g_kva", [1, KVR], F32, kind="ExternalInput").ap()
    w_kvb = nc.dram_tensor("w_kvb", [KVR, NH * (ND + VD)], F32, kind="ExternalInput").ap()
    w_o = nc.dram_tensor("w_o", [D, D], F32, kind="ExternalInput").ap()
    w1 = nc.dram_tensor("w1", [D, FF], F32, kind="ExternalInput").ap()
    b1r = nc.dram_tensor("b1r", [128, 128], F32, kind="ExternalInput").ap()
    w2 = nc.dram_tensor("w2", [FF, D], F32, kind="ExternalInput").ap()
    b2r = nc.dram_tensor("b2r", [1, D], F32, kind="ExternalInput").ap()
    cosk = nc.dram_tensor("cosk", [S, RD], F32, kind="ExternalInput").ap()
    sink = nc.dram_tensor("sink", [S, RD], F32, kind="ExternalInput").ap()
    cosqT = nc.dram_tensor("cosqT", [RD, R], F32, kind="ExternalInput").ap()
    sinqT = nc.dram_tensor("sinqT", [RD, R], F32, kind="ExternalInput").ap()
    identc = nc.dram_tensor("identc", [128, 128], BF16, kind="ExternalInput").ap()
    epsc = nc.dram_tensor("epsc", [128, 1], F32, kind="ExternalInput").ap()
    onesfc = nc.dram_tensor("onesfc", [1, 128], F32, kind="ExternalInput").ap()
    onesf2c = nc.dram_tensor("onesf2c", [128, 128], F32, kind="ExternalInput").ap()
    onesc = nc.dram_tensor("onesc", [128, 128], BF16, kind="ExternalInput").ap()

    out_rows = nc.dram_tensor("out_rows", [R, D], F32, kind="ExternalOutput").ap()

    d_OT = nc.dram_tensor("d_OT", [NH, 128, R], BF16, kind="Internal").ap()
    d_attn = nc.dram_tensor("d_attn", [R, D], F32, kind="Internal").ap()
    d_x2 = nc.dram_tensor("d_x2", [R, D], F32, kind="Internal").ap()
    d_hT = nc.dram_tensor("d_hT", [FF, R], BF16, kind="Internal").ap()
    d_out2 = nc.dram_tensor("d_out2", [R, D], F32, kind="Internal").ap()

    tap_aps = {}
    if taps:
        for name, shape, dt in [
            ("t_qcT", [128, 12 * R], BF16),
            ("t_kcT", [128, 4 * S], BF16),
            ("t_krT", [65, S], BF16),
            ("t_kT0", [128, S], BF16),
            ("t_V0", [128, 64 * VD], BF16),
            ("t_OT0", [128, R], BF16),
            ("t_x2", [R, D], F32),
            ("t_hT0", [128, R], BF16),
            ("t_attn", [R, D], F32),
            ("t_out2", [R, D], F32),
        ]:
            tap_aps[name] = nc.dram_tensor(name, shape, dt, kind="ExternalOutput").ap()

    with ExitStack() as ctx0:
        E0 = ctx0.enter_context

        sems = {}

        def sem(name):
            if name not in sems:
                sems[name] = E0(nc.semaphore(name))
            return sems[name]

        PH = sem("ph")
        CLD = sem("c_ld")
        OST = sem("o_st")
        OST1 = sem("o_st1")
        LA, LB = sem("la"), sem("lb")
        LC, LD2 = sem("lc"), sem("ld2")
        LE, LF = sem("le"), sem("lf")
        ST0, ST1 = sem("st0"), sem("st1")
        SA = sem("sa")
        SG = sem("sg")

        WLD = sem("w_ld")
        cld_n = [0]
        wld_n = [0]
        ph_n = [0]

        ident_sb = E0(nc.sbuf_tensor("ident_sb", [128, 128], BF16))
        eps_sb = E0(nc.sbuf_tensor("eps_sb", [128, 1], F32))
        onesf_sb = E0(nc.sbuf_tensor("onesf_sb", [1, 128], F32))
        onesf2_sb = E0(nc.sbuf_tensor("onesf2_sb", [128, 128], F32))
        ones_sb = E0(nc.sbuf_tensor("ones_sb", [128, 128], BF16))
        junk = E0(nc.sbuf_tensor("junk", [128, 2048], BF16))
        stats = E0(nc.sbuf_tensor("stats", [128, 32], F32))

        block = E0(nc.Block())
        ENG = ["tensor", "scalar", "vector", "sync", "gpsimd"]

        VFS = sem("v_fence")
        SFS = sem("s_fence")
        fence_n = {"v": 0, "s": 0}

        def vfence(v, inst):
            fence_n["v"] += 1
            inst.then_inc(VFS, 1)
            v.wait_ge(VFS, fence_n["v"])

        def sfence(s, inst):
            fence_n["s"] += 1
            inst.then_inc(SFS, 1)
            s.wait_ge(SFS, fence_n["s"])

        def barrier():
            target = ph_n[0] + len(ENG)

            @block.scalar
            def _(s):
                s.activation(stats[:, 30:31], eps_sb[:], AF.Copy).then_inc(PH, 1)
                s.wait_ge(PH, target)

            @block.vector
            def _(v):
                v.memset(stats[:, 31:32], 0.0).then_inc(PH, 1)
                v.wait_ge(PH, target)

            for name in ["tensor", "sync", "gpsimd"]:
                def mk():
                    def _(eng):
                        eng.sem_inc(PH, 1)
                        eng.wait_ge(PH, target)
                    return _
                getattr(block, name)(mk())
            ph_n[0] = target

        @block.sync
        def _(sync):
            sync.dma_start(out=ident_sb[:], in_=identc).then_inc(CLD, 16)
            sync.dma_start(out=ones_sb[:], in_=onesc).then_inc(CLD, 16)
            sync.dma_start(out=eps_sb[:], in_=epsc).then_inc(CLD, 16)
            sync.dma_start(out=onesf_sb[:], in_=onesfc).then_inc(CLD, 16)
            sync.dma_start(out=onesf2_sb[:], in_=onesf2c).then_inc(CLD, 16)
        cld_n[0] += 80

        d_qcT = nc.dram_tensor("d_qcT", [128, 12 * R], BF16, kind="Internal").ap()
        QSP = sem("q_spill")
        with ExitStack() as ctxA:
            EA = ctxA.enter_context
            if True:
                with ExitStack() as ctxQ:
                    EQ = ctxQ.enter_context
                    qcT = EQ(nc.sbuf_tensor("qcT", [128, 12 * R], BF16))
                    xt = [EQ(nc.sbuf_tensor("xt0", [128, D], F32))] * 2
                    xcb = [EQ(nc.sbuf_tensor("xcb0", [128, D], BF16))] * 2
                    xcT = [EQ(nc.sbuf_tensor("xcT0", [128, D], BF16))] * 2
                    wqa_sb = EQ(nc.sbuf_tensor("wqa_sb", [128, 32 * QR], BF16))
                    gqa_b = EQ(nc.sbuf_tensor("gqa_b", [128, QR], F32))
                    qln = EQ(nc.sbuf_tensor("qln", [128, QR], F32))
                    qcb = [EQ(nc.sbuf_tensor("qcb0", [128, QR], BF16))] * 2
                    p_aq = EQ(nc.psum_tensor("p_aq", [128, QR], F32))
                    p_tr = [EQ(nc.psum_tensor(f"p_tr{i}", [128, 1024], F32)) for i in range(2)]

                    NT = 8
                    SQ = sem("q_sq")
                    VXC = sem("q_xc")
                    VVAR = sem("q_var")
                    MTR = sem("q_mtr")
                    EXT = sem("q_ext")
                    MAQ = sem("q_maq")
                    EQL = sem("q_eql")
                    SUQ = sem("q_suq")
                    VQC = sem("q_vqc")
                    MQT = sem("q_mqt")
                    EQT = sem("q_eqt")
                    VSS = sem("q_vss")
                    SUQ2 = sem("q_suq2")

                    @block.gpsimd
                    def _(gp):
                        for s4 in range(4):
                            gp.dma_start(
                                out=wqa_sb[:].rearrange("p (s n) -> p s n", s=32)[:, s4 * 8:(s4 + 1) * 8, :],
                                in_=w_qa.rearrange("(s p) n -> p s n", p=128)[:, s4 * 8:(s4 + 1) * 8, :],
                            ).then_inc(WLD, 16)
                    wld_n[0] += 64

                    @block.sync
                    def _(sync):
                        sync.dma_start(out=gqa_b[:], in_=g_qa.broadcast_to((128, QR))).then_inc(CLD, 16)
                        for t in range(NT):
                            if t >= 1:
                                sync.wait_ge(VXC, t)
                                sync.wait_ge(SQ, t)
                            sync.dma_start(out=xt[0][:], in_=x_loc[t * 128:(t + 1) * 128, :]).then_inc(LA, 16)
                        sync.wait_ge(EQT, 16)
                        sync.dma_start(out=d_qcT, in_=qcT[:]).then_inc(QSP, 16)
                        sync.wait_ge(QSP, 16)
                    cld_n[0] += 16

                    @block.scalar
                    def _(s):
                        s.wait_ge(CLD, cld_n[0])
                        s.wait_ge(WLD, wld_n[0])
                        for t in range(NT):
                            par = t % 2
                            par2 = (t % 2) * 22
                            s.wait_ge(LA, 16 * (t + 1))
                            sfence(s, s.activation(junk[:], xt[0][:, 0:2048], AF.Square,
                                         accum_out=stats[:, 6 + par2:7 + par2]))
                            s.activation(junk[:], xt[0][:, 2048:4096], AF.Square,
                                         accum_out=stats[:, 7 + par2:8 + par2]).then_inc(SQ, 1)
                            s.wait_ge(VVAR, t + 1)
                            sfence(s, s.activation(stats[:, 12:13], stats[:, 10 + par:11 + par], AF.Ln, bias=eps_sb[:]))
                            s.activation(stats[:, 14 + par:15 + par], stats[:, 12:13], AF.Exp, scale=-0.5)
                            if t >= 1:
                                s.wait_ge(MAQ, t)
                            for g in range(4):
                                s.wait_ge(MTR, t * 4 + g + 1)
                                s.copy(xcT[par][:, g * 1024:(g + 1) * 1024], p_tr[g % 2][:]).then_inc(EXT, 1)
                            s.wait_ge(MAQ, t + 1)
                            if t >= 1:
                                s.wait_ge(VQC, t)
                            s.activation(qln[:], p_aq[:], AF.Copy,
                                         scale=stats[:, 14 + par:15 + par]).then_inc(EQL, 1)
                            s.wait_ge(EQL, t + 1)
                            s.activation(junk[:, 0:QR // 2], qln[:, 0:QR // 2], AF.Square, accum_out=stats[:, 22:23])
                            s.activation(junk[:, QR // 2:QR], qln[:, QR // 2:QR], AF.Square,
                                         accum_out=stats[:, 23:24]).then_inc(SUQ, 1)
                            s.wait_ge(VSS, t + 1)
                            sfence(s, s.activation(stats[:, 24:25], stats[:, 27:28], AF.Ln))
                            s.activation(stats[:, 16:17], stats[:, 24:25], AF.Exp, scale=0.5).then_inc(SUQ2, 1)
                            for g in range(2):
                                s.wait_ge(MQT, t * 2 + g + 1)
                                nj = 8 if g == 0 else 4
                                for j8 in range(nj):
                                    j = g * 8 + j8
                                    last = s.copy(qcT[:, j * R + t * 128:j * R + (t + 1) * 128],
                                                  p_tr[g % 2][:, j8 * 128:(j8 + 1) * 128])
                                last.then_inc(EQT, 1)

                    @block.vector
                    def _(v):
                        for t in range(NT):
                            par = t % 2
                            v.wait_ge(LA, 16 * (t + 1))
                            vfence(v, v.reduce_sum(stats[:, 0:1], xt[0][:], axis=AX.X, negate=True))
                            vfence(v, v.tensor_scalar(stats[:, 2 + par:3 + par], stats[:, 0:1], 1.0 / D, None, ALU.mult))
                            if t >= 1:
                                v.wait_ge(MTR, (t - 1) * 4 + 4)
                            v.tensor_scalar(xcb[0][:], xt[0][:], stats[:, 2 + par:3 + par],
                                            None, ALU.add).then_inc(VXC, 1)
                            v.tensor_mul(stats[:, 4:5], stats[:, 2 + par:3 + par], stats[:, 2 + par:3 + par])
                            par2 = (t % 2) * 22
                            v.wait_ge(SQ, t + 1)
                            vfence(v, v.tensor_add(stats[:, 5:6], stats[:, 6 + par2:7 + par2], stats[:, 7 + par2:8 + par2]))
                            vfence(v, v.tensor_scalar(stats[:, 8:9], stats[:, 5:6], 1.0 / D, None, ALU.mult))
                            v.tensor_sub(stats[:, 10 + par:11 + par], stats[:, 8:9], stats[:, 4:5]).then_inc(VVAR, 1)
                            v.wait_ge(SUQ, t + 1)
                            vfence(v, v.tensor_add(stats[:, 26:27], stats[:, 22:23], stats[:, 23:24]))
                            v.tensor_scalar(stats[:, 27:28], stats[:, 26:27], 1.0 / QR, None, ALU.mult).then_inc(VSS, 1)
                            v.wait_ge(SUQ2, t + 1)
                            vfence(v, v.tensor_scalar(stats[:, 20:21], stats[:, 16:17], RMS_EPS, None, ALU.add))
                            vfence(v, v.reciprocal(stats[:, 18:19], stats[:, 20:21]))
                            if t >= 1:
                                v.wait_ge(MQT, (t - 1) * 2 + 2)
                            v.scalar_tensor_tensor(qcb[0][:], qln[:], stats[:, 18:19], gqa_b[:],
                                                   ALU.mult, ALU.mult).then_inc(VQC, 1)

                    @block.tensor
                    def _(tn):
                        tn.wait_ge(CLD, cld_n[0])
                        tn.wait_ge(WLD, wld_n[0])
                        for t in range(NT):
                            par = 0
                            tn.wait_ge(VXC, t + 1)
                            for g in range(4):
                                if g >= 2:
                                    tn.wait_ge(EXT, t * 4 + g - 1)
                                elif t >= 1:
                                    if g == 0:
                                        tn.wait_ge(EQT, (t - 1) * 2 + 1)
                                    else:
                                        tn.wait_ge(EQT, (t - 1) * 2 + 2)
                                for j8 in range(8):
                                    j = g * 8 + j8
                                    mm = tn.matmul(p_tr[g % 2][:, j8 * 128:(j8 + 1) * 128],
                                                   xcb[par][:, j * 128:(j + 1) * 128], ident_sb[:],
                                                   start=True, stop=True)
                                mm.then_inc(MTR, 1)
                            tn.wait_ge(EXT, t * 4 + 4)
                            if t >= 1:
                                tn.wait_ge(EQL, t)
                            for j in range(32):
                                for nch in range(3):
                                    mm = tn.matmul(p_aq[:, nch * 512:(nch + 1) * 512],
                                                   xcT[par][:, j * 128:(j + 1) * 128],
                                                   wqa_sb[:, j * QR + nch * 512:j * QR + nch * 512 + 512],
                                                   start=(j == 0), stop=(j == 31))
                            mm.then_inc(MAQ, 1)
                            tn.wait_ge(VQC, t + 1)
                            for g in range(2):
                                tn.wait_ge(EXT, t * 4 + 3 + g)
                                nj = 8 if g == 0 else 4
                                for j8 in range(nj):
                                    j = g * 8 + j8
                                    mm = tn.matmul(p_tr[g % 2][:, j8 * 128:(j8 + 1) * 128],
                                                   qcb[par][:, j * 128:(j + 1) * 128], ident_sb[:],
                                                   start=True, stop=True)
                                mm.then_inc(MQT, 1)

                    barrier()

                kcT = EA(nc.sbuf_tensor("kcT", [128, 4 * S], BF16))
                krT = EA(nc.sbuf_tensor("krT", [65, S], BF16))
                with ExitStack() as ctxKV:
                    EKV = ctxKV.enter_context
                    xt = [EKV(nc.sbuf_tensor(f"kxt{i}", [128, D], F32)) for i in range(2)]
                    xcb = [EKV(nc.sbuf_tensor(f"kxcb{i}", [128, D], BF16)) for i in range(2)]
                    xcT = [EKV(nc.sbuf_tensor(f"kxcT{i}", [128, D], BF16)) for i in range(2)]
                    wkva_sb = EKV(nc.sbuf_tensor("wkva_sb", [128, 32 * (KVR + RD)], BF16))
                    gkva_b = EKV(nc.sbuf_tensor("gkva_b", [128, KVR], F32))
                    kvln = EKV(nc.sbuf_tensor("kvln", [128, KVR + RD], F32))
                    kcb = [EKV(nc.sbuf_tensor(f"kcb{i}", [128, KVR], BF16)) for i in range(2)]
                    ckb = [EKV(nc.sbuf_tensor(f"ckb{i}", [128, RD], F32)) for i in range(2)]
                    skb = [EKV(nc.sbuf_tensor(f"skb{i}", [128, RD], F32)) for i in range(2)]
                    krf = EKV(nc.sbuf_tensor("krf", [128, RD], BF16))
                    krt1 = EKV(nc.sbuf_tensor("krt1", [128, RD], F32))
                    krt2 = EKV(nc.sbuf_tensor("krt2", [128, RD], F32))
                    p_akv = EKV(nc.psum_tensor("p_akv", [128, 1024], F32))
                    p_t2 = [EKV(nc.psum_tensor(f"p_t2{i}", [128, 1024], F32)) for i in range(2)]

                    NT2 = 64
                    SQ2 = sem("k_sq")
                    VXC2 = sem("k_xc")
                    VVAR2 = sem("k_var")
                    MTR2 = sem("k_mtr")
                    EXT2 = sem("k_ext")
                    MKV = sem("k_mkv")
                    EKL = sem("k_ekl")
                    SUK = sem("k_suk")
                    VKC = sem("k_vkc")
                    VKR = sem("k_vkr")
                    MKT = sem("k_mkt")
                    EKT = sem("k_ekt")

                    @block.gpsimd
                    def _(gp):
                        for s2 in range(2):
                            gp.dma_start(
                                out=wkva_sb[:].rearrange("p (s n) -> p s n", s=32)[:, s2 * 16:(s2 + 1) * 16, :],
                                in_=w_kva.rearrange("(s p) n -> p s n", p=128)[:, s2 * 16:(s2 + 1) * 16, :],
                            ).then_inc(WLD, 16)
                    wld_n[0] += 32

                    @block.vector
                    def _(v):
                        v.memset(krT[64:65, :], -1.0).then_inc(VKR, 1)

                    @block.sync
                    def _(sync):
                        sync.dma_start(out=gkva_b[:], in_=g_kva.broadcast_to((128, KVR))).then_inc(CLD, 16)
                        for t in range(NT2):
                            par = t % 2
                            sl = [LA, LB][par]
                            cl = [LC, LD2][par]
                            if t >= 2:
                                sync.wait_ge(VXC2, t - 1)
                                sync.wait_ge(SQ2, t - 1)
                            sync.dma_start(out=xt[par][:], in_=x_full[t * 128:(t + 1) * 128, :]).then_inc(sl, 16)
                            if t >= 2:
                                sync.wait_ge(VKR, t)
                            sync.dma_start(out=ckb[par][:], in_=cosk[t * 128:(t + 1) * 128, :]).then_inc(cl, 16)
                            sync.dma_start(out=skb[par][:], in_=sink[t * 128:(t + 1) * 128, :]).then_inc(cl, 16)
                    cld_n[0] += 16

                    XB1 = 16 * 8
                    XB2 = 0

                    @block.scalar
                    def _(s):
                        s.wait_ge(CLD, cld_n[0])
                        s.wait_ge(WLD, wld_n[0])

                        def kcr_evac(tm):
                            s.wait_ge(MKT, tm + 1)
                            for j4 in range(4):
                                s.copy(kcT[:, j4 * S + tm * 128:j4 * S + (tm + 1) * 128],
                                       p_t2[0][:, j4 * 128:(j4 + 1) * 128])
                            s.copy(krT[0:64, tm * 128:(tm + 1) * 128],
                                   p_t2[0][0:64, 512:640]).then_inc(EKT, 1)

                        for t in range(NT2):
                            par = t % 2
                            sl = [LA, LB][par]
                            base = [XB1, XB2][par]
                            par2 = par * 22
                            if t >= 2:
                                kcr_evac(t - 2)
                            s.wait_ge(sl, base + 16 * (t // 2 + 1))
                            sfence(s, s.activation(junk[:], xt[par][:, 0:2048], AF.Square,
                                         accum_out=stats[:, 6 + par2:7 + par2]))
                            s.activation(junk[:], xt[par][:, 2048:4096], AF.Square,
                                         accum_out=stats[:, 7 + par2:8 + par2]).then_inc(SQ2, 1)
                            s.wait_ge(VVAR2, t + 1)
                            sfence(s, s.activation(stats[:, 12:13], stats[:, 10 + par:11 + par], AF.Ln, bias=eps_sb[:]))
                            s.activation(stats[:, 14 + par:15 + par], stats[:, 12:13], AF.Exp, scale=-0.5)
                            if t >= 2:
                                s.wait_ge(MKV, t - 1)
                            for g in range(4):
                                s.wait_ge(MTR2, t * 4 + g + 1)
                                s.copy(xcT[par][:, g * 1024:(g + 1) * 1024], p_t2[g % 2][:]).then_inc(EXT2, 1)
                            s.wait_ge(MKV, t + 1)
                            if t >= 1:
                                s.wait_ge(VKC, t)
                                s.wait_ge(VKR, t + 1)
                            s.activation(kvln[:], p_akv[:, 0:KVR + RD], AF.Copy,
                                         scale=stats[:, 14 + par:15 + par]).then_inc(EKL, 1)
                            s.wait_ge(EKL, t + 1)
                            sfence(s, s.activation(junk[:, 0:KVR], kvln[:, 0:KVR], AF.Square, accum_out=stats[:, 22:23]))
                            sfence(s, s.activation(stats[:, 24:25], stats[:, 22:23], AF.Ln, scale=1.0 / KVR))
                            s.activation(stats[:, 16:17], stats[:, 24:25], AF.Exp, scale=0.5).then_inc(SUK, 1)
                            if t == NT2 - 1:
                                kcr_evac(NT2 - 2)
                                kcr_evac(NT2 - 1)

                    @block.vector
                    def _(v):
                        for t in range(NT2):
                            par = t % 2
                            sl = [LA, LB][par]
                            cl = [LC, LD2][par]
                            base = [XB1, XB2][par]
                            v.wait_ge(sl, base + 16 * (t // 2 + 1))
                            vfence(v, v.reduce_sum(stats[:, 0:1], xt[par][:], axis=AX.X, negate=True))
                            vfence(v, v.tensor_scalar(stats[:, 2 + par:3 + par], stats[:, 0:1], 1.0 / D, None, ALU.mult))
                            if t >= 2:
                                v.wait_ge(MTR2, (t - 2) * 4 + 4)
                            v.tensor_scalar(xcb[par][:], xt[par][:], stats[:, 2 + par:3 + par],
                                            None, ALU.add).then_inc(VXC2, 1)
                            v.tensor_mul(stats[:, 4:5], stats[:, 2 + par:3 + par], stats[:, 2 + par:3 + par])
                            par2 = par * 22
                            v.wait_ge(SQ2, t + 1)
                            vfence(v, v.tensor_add(stats[:, 5:6], stats[:, 6 + par2:7 + par2], stats[:, 7 + par2:8 + par2]))
                            vfence(v, v.tensor_scalar(stats[:, 8:9], stats[:, 5:6], 1.0 / D, None, ALU.mult))
                            v.tensor_sub(stats[:, 10 + par:11 + par], stats[:, 8:9], stats[:, 4:5]).then_inc(VVAR2, 1)
                            v.wait_ge(SUK, t + 1)
                            vfence(v, v.tensor_scalar(stats[:, 20:21], stats[:, 16:17], RMS_EPS, None, ALU.add))
                            vfence(v, v.reciprocal(stats[:, 18:19], stats[:, 20:21]))
                            if t >= 2:
                                v.wait_ge(MKT, t - 1)
                            v.scalar_tensor_tensor(kcb[par][:], kvln[:, 0:KVR], stats[:, 18:19], gkva_b[:],
                                                   ALU.mult, ALU.mult).then_inc(VKC, 1)
                            v.wait_ge(cl, 32 * (t // 2 + 1))
                            if t >= 1:
                                v.wait_ge(MKT, t)
                            v.tensor_scalar(krt1[:, 0:32], kvln[:, KVR + 32:KVR + 64], -1.0, None, ALU.mult)
                            vfence(v, v.tensor_copy(krt1[:, 32:64], kvln[:, KVR:KVR + 32]))
                            v.tensor_mul(krt2[:], kvln[:, KVR:KVR + RD], ckb[par][:])
                            vfence(v, v.tensor_mul(krt1[:], krt1[:], skb[par][:]))
                            v.tensor_add(krf[:], krt2[:], krt1[:]).then_inc(VKR, 1)

                    @block.tensor
                    def _(tn):
                        tn.wait_ge(CLD, cld_n[0])
                        tn.wait_ge(WLD, wld_n[0])

                        def kcr_transposes(t):
                            par = t % 2
                            tn.wait_ge(VKC, t + 1)
                            tn.wait_ge(VKR, t + 2)
                            tn.wait_ge(EXT2, min((t + 1) * 4 + 3, NT2 * 4))
                            if t >= 1:
                                tn.wait_ge(EKT, t)
                            for j4 in range(4):
                                tn.matmul(p_t2[0][:, j4 * 128:(j4 + 1) * 128],
                                          kcb[par][:, j4 * 128:(j4 + 1) * 128], ident_sb[:],
                                          start=True, stop=True)
                            mm = tn.matmul(p_t2[0][0:64, 512:640], krf[:], ident_sb[:],
                                           start=True, stop=True)
                            mm.then_inc(MKT, 1)

                        for t in range(NT2):
                            par = t % 2
                            tn.wait_ge(VXC2, t + 1)
                            for g in range(4):
                                if g >= 2:
                                    tn.wait_ge(EXT2, t * 4 + g - 1)
                                elif t >= 1:
                                    if g == 0:
                                        if t >= 2:
                                            tn.wait_ge(EKT, t - 1)
                                    else:
                                        tn.wait_ge(EXT2, (t - 1) * 4 + 4)
                                for j8 in range(8):
                                    j = g * 8 + j8
                                    mm = tn.matmul(p_t2[g % 2][:, j8 * 128:(j8 + 1) * 128],
                                                   xcb[par][:, j * 128:(j + 1) * 128], ident_sb[:],
                                                   start=True, stop=True)
                                mm.then_inc(MTR2, 1)
                            tn.wait_ge(EXT2, t * 4 + 4)
                            if t >= 1:
                                tn.wait_ge(EKL, t)
                            for j in range(32):
                                for nch in range(2):
                                    lo = nch * 512
                                    wdt = 512 if nch == 0 else 64
                                    mm = tn.matmul(p_akv[:, lo:lo + wdt],
                                                   xcT[par][:, j * 128:(j + 1) * 128],
                                                   wkva_sb[:, j * (KVR + RD) + lo:j * (KVR + RD) + lo + wdt],
                                                   start=(j == 0), stop=(j == 31))
                            mm.then_inc(MKV, 1)
                            if t >= 1:
                                kcr_transposes(t - 1)
                        kcr_transposes(NT2 - 1)

                    barrier()

            if taps:
                @block.sync
                def _(sync):
                    sync.wait_ge(QSP, 16)
                    sync.dma_start(out=tap_aps["t_qcT"], in_=d_qcT).then_inc(SG, 16)
                    sync.dma_start(out=tap_aps["t_kcT"], in_=kcT[:]).then_inc(SG, 16)
                    sync.dma_start(out=tap_aps["t_krT"], in_=krT[:]).then_inc(SG, 16)
                    sync.wait_ge(SG, 48)
                barrier()
            if max_phase <= 1:
                return nc

            with ExitStack() as ctxT:
                ET = ctxT.enter_context
                qcT2 = ET(nc.sbuf_tensor("qcT2", [128, 12 * R], BF16))
                wqb_sb = [ET(nc.sbuf_tensor(f"wqb{i}", [128, 12 * QK], BF16)) for i in range(2)]
                wkvb_sb = [ET(nc.sbuf_tensor(f"wkvb{i}", [128, 4 * (ND + VD)], BF16)) for i in range(2)]
                cqT = ET(nc.sbuf_tensor("cqT", [RD, R], F32))
                sqT = ET(nc.sbuf_tensor("sqT", [RD, R], F32))
                qT_nope = ET(nc.sbuf_tensor("qT_nope", [128, R], BF16))
                qT_s2 = ET(nc.sbuf_tensor("qT_s2", [65, R], BF16))
                qrr = ET(nc.sbuf_tensor("qrr", [RD, R], F32))
                rt1 = ET(nc.sbuf_tensor("rt1", [RD, R], F32))
                kT_nope = ET(nc.sbuf_tensor("kT_nope", [128, S], BF16))
                Vb = ET(nc.sbuf_tensor("Vb", [128, 64 * VD], BF16))
                sqb = [ET(nc.sbuf_tensor(f"sqb{i}", [128, 512], BF16)) for i in range(2)]
                sqb2 = [ET(nc.sbuf_tensor(f"sqb2{i}", [65, 512], BF16)) for i in range(2)]
                sqq0 = ET(nc.sbuf_tensor("sqq0", [128, 512], BF16))
                sqq1 = ET(nc.sbuf_tensor("sqq1", [64, 512], BF16))
                PT = ET(nc.sbuf_tensor("PT", [128, 4 * 1024], BF16))
                maxb = ET(nc.sbuf_tensor("maxb", [128, 18], F32))
                lkv = ET(nc.sbuf_tensor("lkv", [1, 4], F32))
                nqln = ET(nc.sbuf_tensor("nqln", [1, 1024], F32))
                lacc = ET(nc.sbuf_tensor("lacc", [128, 1024], F32))
                ptp = ET(nc.sbuf_tensor("ptp", [128, 1024], BF16))
                rlb = ET(nc.sbuf_tensor("rlb", [128, 1024], F32))
                OT_h = ET(nc.sbuf_tensor("OT_h", [128, R], BF16))
                p_sp = [ET(nc.psum_tensor(f"p_sp{i}", [128, 1024], F32)) for i in range(2)]
                p_pvT = ET(nc.psum_tensor("p_pvT", [128, 1024], F32))
                p_gen = [ET(nc.psum_tensor(f"p_gen{i}", [128, 512], F32)) for i in range(2)]

                AKGM = sem("a_kgm")
                AKGE = sem("a_kge")
                ASQK = sem("a_sqk")
                ANKM = sem("a_nkm")
                ANKR = sem("a_nkr")
                AMK = sem("a_mk")
                ALK = sem("a_lk")
                ALKH = sem("a_lkh")
                AVGM = sem("a_vgm")
                AVGE = sem("a_vge")
                AQGM = sem("a_qgm")
                AQGE = sem("a_qge")
                AQRM = sem("a_qrm")
                AQRE = sem("a_qre")
                AROP = sem("a_rop")
                ASQQ = sem("a_sqq")
                ANQM = sem("a_nqm")
                AMRW = sem("a_mrw")
                AEVQ = sem("a_evq")
                ASP = sem("a_sp")
                AEXP = sem("a_exp")
                APV = sem("a_pv")
                ALC = sem("a_lc")
                ABC = sem("a_bc")
                AOTE = sem("a_ote")

                @block.sync
                def _(sync):
                    sync.dma_start(out=cqT[:], in_=cosqT).then_inc(CLD, 16)
                    sync.dma_start(out=sqT[:], in_=sinqT).then_inc(CLD, 16)
                    sync.wait_ge(QSP, 16)
                    sync.dma_start(out=qcT2[:], in_=d_qcT).then_inc(CLD, 16)
                cld_n[0] += 48

                @block.gpsimd
                def _(gp):
                    for h in range(nheads):
                        par = h % 2
                        sl = [LE, LF][par]
                        if h >= 2:
                            gp.wait_ge(AQRM, (h - 2) * 2 + 2)
                            gp.wait_ge(AVGM, (h - 2) * 64 + 64)
                        gp.dma_start(
                            out=wqb_sb[par][:].rearrange("p (s n) -> p s n", s=12),
                            in_=w_qb[:, h * QK:(h + 1) * QK].rearrange("(s p) n -> p s n", p=128),
                        ).then_inc(sl, 16)
                        gp.dma_start(
                            out=wkvb_sb[par][:].rearrange("p (s n) -> p s n", s=4),
                            in_=w_kvb[:, h * (ND + VD):(h + 1) * (ND + VD)].rearrange("(s p) n -> p s n", p=128),
                        ).then_inc(sl, 16)

                @block.tensor
                def _(tn):
                    tn.wait_ge(CLD, cld_n[0])
                    tn.wait_ge(WLD, wld_n[0])
                    for h in range(nheads):
                        par = h % 2
                        sl = [LE, LF][par]
                        tn.wait_ge(sl, 32 * (h // 2 + 1))
                        hb16 = h * 16
                        hb64 = h * 64
                        hb2 = h * 2
                        for c in range(16):
                            gb = p_gen[c % 2]
                            if c >= 2:
                                tn.wait_ge(AKGE, hb16 + c - 1)
                            elif h >= 1:
                                tn.wait_ge(AVGE, (h - 1) * 64 + 63 + c)
                            for j in range(4):
                                mm = tn.matmul(gb[:],
                                               wkvb_sb[par][:, j * (ND + VD):j * (ND + VD) + ND],
                                               kcT[:, j * S + c * 512:j * S + c * 512 + 512],
                                               start=(j == 0), stop=(j == 3))
                            mm.then_inc(AKGM, 1)
                        tn.wait_ge(AKGE, hb16 + 16)
                        for j in range(12):
                            for qc in range(2):
                                mm = tn.matmul(p_gen[qc][:],
                                               wqb_sb[par][:, j * QK:j * QK + ND],
                                               qcT2[:, j * R + qc * 512:j * R + qc * 512 + 512],
                                               start=(j == 0), stop=(j == 11))
                                if j == 11:
                                    mm.then_inc(AQGM, 1)
                        tn.wait_ge(AQGE, hb2 + 2)
                        for j in range(12):
                            for qc in range(2):
                                mm = tn.matmul(p_gen[qc][0:64, :],
                                               wqb_sb[par][:, j * QK + ND:j * QK + QK],
                                               qcT2[:, j * R + qc * 512:j * R + qc * 512 + 512],
                                               start=(j == 0), stop=(j == 11))
                                if j == 11:
                                    mm.then_inc(AQRM, 1)
                        for qc in range(2):
                            gb = p_gen[qc % 2]
                            tn.wait_ge(ASQQ, h * 4 + qc * 2 + 2)
                            tn.wait_ge(AQRE, hb2 + qc + 1)
                            mm = tn.matmul(gb[:], ones_sb[:], sqq0[:], start=True, stop=False)
                            mm = tn.matmul(gb[:], ones_sb[0:64, :], sqq1[0:64, :], start=False, stop=True)
                            mm.then_inc(ANQM, 1)
                        for c in range(16):
                            gb = p_gen[c % 2]
                            tn.wait_ge(ASQK, hb16 + c + 1)
                            if c >= 2:
                                tn.wait_ge(ANKR, hb16 + c - 1)
                            else:
                                tn.wait_ge(AEVQ, hb2 + c + 1)
                            mm = tn.matmul(gb[:], ones_sb[:], sqb[c % 2][:], start=True, stop=False)
                            mm = tn.matmul(gb[:], ones_sb[0:65, :], sqb2[c % 2][:],
                                           start=False, stop=True)
                            mm.then_inc(ANKM, 1)
                        for kt in range(64):
                            gb = p_gen[kt % 2]
                            if kt >= 2:
                                tn.wait_ge(AVGE, hb64 + kt - 1)
                            else:
                                tn.wait_ge(ANKR, hb16 + 15 + kt)
                            for j in range(4):
                                mm = tn.matmul(gb[:, 0:128],
                                               kcT[:, j * S + kt * 128:j * S + (kt + 1) * 128],
                                               wkvb_sb[par][:, j * (ND + VD) + ND:j * (ND + VD) + ND + VD],
                                               start=(j == 0), stop=(j == 3))
                            mm.then_inc(AVGM, 1)
                        def emit_pv(kt):
                            tn.wait_ge(AEXP, hb64 + kt + 1)
                            vs = Vb[:, kt * VD:(kt + 1) * VD]
                            slot = (kt % 4) * 1024
                            tn.matmul(p_pvT[:, 0:512], vs, PT[:, slot:slot + 512],
                                      start=(kt == 0), stop=(kt == 63))
                            mm = tn.matmul(p_pvT[:, 512:1024], vs, PT[:, slot + 512:slot + 1024],
                                           start=(kt == 0), stop=(kt == 63))
                            mm.then_inc(APV, 1)

                        for kt in range(64):
                            spb = p_sp[kt % 2]
                            if kt >= 2:
                                tn.wait_ge(AEXP, hb64 + kt - 1)
                            elif h >= 1:
                                tn.wait_ge(AOTE, h)
                            if kt == 0:
                                tn.wait_ge(AMRW, hb2 + 2)
                                tn.wait_ge(AROP, hb2 + 2)
                                tn.wait_ge(AKGE, hb16 + 16)
                                tn.wait_ge(AVGE, hb64 + 64)
                            ks = kT_nope[:, kt * 128:(kt + 1) * 128]
                            rs = krT[:, kt * 128:(kt + 1) * 128]
                            tn.matmul(spb[:, 0:512], ks, qT_nope[:, 0:512], start=True, stop=False)
                            tn.matmul(spb[:, 512:1024], ks, qT_nope[:, 512:1024], start=True, stop=False)
                            tn.matmul(spb[:, 0:512], rs, qT_s2[:, 0:512], start=False, stop=True)
                            mm = tn.matmul(spb[:, 512:1024], rs, qT_s2[:, 512:1024], start=False, stop=True)
                            mm.then_inc(ASP, 1)
                            if kt >= 1:
                                emit_pv(kt - 1)
                        emit_pv(63)
                        tn.wait_ge(ALC, h * 32 + 32)
                        tn.wait_ge(AEXP, hb64 + 64)
                        tn.matmul(p_sp[0][:, 0:512], onesf2_sb[:], lacc[:, 0:512], start=True, stop=True)
                        mm = tn.matmul(p_sp[0][:, 512:1024], onesf2_sb[:], lacc[:, 512:1024], start=True, stop=True)
                        mm.then_inc(ABC, 1)

                @block.scalar
                def _(s):
                    for h in range(nheads):
                        hb16, hb64, hb2 = h * 16, h * 64, h * 2
                        for c in range(16):
                            s.wait_ge(AKGM, hb16 + c + 1)
                            s.copy(kT_nope[:, c * 512:(c + 1) * 512], p_gen[c % 2][:]).then_inc(AKGE, 1)
                        for qc in range(2):
                            s.wait_ge(AQGM, hb2 + qc + 1)
                            s.activation(qT_nope[:, qc * 512:qc * 512 + 512], p_gen[qc % 2][:],
                                         AF.Copy, scale=SC).then_inc(AQGE, 1)
                        for qc in range(2):
                            s.wait_ge(AQRM, hb2 + qc + 1)
                            s.activation(qrr[:, qc * 512:qc * 512 + 512], p_gen[qc % 2][0:64, :],
                                         AF.Copy, scale=SC).then_inc(AQRE, 1)
                        for qc in range(2):
                            s.wait_ge(ANQM, hb2 + qc + 1)
                            s.activation(nqln[0:1, qc * 512:qc * 512 + 512],
                                         p_gen[qc % 2][0:1, :], AF.Ln).then_inc(AEVQ, 1)
                        s.wait_ge(AMK, h + 1)
                        s.activation(lkv[0:1, 0:1], maxb[0:1, 16:17], AF.Ln).then_inc(ALK, 1)
                        for qc in range(2):
                            s.wait_ge(ALKH, h + 1)
                            s.wait_ge(AEVQ, hb2 + qc + 1)
                            s.activation(qT_s2[64:65, qc * 512:qc * 512 + 512],
                                         nqln[0:1, qc * 512:qc * 512 + 512], AF.Exp,
                                         scale=0.5, bias=lkv[0:1, 1:2]).then_inc(AMRW, 1)
                        hb32 = h * 32
                        for kt in range(64):
                            s.wait_ge(ASP, hb64 + kt + 1)
                            pvw = hb64 + kt - 3
                            if pvw >= 1:
                                s.wait_ge(APV, pvw)
                            lcw = hb32 + (kt - 4) // 2 + 1
                            if kt >= 4 or h >= 1:
                                s.wait_ge(ALC, lcw)
                            s.activation(PT[:, (kt % 4) * 1024:(kt % 4) * 1024 + 1024],
                                         p_sp[kt % 2][:], AF.Exp).then_inc(AEXP, 1)

                @block.vector
                def _(v):
                    for h in range(nheads):
                        hb16, hb64, hb2 = h * 16, h * 64, h * 2
                        for qc in range(2):
                            lo = qc * 512
                            v.wait_ge(AQRE, hb2 + qc + 1)
                            v.tensor_scalar(rt1[0:32, lo:lo + 512], qrr[32:64, lo:lo + 512], -1.0, None, ALU.mult)
                            vfence(v, v.tensor_copy(rt1[32:64, lo:lo + 512], qrr[0:32, lo:lo + 512]))
                            vfence(v, v.tensor_mul(qrr[:, lo:lo + 512], qrr[:, lo:lo + 512], cqT[:, lo:lo + 512]))
                            vfence(v, v.tensor_mul(rt1[:, lo:lo + 512], rt1[:, lo:lo + 512], sqT[:, lo:lo + 512]))
                            v.tensor_add(qT_s2[0:64, lo:lo + 512], qrr[:, lo:lo + 512],
                                         rt1[:, lo:lo + 512]).then_inc(AROP, 1)
                            v.wait_ge(AQGE, hb2 + qc + 1)
                            if h >= 1 or qc >= 1:
                                v.wait_ge(ANQM, hb2 + qc)
                            v.tensor_mul(sqq0[:], qT_nope[:, lo:lo + 512],
                                         qT_nope[:, lo:lo + 512]).then_inc(ASQQ, 1)
                            v.wait_ge(AROP, hb2 + qc + 1)
                            v.tensor_mul(sqq1[:], qT_s2[0:64, lo:lo + 512],
                                         qT_s2[0:64, lo:lo + 512]).then_inc(ASQQ, 1)
                        for c in range(16):
                            v.wait_ge(AKGE, hb16 + c + 1)
                            if c >= 2:
                                v.wait_ge(ANKM, hb16 + c - 1)
                            elif h >= 1:
                                v.wait_ge(ANKM, (h - 1) * 16 + 15 + c)
                            v.tensor_mul(sqb[c % 2][:], kT_nope[:, c * 512:(c + 1) * 512],
                                         kT_nope[:, c * 512:(c + 1) * 512])
                            v.tensor_mul(sqb2[c % 2][:], krT[:, c * 512:(c + 1) * 512],
                                         krT[:, c * 512:(c + 1) * 512]).then_inc(ASQK, 1)
                            if c >= 2:
                                v.wait_ge(ANKM, hb16 + c - 1)
                                v.reduce_max(maxb[:, c - 2:c - 1], p_gen[c % 2][:], axis=AX.X).then_inc(ANKR, 1)
                        for c in range(14, 16):
                            v.wait_ge(ANKM, hb16 + c + 1)
                            v.reduce_max(maxb[:, c:c + 1], p_gen[c % 2][:], axis=AX.X).then_inc(ANKR, 1)
                        v.wait_ge(ANKR, hb16 + 16)
                        v.reduce_max(maxb[:, 16:17], maxb[:, 0:16], axis=AX.X).then_inc(AMK, 1)
                        v.wait_ge(ALK, h + 1)
                        v.tensor_scalar(lkv[0:1, 1:2], lkv[0:1, 0:1], 0.5, None, ALU.mult).then_inc(ALKH, 1)
                        for kt in range(64):
                            v.wait_ge(AVGM, hb64 + kt + 1)
                            v.tensor_copy(Vb[:, kt * VD:(kt + 1) * VD],
                                          p_gen[kt % 2][:, 0:128]).then_inc(AVGE, 1)
                        hb32 = h * 32
                        v.wait_ge(ABC, h)
                        vfence(v, v.memset(lacc[:], 0.0))
                        for p2 in range(32):
                            v.wait_ge(AEXP, hb64 + 2 * p2 + 2)
                            if p2 >= 1:
                                v.wait_ge(ALC, hb32 + p2)
                            s0 = (2 * p2 % 4) * 1024
                            s1 = ((2 * p2 + 1) % 4) * 1024
                            vfence(v, v.tensor_add(ptp[:], PT[:, s0:s0 + 1024], PT[:, s1:s1 + 1024]))
                            v.tensor_add(lacc[:], lacc[:], ptp[:]).then_inc(ALC, 1)
                        v.wait_ge(ABC, h + 1)
                        vfence(v, v.reciprocal(rlb[:], p_sp[0][:]))
                        v.wait_ge(APV, hb64 + 64)
                        if h >= 1:
                            v.wait_ge(SA, 16 * h)
                        v.tensor_mul(OT_h[:], p_pvT[:], rlb[:]).then_inc(AOTE, 1)

                @block.sync
                def _(sync):
                    for h in range(nheads):
                        sync.wait_ge(AOTE, h + 1)
                        sync.dma_start(out=d_OT[h, :, :], in_=OT_h[:]).then_inc(SA, 16)

                if taps:
                    @block.sync
                    def _(sync):
                        sync.wait_ge(AKGE, 16)
                        sync.dma_start(out=tap_aps["t_kT0"], in_=kT_nope[:]).then_inc(SG, 16)
                        sync.wait_ge(AVGE, 64)
                        sync.dma_start(out=tap_aps["t_V0"], in_=Vb[:]).then_inc(SG, 16)
                        sync.wait_ge(AOTE, 1)
                        sync.dma_start(out=tap_aps["t_OT0"], in_=OT_h[:]).then_inc(SG, 16)

                barrier()
                if max_phase <= 2:
                    return nc

        with ExitStack() as ctxLF:
            ELF = ctxLF.enter_context
            ln2T = ELF(nc.sbuf_tensor("ln2T", [128, 32 * R], BF16))
            sums = ELF(nc.sbuf_tensor("sums", [128, 8], F32))
            ssqs = ELF(nc.sbuf_tensor("ssqs", [128, 8], F32))

            with ExitStack() as ctxP:
                EP = ctxP.enter_context
                ctxPP = ExitStack()
                EPP = ctxPP.enter_context
                OT_all = EPP(nc.sbuf_tensor("OT_all", [128, 32 * R], BF16))
                wo_sb = [EPP(nc.sbuf_tensor(f"wo{i}", [128, 32 * 512], BF16)) for i in range(2)]
                atile = [EPP(nc.sbuf_tensor(f"atile{i}", [128, 512], F32)) for i in range(2)]
                tstat = EPP(nc.sbuf_tensor("tstat", [128, 4], F32))
                p_pr = [EPP(nc.psum_tensor(f"p_pr{i}", [128, 512], F32)) for i in range(4)]

                PMM = sem("p_mm")
                PEV = sem("p_ev")
                PSQ = sem("p_sq")
                PST = sem("p_stat")

                @block.sync
                def _(sync):
                    sync.wait_ge(SA, 16 * nheads)
                    for s2 in range(2):
                        sync.dma_start(
                            out=OT_all[:].rearrange("p (s n) -> p s n", s=32)[:, s2 * 16:(s2 + 1) * 16, :],
                            in_=d_OT.rearrange("h p q -> p h q")[:, s2 * 16:(s2 + 1) * 16, :],
                        ).then_inc(CLD, 16)
                cld_n[0] += 32

                @block.gpsimd
                def _(gp):
                    gp.wait_ge(SA, 16 * nheads)
                    for dc in range(8):
                        par = dc % 2
                        sl = [LE, LF][par]
                        if dc >= 2:
                            gp.wait_ge(PMM, (dc - 2) * 8 + 8)
                        for s2 in range(2):
                            gp.dma_start(
                                out=wo_sb[par][:].rearrange("p (s n) -> p s n", s=32)[:, s2 * 16:(s2 + 1) * 16, :],
                                in_=w_o[:, dc * 512:(dc + 1) * 512].rearrange("(s p) n -> p s n", p=128)[:, s2 * 16:(s2 + 1) * 16, :],
                            ).then_inc(sl, 16)

                WOB = [32 * ((nheads + 1) // 2), 32 * (nheads // 2)]

                @block.tensor
                def _(tn):
                    tn.wait_ge(CLD, cld_n[0])
                    tn.wait_ge(WLD, wld_n[0])
                    for dc in range(8):
                        par = dc % 2
                        sl = [LE, LF][par]
                        tn.wait_ge(sl, WOB[par] + 32 * (dc // 2 + 1))
                        for qt in range(8):
                            idx = dc * 8 + qt
                            if idx >= 4:
                                tn.wait_ge(PEV, idx - 3)
                            for s32 in range(32):
                                mm = tn.matmul(p_pr[qt % 4][:],
                                               OT_all[:, s32 * R + qt * 128:s32 * R + (qt + 1) * 128],
                                               wo_sb[par][:, s32 * 512:(s32 + 1) * 512],
                                               start=(s32 == 0), stop=(s32 == 31))
                            mm.then_inc(PMM, 1)

                @block.scalar
                def _(s):
                    for dc in range(8):
                        for qt in range(8):
                            idx = dc * 8 + qt
                            par = idx % 2
                            s.wait_ge(PMM, idx + 1)
                            if idx >= 2:
                                s.wait_ge([ST0, ST1][par], 16 * (idx // 2))
                            if idx >= 2:
                                s.wait_ge(PST, idx - 1)
                            s.activation(atile[par][:], p_pr[qt % 4][:], AF.Copy,
                                         accum_out=tstat[:, 2 * par:2 * par + 1]).then_inc(PEV, 1)
                            s.wait_ge(PEV, idx + 1)
                            s.activation(junk[:, par * 512:(par + 1) * 512], atile[par][:], AF.Square,
                                         accum_out=tstat[:, 2 * par + 1:2 * par + 2]).then_inc(PSQ, 1)

                @block.vector
                def _(v):
                    v.memset(sums[:], 0.0)
                    vfence(v, v.memset(ssqs[:], 0.0))
                    for dc in range(8):
                        for qt in range(8):
                            idx = dc * 8 + qt
                            par = idx % 2
                            v.wait_ge(PSQ, idx + 1)
                            vfence(v, v.tensor_add(sums[:, qt:qt + 1], sums[:, qt:qt + 1], tstat[:, 2 * par:2 * par + 1]))
                            v.tensor_add(ssqs[:, qt:qt + 1], ssqs[:, qt:qt + 1],
                                         tstat[:, 2 * par + 1:2 * par + 2]).then_inc(PST, 1)

                @block.sync
                def _(sync):
                    for dc in range(8):
                        for qt in range(8):
                            idx = dc * 8 + qt
                            par = idx % 2
                            sync.wait_ge(PSQ, idx + 1)
                            sync.dma_start(out=d_attn[qt * 128:(qt + 1) * 128, dc * 512:(dc + 1) * 512],
                                           in_=atile[par][:]).then_inc([ST0, ST1][par], 16)

                barrier()

                if taps:
                    @block.sync
                    def _(sync):
                        sync.wait_ge(ST0, 16 * 32)
                        sync.wait_ge(ST1, 16 * 32)
                        sync.dma_start(out=tap_aps["t_attn"], in_=d_attn).then_inc(SG, 16)
                    barrier()
                if max_phase <= 2.5:
                    ctxPP.close()
                    return nc

                ctxPP.close()
                xrt = [EP(nc.sbuf_tensor(f"xrt{i}", [128, D], F32)) for i in range(2)]
                art = [EP(nc.sbuf_tensor(f"art{i}", [128, D], F32)) for i in range(2)]
                x2t = EP(nc.sbuf_tensor("x2t", [128, D], F32))
                l2f = EP(nc.sbuf_tensor("l2f", [128, D], F32))
                l2bf = EP(nc.sbuf_tensor("l2bf", [128, D], BF16))
                p_t3 = [EP(nc.psum_tensor(f"p_t3{i}", [128, 1024], F32)) for i in range(2)]

                RVAR = sem("r_var")
                RRS = sem("r_rs")
                RX2 = sem("r_x2")
                RSQ = sem("r_sq")
                RVB = sem("r_vb")
                RRS2 = sem("r_rs2")
                RL2 = sem("r_l2")
                RMM = sem("r_mm")
                REV = sem("r_ev")

                XRB = [40 * 16, 32 * 16]
                X2SB = 16 * 32

                @block.sync
                def _(sync):
                    sync.wait_ge(ST0, 16 * 32)
                    sync.wait_ge(ST1, 16 * 32)
                    for qt in range(8):
                        par = qt % 2
                        sl = [LA, LB][par]
                        if qt >= 2:
                            sync.wait_ge(RL2, qt - 1)
                        sync.dma_start(out=xrt[par][:], in_=x_loc[qt * 128:(qt + 1) * 128, :]).then_inc(sl, 16)
                        sync.dma_start(out=art[par][:], in_=d_attn[qt * 128:(qt + 1) * 128, :]).then_inc(sl, 16)
                        if qt >= 1:
                            sync.wait_ge(RX2, qt)
                            sync.dma_start(out=d_x2[(qt - 1) * 128:qt * 128, :], in_=x2t[:]).then_inc(ST0, 16)
                    sync.wait_ge(RX2, 8)
                    sync.dma_start(out=d_x2[7 * 128:8 * 128, :], in_=x2t[:]).then_inc(ST0, 16)
                    sync.wait_ge(ST0, X2SB + 16 * 8)

                @block.vector
                def _(v):
                    for qt in range(8):
                        par = qt % 2
                        sl = [LA, LB][par]
                        v.wait_ge(sl, XRB[par] + 32 * (qt // 2 + 1))
                        v.tensor_scalar(stats[:, 0:1], sums[:, qt:qt + 1], -1.0 / D, None, ALU.mult)
                        vfence(v, v.tensor_scalar(stats[:, 1:2], ssqs[:, qt:qt + 1], 1.0 / D, None, ALU.mult))
                        vfence(v, v.tensor_mul(stats[:, 2:3], stats[:, 0:1], stats[:, 0:1]))
                        v.tensor_sub(stats[:, 3:4], stats[:, 1:2], stats[:, 2:3]).then_inc(RVAR, 1)
                        v.wait_ge(RRS, qt + 1)
                        vfence(v, v.tensor_scalar(art[par][:], art[par][:], stats[:, 0:1], None, ALU.add))
                        vfence(v, v.tensor_scalar(art[par][:], art[par][:], stats[:, 4:5], None, ALU.mult))
                        if qt >= 1:
                            v.wait_ge(RSQ, qt)
                            v.wait_ge(ST0, X2SB + 16 * qt)
                        v.tensor_add(x2t[:], xrt[par][:], art[par][:]).then_inc(RX2, 1)
                        v.wait_ge(RX2, qt + 1)
                        vfence(v, v.reduce_sum(stats[:, 5:6], x2t[:], axis=AX.X, negate=True))
                        vfence(v, v.tensor_scalar(stats[:, 6:7], stats[:, 5:6], 1.0 / D, None, ALU.mult))
                        v.tensor_mul(stats[:, 7:8], stats[:, 6:7], stats[:, 6:7])
                        v.wait_ge(RSQ, qt + 1)
                        vfence(v, v.tensor_add(stats[:, 23:24], stats[:, 9:10], stats[:, 22:23]))
                        vfence(v, v.tensor_scalar(stats[:, 8:9], stats[:, 23:24], 1.0 / D, None, ALU.mult))
                        v.tensor_sub(stats[:, 10:11], stats[:, 8:9], stats[:, 7:8]).then_inc(RVB, 1)
                        v.wait_ge(RRS2, qt + 1)
                        vfence(v, v.tensor_scalar(l2f[:], x2t[:], stats[:, 6:7], None, ALU.add))
                        if qt >= 1:
                            v.wait_ge(RMM, 4 * qt)
                        v.tensor_scalar(l2bf[:], l2f[:], stats[:, 11:12], None, ALU.mult).then_inc(RL2, 1)

                @block.scalar
                def _(s):
                    for qt in range(8):
                        s.wait_ge(RVAR, qt + 1)
                        sfence(s, s.activation(stats[:, 20:21], stats[:, 3:4], AF.Ln, bias=eps_sb[:]))
                        s.activation(stats[:, 4:5], stats[:, 20:21], AF.Exp, scale=-0.5).then_inc(RRS, 1)
                        s.wait_ge(RX2, qt + 1)
                        sfence(s, s.activation(junk[:], x2t[:, 0:2048], AF.Square, accum_out=stats[:, 9:10]))
                        s.activation(junk[:], x2t[:, 2048:4096], AF.Square, accum_out=stats[:, 22:23]).then_inc(RSQ, 1)
                        s.wait_ge(RVB, qt + 1)
                        sfence(s, s.activation(stats[:, 21:22], stats[:, 10:11], AF.Ln, bias=eps_sb[:]))
                        s.activation(stats[:, 11:12], stats[:, 21:22], AF.Exp, scale=-0.5).then_inc(RRS2, 1)
                        for g in range(4):
                            s.wait_ge(RMM, qt * 4 + g + 1)
                            for j8 in range(8):
                                j = g * 8 + j8
                                last = s.copy(ln2T[:, j * R + qt * 128:j * R + (qt + 1) * 128],
                                              p_t3[g % 2][:, j8 * 128:(j8 + 1) * 128])
                            last.then_inc(REV, 1)

                @block.tensor
                def _(tn):
                    for qt in range(8):
                        tn.wait_ge(RL2, qt + 1)
                        for g in range(4):
                            if g >= 2:
                                tn.wait_ge(REV, qt * 4 + g - 1)
                            elif qt >= 1:
                                tn.wait_ge(REV, (qt - 1) * 4 + g + 3)
                            for j8 in range(8):
                                j = g * 8 + j8
                                mm = tn.matmul(p_t3[g % 2][:, j8 * 128:(j8 + 1) * 128],
                                               l2bf[:, j * 128:(j + 1) * 128], ident_sb[:],
                                               start=True, stop=True)
                            mm.then_inc(RMM, 1)

                if taps:
                    @block.sync
                    def _(sync):
                        sync.wait_ge(ST0, X2SB + 16 * 8)
                        sync.dma_start(out=tap_aps["t_x2"], in_=d_x2).then_inc(SG, 16)

                barrier()
                if max_phase <= 3:
                    return nc

            with ExitStack() as ctxF1:
                EF1 = ctxF1.enter_context
                w1_sb = [EF1(nc.sbuf_tensor(f"w1sb{i}", [128, 32 * 512], BF16)) for i in range(2)]
                b1_sb = EF1(nc.sbuf_tensor("b1_sb", [128, 128], F32))
                hT_sb = [EF1(nc.sbuf_tensor(f"hTsb{i}", [128, 1024], BF16)) for i in range(2)]
                p_h = [EF1(nc.psum_tensor(f"p_h{i}", [128, 1024], F32)) for i in range(2)]

                FMM = sem("f_mm")
                FEV = sem("f_ev")
                FST = sem("f_st")

                @block.sync
                def _(sync):
                    sync.dma_start(out=b1_sb[:], in_=b1r).then_inc(CLD, 16)
                cld_n[0] += 16

                WOB2 = [WOB[0] + 32 * 4, WOB[1] + 32 * 4]

                @block.gpsimd
                def _(gp):
                    gp.wait_ge(ST0, 16 * 40)
                    gp.wait_ge(ST1, 16 * 32)
                    for fs in range(32):
                        par = fs % 2
                        sl = [LE, LF][par]
                        if fs >= 2:
                            gp.wait_ge(FMM, (fs - 2) * 4 + 4)
                        for s2 in range(2):
                            gp.dma_start(
                                out=w1_sb[par][:].rearrange("p (s n) -> p s n", s=32)[:, s2 * 16:(s2 + 1) * 16, :],
                                in_=w1[:, fs * 512:(fs + 1) * 512].rearrange("(s p) n -> p s n", p=128)[:, s2 * 16:(s2 + 1) * 16, :],
                            ).then_inc(sl, 16)

                @block.tensor
                def _(tn):
                    tn.wait_ge(CLD, cld_n[0])
                    tn.wait_ge(WLD, wld_n[0])
                    for fs in range(32):
                        par = fs % 2
                        sl = [LE, LF][par]
                        tn.wait_ge(sl, WOB2[par] + 32 * (fs // 2 + 1))
                        for f4 in range(4):
                            key = fs * 4 + f4
                            if key >= 2:
                                tn.wait_ge(FEV, key - 1)
                            for j in range(32):
                                for qc in range(2):
                                    mm = tn.matmul(p_h[f4 % 2][:, qc * 512:qc * 512 + 512],
                                                   w1_sb[par][:, j * 512 + f4 * 128:j * 512 + (f4 + 1) * 128],
                                                   ln2T[:, j * R + qc * 512:j * R + qc * 512 + 512],
                                                   start=(j == 0), stop=(j == 31))
                            mm.then_inc(FMM, 1)

                HSB = [16 * 40, 16 * 32]

                @block.scalar
                def _(s):
                    for key in range(128):
                        par = key % 2
                        s.wait_ge(FMM, key + 1)
                        if key >= 2:
                            s.wait_ge([ST0, ST1][par], HSB[par] + 16 * (key // 2))
                        s.activation(hT_sb[par][:], p_h[key % 2][:], AF.Gelu_apprx_tanh,
                                     bias=b1_sb[:, key:key + 1]).then_inc(FEV, 1)

                @block.sync
                def _(sync):
                    for key in range(128):
                        par = key % 2
                        sync.wait_ge(FEV, key + 1)
                        sync.dma_start(out=d_hT[key * 128:(key + 1) * 128, :],
                                       in_=hT_sb[par][:]).then_inc([ST0, ST1][par], 16)

                if taps:
                    @block.sync
                    def _(sync):
                        sync.wait_ge(ST0, HSB[0] + 16 * 64)
                        sync.wait_ge(ST1, HSB[1] + 16 * 64)
                        sync.dma_start(out=tap_aps["t_hT0"], in_=d_hT[0:128, :]).then_inc(SG, 16)

                barrier()
                if max_phase <= 4:
                    return nc

            with ExitStack() as ctxF2:
                EF2 = ctxF2.enter_context
                w2_sb = [EF2(nc.sbuf_tensor(f"w2sb{i}", [128, 16 * 512], BF16)) for i in range(2)]
                hTp = [EF2(nc.sbuf_tensor(f"hTp{i}", [128, 16 * 1024], BF16)) for i in range(2)]
                b2_sb = EF2(nc.sbuf_tensor("b2_sb", [1, D], F32))
                o2t = [EF2(nc.sbuf_tensor(f"o2t{i}", [128, 512], F32)) for i in range(2)]
                tstat2 = EF2(nc.sbuf_tensor("tstat2", [128, 4], F32))
                p_s = [EF2(nc.psum_tensor(f"p_s{i}", [128, 512], F32)) for i in range(8)]

                GFB = sem("g_fb")
                GMM = sem("g_mm")
                GEV = sem("g_ev")
                GSQ = sem("g_sq")
                GST = sem("g_stat")

                W2B = [WOB2[0] + 32 * 16, WOB2[1] + 32 * 16]
                HLB = [32 * 32, 32 * 32]

                F2SB = [16 * (32 + 8 + 64), 16 * (32 + 64)]

                @block.sync
                def _(sync):
                    sync.dma_start(out=b2_sb[:], in_=b2r).then_inc(CLD, 16)
                    sync.wait_ge(ST0, F2SB[0])
                    sync.wait_ge(ST1, F2SB[1])
                    for dc in range(8):
                        if dc >= 1:
                            for qt in range(8):
                                idx = (dc - 1) * 8 + qt
                                par = idx % 2
                                sync.wait_ge(GSQ, idx + 1)
                                sync.dma_start(out=d_out2[qt * 128:(qt + 1) * 128, (dc - 1) * 512:dc * 512],
                                               in_=o2t[par][:]).then_inc([ST0, ST1][par], 16)
                        for fb in range(8):
                            key = dc * 8 + fb
                            par = key % 2
                            sl = [LC, LD2][par]
                            if key >= 2:
                                sync.wait_ge(GFB, key - 1)
                            sync.dma_start(
                                out=hTp[par][:].rearrange("p (a q) -> p a q", a=16),
                                in_=d_hT[fb * 2048:(fb + 1) * 2048, :].rearrange("(a p) q -> p a q", p=128),
                            ).then_inc(sl, 16)
                    for qt in range(8):
                        idx = 7 * 8 + qt
                        par = idx % 2
                        sync.wait_ge(GSQ, idx + 1)
                        sync.dma_start(out=d_out2[qt * 128:(qt + 1) * 128, 7 * 512:8 * 512],
                                       in_=o2t[par][:]).then_inc([ST0, ST1][par], 16)
                    sync.wait_ge(ST0, F2SB[0] + 16 * 32)
                    sync.wait_ge(ST1, F2SB[1] + 16 * 32)
                    if taps:
                        sync.dma_start(out=tap_aps["t_out2"], in_=d_out2).then_inc(SG, 16)
                cld_n[0] += 16

                @block.gpsimd
                def _(gp):
                    gp.wait_ge(ST0, F2SB[0])
                    gp.wait_ge(ST1, F2SB[1])
                    for dc in range(8):
                        for fb in range(8):
                            key = dc * 8 + fb
                            par = key % 2
                            sl = [LE, LF][par]
                            if key >= 2:
                                gp.wait_ge(GFB, key - 1)
                            gp.dma_start(
                                out=w2_sb[par][:].rearrange("p (s n) -> p s n", s=16),
                                in_=w2[fb * 2048:(fb + 1) * 2048, dc * 512:(dc + 1) * 512].rearrange("(s p) n -> p s n", p=128),
                            ).then_inc(sl, 16)

                @block.tensor
                def _(tn):
                    tn.wait_ge(CLD, cld_n[0])
                    tn.wait_ge(WLD, wld_n[0])
                    for dc in range(8):
                        for fb in range(8):
                            key = dc * 8 + fb
                            par = key % 2
                            tn.wait_ge([LE, LF][par], W2B[par] + 16 * (key // 2 + 1))
                            tn.wait_ge([LC, LD2][par], HLB[par] + 16 * (key // 2 + 1))
                            if fb == 0 and dc >= 1:
                                tn.wait_ge(GSQ, dc * 8)
                            for qt in range(8):
                                for j in range(16):
                                    mm = tn.matmul(p_s[qt][:],
                                                   hTp[par][:, j * 1024 + qt * 128:j * 1024 + (qt + 1) * 128],
                                                   w2_sb[par][:, j * 512:(j + 1) * 512],
                                                   start=(fb == 0 and j == 0), stop=False)
                            mm.then_inc(GFB, 1)
                        for qt in range(8):
                            mm = tn.matmul(p_s[qt][:], onesf_sb[:], b2_sb[:, dc * 512:(dc + 1) * 512],
                                           start=False, stop=True)
                            mm.then_inc(GMM, 1)

                @block.scalar
                def _(s):
                    for dc in range(8):
                        for qt in range(8):
                            idx = dc * 8 + qt
                            par = idx % 2
                            s.wait_ge(GMM, idx + 1)
                            if idx >= 2:
                                s.wait_ge([ST0, ST1][par], F2SB[par] + 16 * (idx // 2))
                            if idx >= 2:
                                s.wait_ge(GST, idx - 1)
                            s.activation(o2t[par][:], p_s[qt][:], AF.Copy,
                                         accum_out=tstat2[:, 2 * par:2 * par + 1]).then_inc(GEV, 1)
                            s.wait_ge(GEV, idx + 1)
                            s.activation(junk[:, par * 512:(par + 1) * 512], o2t[par][:], AF.Square,
                                         accum_out=tstat2[:, 2 * par + 1:2 * par + 2]).then_inc(GSQ, 1)

                @block.vector
                def _(v):
                    v.memset(sums[:], 0.0)
                    vfence(v, v.memset(ssqs[:], 0.0))
                    for dc in range(8):
                        for qt in range(8):
                            idx = dc * 8 + qt
                            par = idx % 2
                            v.wait_ge(GSQ, idx + 1)
                            vfence(v, v.tensor_add(sums[:, qt:qt + 1], sums[:, qt:qt + 1], tstat2[:, 2 * par:2 * par + 1]))
                            v.tensor_add(ssqs[:, qt:qt + 1], ssqs[:, qt:qt + 1],
                                         tstat2[:, 2 * par + 1:2 * par + 2]).then_inc(GST, 1)

                barrier()

            with ExitStack() as ctxF3:
                EF3 = ctxF3.enter_context
                x2r = [EF3(nc.sbuf_tensor(f"x2r{i}", [128, D], F32)) for i in range(2)]
                o2r = [EF3(nc.sbuf_tensor(f"o2r{i}", [128, D], F32)) for i in range(2)]
                xo = [EF3(nc.sbuf_tensor(f"xo{i}", [128, D], F32)) for i in range(2)]

                ZVAR = sem("z_var")
                ZRS = sem("z_rs")
                ZXO = sem("z_xo")

                ZB = [(8 + 32 + 4 * 2) * 16, (32 + 4 * 2) * 16]

                @block.sync
                def _(sync):
                    for qt in range(8):
                        par = qt % 2
                        sl = [LA, LB][par]
                        if qt >= 2:
                            sync.wait_ge(ZXO, qt - 1)
                        sync.dma_start(out=x2r[par][:], in_=d_x2[qt * 128:(qt + 1) * 128, :]).then_inc(sl, 16)
                        sync.dma_start(out=o2r[par][:], in_=d_out2[qt * 128:(qt + 1) * 128, :]).then_inc(sl, 16)
                        if qt >= 2:
                            sync.dma_start(out=out_rows[(qt - 2) * 128:(qt - 1) * 128, :],
                                           in_=xo[(qt - 2) % 2][:]).then_inc([OST, OST1][qt % 2], 16)

                @block.vector
                def _(v):
                    for qt in range(8):
                        par = qt % 2
                        sl = [LA, LB][par]
                        v.wait_ge(sl, ZB[par] + 32 * (qt // 2 + 1))
                        v.tensor_scalar(stats[:, 0:1], sums[:, qt:qt + 1], -1.0 / D, None, ALU.mult)
                        vfence(v, v.tensor_scalar(stats[:, 1:2], ssqs[:, qt:qt + 1], 1.0 / D, None, ALU.mult))
                        vfence(v, v.tensor_mul(stats[:, 2:3], stats[:, 0:1], stats[:, 0:1]))
                        v.tensor_sub(stats[:, 3:4], stats[:, 1:2], stats[:, 2:3]).then_inc(ZVAR, 1)
                        v.wait_ge(ZRS, qt + 1)
                        vfence(v, v.tensor_scalar(o2r[par][:], o2r[par][:], stats[:, 0:1], None, ALU.add))
                        vfence(v, v.tensor_scalar(o2r[par][:], o2r[par][:], stats[:, 4:5], None, ALU.mult))
                        if qt >= 2:
                            v.wait_ge([OST, OST1][qt % 2], 16 * (qt // 2))
                        v.tensor_add(xo[par][:], x2r[par][:], o2r[par][:]).then_inc(ZXO, 1)

                @block.scalar
                def _(s):
                    for qt in range(8):
                        s.wait_ge(ZVAR, qt + 1)
                        sfence(s, s.activation(stats[:, 20:21], stats[:, 3:4], AF.Ln, bias=eps_sb[:]))
                        s.activation(stats[:, 4:5], stats[:, 20:21], AF.Exp, scale=-0.5).then_inc(ZRS, 1)

                @block.sync
                def _(sync):
                    for qt in range(6, 8):
                        par = qt % 2
                        sync.wait_ge(ZXO, qt + 1)
                        sync.dma_start(out=out_rows[qt * 128:(qt + 1) * 128, :],
                                       in_=xo[par][:]).then_inc([OST, OST1][qt % 2], 16)
                    sync.wait_ge(OST, 16 * 4)
                    sync.wait_ge(OST1, 16 * 4)
                    if taps:
                        sync.wait_ge(SG, 16 * 9)

    return nc


def _host_constants():
    inv_freq = 1.0 / (10000.0 ** (np.arange(0, RD, 2, dtype=np.float64) / RD))
    t = np.arange(S, dtype=np.float64)
    freqs = np.outer(t, inv_freq)
    emb = np.concatenate([freqs, freqs], axis=-1)
    cosk = np.cos(emb).astype(np.float32)
    sink = np.sin(emb).astype(np.float32)
    ident = np.eye(128, dtype=ml_dtypes.bfloat16)
    ones_c = np.ones((128, 128), dtype=ml_dtypes.bfloat16)
    return cosk, sink, ident, ones_c


def make_in_maps(inputs):
    x = np.ascontiguousarray(np.asarray(inputs["L0_attn_hidden_states"], dtype=np.float32).reshape(S, D))
    cosk, sink, ident, ones_c = _host_constants()
    common = {
        "x_full": x,
        "w_qa": np.ascontiguousarray(inputs["L0_attn_w_qa"], dtype=np.float32),
        "g_qa": np.ascontiguousarray(inputs["L0_attn_g_qa"], dtype=np.float32).reshape(1, QR),
        "w_qb": np.ascontiguousarray(inputs["L0_attn_w_qb"], dtype=np.float32),
        "w_kva": np.ascontiguousarray(inputs["L0_attn_w_kva"], dtype=np.float32),
        "g_kva": np.ascontiguousarray(inputs["L0_attn_g_kva"], dtype=np.float32).reshape(1, KVR),
        "w_kvb": np.ascontiguousarray(inputs["L0_attn_w_kvb"], dtype=np.float32),
        "w_o": np.ascontiguousarray(inputs["L0_attn_w_o"], dtype=np.float32),
        "w1": np.ascontiguousarray(inputs["L0_ffn_inter_w"], dtype=np.float32),
        "b1r": np.ascontiguousarray(np.asarray(inputs["L0_ffn_inter_b"], dtype=np.float32).reshape(128, 128).T),
        "w2": np.ascontiguousarray(inputs["L0_ffn_output_w"], dtype=np.float32),
        "b2r": np.ascontiguousarray(inputs["L0_ffn_output_b"], dtype=np.float32).reshape(1, D),
        "cosk": cosk,
        "sink": sink,
        "identc": ident,
        "epsc": np.full((128, 1), LN_EPS, np.float32),
        "onesfc": np.ones((1, 128), np.float32),
        "onesf2c": np.ones((128, 128), np.float32),
        "onesc": ones_c,
    }
    in_maps = []
    for c in range(NCORES):
        m = dict(common)
        m["x_loc"] = np.ascontiguousarray(x[c * R:(c + 1) * R, :])
        m["cosqT"] = np.ascontiguousarray(cosk[c * R:(c + 1) * R, :].T)
        m["sinqT"] = np.ascontiguousarray(sink[c * R:(c + 1) * R, :].T)
        in_maps.append(m)
    return in_maps


def kernel(**inputs):
    in_maps = make_in_maps(inputs)
    nc = build_nc(taps=False)
    res = run_bass_kernel_spmd(nc, in_maps, core_ids=list(range(NCORES)))
    out = np.concatenate([res.results[c]["out_rows"] for c in range(NCORES)], axis=0)
    return out.reshape(1, S, D).astype(np.float32)


# version 07
# speedup vs baseline: 1.4610x; 1.0377x over previous
import numpy as np
import ml_dtypes
from contextlib import ExitStack
import concourse.bass as bass
import concourse.mybir as mybir
from concourse.bass_utils import run_bass_kernel_spmd

F32 = mybir.dt.float32
BF16 = mybir.dt.bfloat16
AF = mybir.ActivationFunctionType
ALU = mybir.AluOpType
AX = mybir.AxisListType

S, D, R = 8192, 4096, 1024
NH, QR, KVR, RD, ND, VD = 32, 1536, 512, 64, 128, 128
QK = ND + RD
FF = 16384
SC = float(QK) ** -0.5
NCORES = 8
LN_EPS = 1e-5
RMS_EPS = 1e-6


def build_nc(taps=False, nheads=NH, max_phase=99):
    nc = bass.Bass("TRN2", target_bir_lowering=False)

    x_full = nc.dram_tensor("x_full", [S, D], F32, kind="ExternalInput").ap()
    x_loc = nc.dram_tensor("x_loc", [R, D], F32, kind="ExternalInput").ap()
    w_qa = nc.dram_tensor("w_qa", [D, QR], F32, kind="ExternalInput").ap()
    g_qa = nc.dram_tensor("g_qa", [1, QR], F32, kind="ExternalInput").ap()
    w_qb = nc.dram_tensor("w_qb", [QR, NH * QK], F32, kind="ExternalInput").ap()
    w_kva = nc.dram_tensor("w_kva", [D, KVR + RD], F32, kind="ExternalInput").ap()
    g_kva = nc.dram_tensor("g_kva", [1, KVR], F32, kind="ExternalInput").ap()
    w_kvb = nc.dram_tensor("w_kvb", [KVR, NH * (ND + VD)], F32, kind="ExternalInput").ap()
    w_o = nc.dram_tensor("w_o", [D, D], F32, kind="ExternalInput").ap()
    w1 = nc.dram_tensor("w1", [D, FF], F32, kind="ExternalInput").ap()
    b1r = nc.dram_tensor("b1r", [128, 128], F32, kind="ExternalInput").ap()
    w2 = nc.dram_tensor("w2", [FF, D], F32, kind="ExternalInput").ap()
    b2r = nc.dram_tensor("b2r", [1, D], F32, kind="ExternalInput").ap()
    cosk = nc.dram_tensor("cosk", [S, RD], F32, kind="ExternalInput").ap()
    sink = nc.dram_tensor("sink", [S, RD], F32, kind="ExternalInput").ap()
    cosqT = nc.dram_tensor("cosqT", [RD, R], F32, kind="ExternalInput").ap()
    sinqT = nc.dram_tensor("sinqT", [RD, R], F32, kind="ExternalInput").ap()
    identc = nc.dram_tensor("identc", [128, 128], BF16, kind="ExternalInput").ap()
    epsc = nc.dram_tensor("epsc", [128, 1], F32, kind="ExternalInput").ap()
    onesfc = nc.dram_tensor("onesfc", [1, 128], F32, kind="ExternalInput").ap()
    onesf2c = nc.dram_tensor("onesf2c", [128, 128], F32, kind="ExternalInput").ap()
    identfc = nc.dram_tensor("identfc", [128, 128], F32, kind="ExternalInput").ap()
    onesc = nc.dram_tensor("onesc", [128, 128], BF16, kind="ExternalInput").ap()

    out_rows = nc.dram_tensor("out_rows", [R, D], F32, kind="ExternalOutput").ap()

    d_OT = nc.dram_tensor("d_OT", [NH, 128, R], BF16, kind="Internal").ap()
    d_attn = nc.dram_tensor("d_attn", [R, D], F32, kind="Internal").ap()
    d_x2 = nc.dram_tensor("d_x2", [R, D], F32, kind="Internal").ap()
    d_hT = nc.dram_tensor("d_hT", [FF, R], BF16, kind="Internal").ap()
    d_out2 = nc.dram_tensor("d_out2", [R, D], F32, kind="Internal").ap()

    tap_aps = {}
    if taps:
        for name, shape, dt in [
            ("t_qcT", [128, 12 * R], BF16),
            ("t_kcT", [128, 4 * S], BF16),
            ("t_krT", [65, S], BF16),
            ("t_kT0", [128, S], BF16),
            ("t_V0", [128, 64 * VD], BF16),
            ("t_OT0", [128, R], BF16),
            ("t_x2", [R, D], F32),
            ("t_hT0", [128, R], BF16),
            ("t_attn", [R, D], F32),
            ("t_out2", [R, D], F32),
        ]:
            tap_aps[name] = nc.dram_tensor(name, shape, dt, kind="ExternalOutput").ap()

    with ExitStack() as ctx0:
        E0 = ctx0.enter_context

        sems = {}

        def sem(name):
            if name not in sems:
                sems[name] = E0(nc.semaphore(name))
            return sems[name]

        PH = sem("ph")
        CLD = sem("c_ld")
        OST = sem("o_st")
        OST1 = sem("o_st1")
        LA, LB = sem("la"), sem("lb")
        LC, LD2 = sem("lc"), sem("ld2")
        LE, LF = sem("le"), sem("lf")
        ST0, ST1 = sem("st0"), sem("st1")
        SA = sem("sa")
        SG = sem("sg")

        WLD = sem("w_ld")
        cld_n = [0]
        wld_n = [0]
        ph_n = [0]

        ident_sb = E0(nc.sbuf_tensor("ident_sb", [128, 128], BF16))
        eps_sb = E0(nc.sbuf_tensor("eps_sb", [128, 1], F32))
        onesf_sb = E0(nc.sbuf_tensor("onesf_sb", [1, 128], F32))
        onesf2_sb = E0(nc.sbuf_tensor("onesf2_sb", [128, 128], F32))
        identf_sb = E0(nc.sbuf_tensor("identf_sb", [128, 128], F32))
        ones_sb = E0(nc.sbuf_tensor("ones_sb", [128, 128], BF16))
        junk = E0(nc.sbuf_tensor("junk", [128, 2048], BF16))
        stats = E0(nc.sbuf_tensor("stats", [128, 32], F32))

        block = E0(nc.Block())
        ENG = ["tensor", "scalar", "vector", "sync", "gpsimd"]

        VFS = sem("v_fence")
        SFS = sem("s_fence")
        fence_n = {"v": 0, "s": 0}

        def vfence(v, inst):
            fence_n["v"] += 1
            inst.then_inc(VFS, 1)
            v.wait_ge(VFS, fence_n["v"])

        def sfence(s, inst):
            fence_n["s"] += 1
            inst.then_inc(SFS, 1)
            s.wait_ge(SFS, fence_n["s"])

        def barrier():
            target = ph_n[0] + len(ENG)

            @block.scalar
            def _(s):
                s.activation(stats[:, 30:31], eps_sb[:], AF.Copy).then_inc(PH, 1)
                s.wait_ge(PH, target)

            @block.vector
            def _(v):
                v.memset(stats[:, 31:32], 0.0).then_inc(PH, 1)
                v.wait_ge(PH, target)

            for name in ["tensor", "sync", "gpsimd"]:
                def mk():
                    def _(eng):
                        eng.sem_inc(PH, 1)
                        eng.wait_ge(PH, target)
                    return _
                getattr(block, name)(mk())
            ph_n[0] = target

        @block.sync
        def _(sync):
            sync.dma_start(out=ident_sb[:], in_=identc).then_inc(CLD, 16)
            sync.dma_start(out=ones_sb[:], in_=onesc).then_inc(CLD, 16)
            sync.dma_start(out=eps_sb[:], in_=epsc).then_inc(CLD, 16)
            sync.dma_start(out=onesf_sb[:], in_=onesfc).then_inc(CLD, 16)
            sync.dma_start(out=onesf2_sb[:], in_=onesf2c).then_inc(CLD, 16)
            sync.dma_start(out=identf_sb[:], in_=identfc).then_inc(CLD, 16)
        cld_n[0] += 96

        d_qcT = nc.dram_tensor("d_qcT", [128, 12 * R], BF16, kind="Internal").ap()
        QSP = sem("q_spill")
        with ExitStack() as ctxA:
            EA = ctxA.enter_context
            if True:
                with ExitStack() as ctxQ:
                    EQ = ctxQ.enter_context
                    qcT = EQ(nc.sbuf_tensor("qcT", [128, 12 * R], BF16))
                    xt = [EQ(nc.sbuf_tensor("xt0", [128, D], F32))] * 2
                    xcb = [EQ(nc.sbuf_tensor("xcb0", [128, D], BF16))] * 2
                    xcT = [EQ(nc.sbuf_tensor("xcT0", [128, D], BF16))] * 2
                    wqa_sb = EQ(nc.sbuf_tensor("wqa_sb", [128, 32 * QR], BF16))
                    gqa_b = EQ(nc.sbuf_tensor("gqa_b", [128, QR], F32))
                    qln = EQ(nc.sbuf_tensor("qln", [128, QR], F32))
                    qcb = [EQ(nc.sbuf_tensor("qcb0", [128, QR], BF16))] * 2
                    p_aq = EQ(nc.psum_tensor("p_aq", [128, QR], F32))
                    p_tr = [EQ(nc.psum_tensor(f"p_tr{i}", [128, 1024], F32)) for i in range(2)]
                    colqa = EQ(nc.sbuf_tensor("colqa", [1, QR], BF16))
                    mntq = EQ(nc.sbuf_tensor("mntq", [1, 128], BF16))
                    p_mq = EQ(nc.psum_tensor("p_mq", [1, 128], F32))

                    NT = 8
                    VMN1 = sem("q_vmn")
                    MMT1 = sem("q_mmt")
                    EMT1 = sem("q_emt")
                    SQ = sem("q_sq")
                    VXC = sem("q_xc")
                    VVAR = sem("q_var")
                    MTR = sem("q_mtr")
                    EXT = sem("q_ext")
                    MAQ = sem("q_maq")
                    EQL = sem("q_eql")
                    SUQ = sem("q_suq")
                    VQC = sem("q_vqc")
                    MQT = sem("q_mqt")
                    EQT = sem("q_eqt")
                    VSS = sem("q_vss")
                    SUQ2 = sem("q_suq2")

                    @block.gpsimd
                    def _(gp):
                        for s4 in range(4):
                            gp.dma_start(
                                out=wqa_sb[:].rearrange("p (s n) -> p s n", s=32)[:, s4 * 8:(s4 + 1) * 8, :],
                                in_=w_qa.rearrange("(s p) n -> p s n", p=128)[:, s4 * 8:(s4 + 1) * 8, :],
                            ).then_inc(WLD, 16)
                    wld_n[0] += 64

                    @block.sync
                    def _(sync):
                        sync.dma_start(out=gqa_b[:], in_=g_qa.broadcast_to((128, QR))).then_inc(CLD, 16)
                        for t in range(NT):
                            if t >= 1:
                                sync.wait_ge(VMN1, t)
                                sync.wait_ge(SQ, t)
                            sync.dma_start(out=xt[0][:], in_=x_loc[t * 128:(t + 1) * 128, :]).then_inc(LA, 16)
                        sync.wait_ge(EQT, 16)
                        sync.dma_start(out=d_qcT, in_=qcT[:]).then_inc(QSP, 16)
                        sync.wait_ge(QSP, 16)
                    cld_n[0] += 16

                    @block.scalar
                    def _(s):
                        s.wait_ge(CLD, cld_n[0])
                        s.wait_ge(WLD, wld_n[0])
                        s.wait_ge(MMT1, 1)
                        s.copy(colqa[:], p_aq[0:1, :]).then_inc(EMT1, 1)
                        for t in range(NT):
                            par = t % 2
                            par2 = (t % 2) * 22
                            s.wait_ge(LA, 16 * (t + 1))
                            sfence(s, s.activation(junk[:], xt[0][:, 0:2048], AF.Square,
                                         accum_out=stats[:, 6 + par2:7 + par2]))
                            s.activation(junk[:], xt[0][:, 2048:4096], AF.Square,
                                         accum_out=stats[:, 7 + par2:8 + par2]).then_inc(SQ, 1)
                            s.wait_ge(VVAR, t + 1)
                            sfence(s, s.activation(stats[:, 12:13], stats[:, 10 + par:11 + par], AF.Ln, bias=eps_sb[:]))
                            s.activation(stats[:, 14 + par:15 + par], stats[:, 12:13], AF.Exp, scale=-0.5)
                            if t >= 1:
                                s.wait_ge(MAQ, t)
                            for g in range(4):
                                s.wait_ge(MTR, t * 4 + g + 1)
                                s.copy(xcT[par][:, g * 1024:(g + 1) * 1024], p_tr[g % 2][:]).then_inc(EXT, 1)
                            s.wait_ge(MMT1, t + 2)
                            s.copy(mntq[:], p_mq[:]).then_inc(EMT1, 1)
                            s.wait_ge(MAQ, t + 1)
                            if t >= 1:
                                s.wait_ge(VQC, t)
                            s.activation(qln[:], p_aq[:], AF.Copy,
                                         scale=stats[:, 14 + par:15 + par]).then_inc(EQL, 1)
                            s.wait_ge(EQL, t + 1)
                            s.activation(junk[:, 0:QR // 2], qln[:, 0:QR // 2], AF.Square, accum_out=stats[:, 22:23])
                            s.activation(junk[:, QR // 2:QR], qln[:, QR // 2:QR], AF.Square,
                                         accum_out=stats[:, 23:24]).then_inc(SUQ, 1)
                            s.wait_ge(VSS, t + 1)
                            sfence(s, s.activation(stats[:, 24:25], stats[:, 27:28], AF.Ln))
                            s.activation(stats[:, 16:17], stats[:, 24:25], AF.Exp, scale=0.5).then_inc(SUQ2, 1)
                            for g in range(2):
                                s.wait_ge(MQT, t * 2 + g + 1)
                                nj = 8 if g == 0 else 4
                                for j8 in range(nj):
                                    j = g * 8 + j8
                                    last = s.copy(qcT[:, j * R + t * 128:j * R + (t + 1) * 128],
                                                  p_tr[g % 2][:, j8 * 128:(j8 + 1) * 128])
                                last.then_inc(EQT, 1)

                    @block.vector
                    def _(v):
                        for t in range(NT):
                            par = t % 2
                            v.wait_ge(LA, 16 * (t + 1))
                            if t >= 1:
                                v.wait_ge(MTR, (t - 1) * 4 + 4)
                            v.tensor_copy(xcb[0][:], xt[0][:]).then_inc(VXC, 1)
                            vfence(v, v.reduce_sum(stats[:, 0:1], xt[0][:], axis=AX.X, negate=True))
                            if t >= 2:
                                v.wait_ge(MMT1, t)
                            v.tensor_scalar(stats[:, 2 + par:3 + par], stats[:, 0:1], 1.0 / D,
                                            None, ALU.mult).then_inc(VMN1, 1)
                            v.wait_ge(VMN1, t + 1)
                            v.tensor_mul(stats[:, 4:5], stats[:, 2 + par:3 + par], stats[:, 2 + par:3 + par])
                            par2 = (t % 2) * 22
                            v.wait_ge(SQ, t + 1)
                            vfence(v, v.tensor_add(stats[:, 5:6], stats[:, 6 + par2:7 + par2], stats[:, 7 + par2:8 + par2]))
                            vfence(v, v.tensor_scalar(stats[:, 8:9], stats[:, 5:6], 1.0 / D, None, ALU.mult))
                            v.tensor_sub(stats[:, 10 + par:11 + par], stats[:, 8:9], stats[:, 4:5]).then_inc(VVAR, 1)
                            v.wait_ge(SUQ, t + 1)
                            vfence(v, v.tensor_add(stats[:, 26:27], stats[:, 22:23], stats[:, 23:24]))
                            v.tensor_scalar(stats[:, 27:28], stats[:, 26:27], 1.0 / QR, None, ALU.mult).then_inc(VSS, 1)
                            v.wait_ge(SUQ2, t + 1)
                            vfence(v, v.tensor_scalar(stats[:, 20:21], stats[:, 16:17], RMS_EPS, None, ALU.add))
                            vfence(v, v.reciprocal(stats[:, 18:19], stats[:, 20:21]))
                            if t >= 1:
                                v.wait_ge(MQT, (t - 1) * 2 + 2)
                            v.scalar_tensor_tensor(qcb[0][:], qln[:], stats[:, 18:19], gqa_b[:],
                                                   ALU.mult, ALU.mult).then_inc(VQC, 1)

                    @block.tensor
                    def _(tn):
                        tn.wait_ge(CLD, cld_n[0])
                        tn.wait_ge(WLD, wld_n[0])
                        for nch in range(3):
                            lo = nch * 512
                            for j in range(32):
                                mm = tn.matmul(p_aq[0:1, lo:lo + 512],
                                               ones_sb[:, 0:1],
                                               wqa_sb[:, j * QR + lo:j * QR + lo + 512],
                                               start=(j == 0), stop=(j == 31))
                        mm.then_inc(MMT1, 1)
                        for t in range(NT):
                            par = 0
                            tn.wait_ge(VXC, t + 1)
                            for g in range(4):
                                if g >= 2:
                                    tn.wait_ge(EXT, t * 4 + g - 1)
                                elif t >= 1:
                                    if g == 0:
                                        tn.wait_ge(EQT, (t - 1) * 2 + 1)
                                    else:
                                        tn.wait_ge(EQT, (t - 1) * 2 + 2)
                                for j8 in range(8):
                                    j = g * 8 + j8
                                    mm = tn.matmul(p_tr[g % 2][:, j8 * 128:(j8 + 1) * 128],
                                                   xcb[par][:, j * 128:(j + 1) * 128], ident_sb[:],
                                                   start=True, stop=True)
                                mm.then_inc(MTR, 1)
                            tn.wait_ge(VMN1, t + 1)
                            if t >= 1:
                                tn.wait_ge(EMT1, t + 1)
                            tn.matmul(p_mq[:], stats[0:128, 2 + t % 2:3 + t % 2], identf_sb[:],
                                      start=True, stop=True).then_inc(MMT1, 1)
                            tn.wait_ge(EXT, t * 4 + 4)
                            if t >= 1:
                                tn.wait_ge(EQL, t)
                            for j in range(32):
                                for nch in range(3):
                                    mm = tn.matmul(p_aq[:, nch * 512:(nch + 1) * 512],
                                                   xcT[par][:, j * 128:(j + 1) * 128],
                                                   wqa_sb[:, j * QR + nch * 512:j * QR + nch * 512 + 512],
                                                   start=(j == 0), stop=False)
                            tn.wait_ge(EMT1, t + 2)
                            for nch in range(3):
                                mm = tn.matmul(p_aq[:, nch * 512:(nch + 1) * 512], mntq[:],
                                               colqa[:, nch * 512:(nch + 1) * 512],
                                               start=False, stop=True)
                            mm.then_inc(MAQ, 1)
                            tn.wait_ge(VQC, t + 1)
                            for g in range(2):
                                tn.wait_ge(EXT, t * 4 + 3 + g)
                                nj = 8 if g == 0 else 4
                                for j8 in range(nj):
                                    j = g * 8 + j8
                                    mm = tn.matmul(p_tr[g % 2][:, j8 * 128:(j8 + 1) * 128],
                                                   qcb[par][:, j * 128:(j + 1) * 128], ident_sb[:],
                                                   start=True, stop=True)
                                mm.then_inc(MQT, 1)

                    barrier()

                kcT = EA(nc.sbuf_tensor("kcT", [128, 4 * S], BF16))
                krT = EA(nc.sbuf_tensor("krT", [65, S], BF16))
                with ExitStack() as ctxKV:
                    EKV = ctxKV.enter_context
                    xt = [EKV(nc.sbuf_tensor(f"kxt{i}", [128, D], F32)) for i in range(2)]
                    xcb = [EKV(nc.sbuf_tensor(f"kxcb{i}", [128, D], BF16)) for i in range(2)]
                    xcT = [EKV(nc.sbuf_tensor(f"kxcT{i}", [128, D], BF16)) for i in range(2)]
                    wkva_sb = EKV(nc.sbuf_tensor("wkva_sb", [128, 32 * (KVR + RD)], BF16))
                    gkva_b = EKV(nc.sbuf_tensor("gkva_b", [128, KVR], F32))
                    kvln = EKV(nc.sbuf_tensor("kvln", [128, KVR + RD], F32))
                    kcb = [EKV(nc.sbuf_tensor(f"kcb{i}", [128, KVR], BF16)) for i in range(2)]
                    ckb = [EKV(nc.sbuf_tensor(f"ckb{i}", [128, RD], F32)) for i in range(2)]
                    skb = [EKV(nc.sbuf_tensor(f"skb{i}", [128, RD], F32)) for i in range(2)]
                    krf = EKV(nc.sbuf_tensor("krf", [128, RD], BF16))
                    krt1 = EKV(nc.sbuf_tensor("krt1", [128, RD], F32))
                    krt2 = EKV(nc.sbuf_tensor("krt2", [128, RD], F32))
                    p_akv = EKV(nc.psum_tensor("p_akv", [128, 1024], F32))
                    p_t2 = [EKV(nc.psum_tensor(f"p_t2{i}", [128, 1024], F32)) for i in range(2)]

                    colkva = EKV(nc.sbuf_tensor("colkva", [1, KVR + RD], BF16))
                    mntb = EKV(nc.sbuf_tensor("mntb", [1, 128], BF16))
                    p_mt = EKV(nc.psum_tensor("p_mt", [1, 128], F32))
                    NT2 = 64
                    VMN2 = sem("k_vmn")
                    MMT2 = sem("k_mmt")
                    EMT2 = sem("k_emt")
                    SQ2 = sem("k_sq")
                    VXC2 = sem("k_xc")
                    VVAR2 = sem("k_var")
                    MTR2 = sem("k_mtr")
                    EXT2 = sem("k_ext")
                    MKV = sem("k_mkv")
                    EKL = sem("k_ekl")
                    SUK = sem("k_suk")
                    VKC = sem("k_vkc")
                    VKR = sem("k_vkr")
                    MKT = sem("k_mkt")
                    EKT = sem("k_ekt")

                    @block.gpsimd
                    def _(gp):
                        for s2 in range(2):
                            gp.dma_start(
                                out=wkva_sb[:].rearrange("p (s n) -> p s n", s=32)[:, s2 * 16:(s2 + 1) * 16, :],
                                in_=w_kva.rearrange("(s p) n -> p s n", p=128)[:, s2 * 16:(s2 + 1) * 16, :],
                            ).then_inc(WLD, 16)
                    wld_n[0] += 32

                    @block.vector
                    def _(v):
                        v.memset(krT[64:65, :], -1.0).then_inc(VKR, 1)

                    @block.sync
                    def _(sync):
                        sync.dma_start(out=gkva_b[:], in_=g_kva.broadcast_to((128, KVR))).then_inc(CLD, 16)
                        for t in range(NT2):
                            par = t % 2
                            sl = [LA, LB][par]
                            cl = [LC, LD2][par]
                            if t >= 2:
                                sync.wait_ge(VMN2, t - 1)
                                sync.wait_ge(SQ2, t - 1)
                            sync.dma_start(out=xt[par][:], in_=x_full[t * 128:(t + 1) * 128, :]).then_inc(sl, 16)
                            if t >= 2:
                                sync.wait_ge(VKR, t)
                            sync.dma_start(out=ckb[par][:], in_=cosk[t * 128:(t + 1) * 128, :]).then_inc(cl, 16)
                            sync.dma_start(out=skb[par][:], in_=sink[t * 128:(t + 1) * 128, :]).then_inc(cl, 16)
                    cld_n[0] += 16

                    XB1 = 16 * 8
                    XB2 = 0

                    @block.scalar
                    def _(s):
                        s.wait_ge(CLD, cld_n[0])
                        s.wait_ge(WLD, wld_n[0])

                        s.wait_ge(MMT2, 1)
                        s.copy(colkva[:], p_akv[0:1, 0:KVR + RD]).then_inc(EMT2, 1)

                        def kcr_evac(tm):
                            s.wait_ge(MKT, tm + 1)
                            for j4 in range(4):
                                s.copy(kcT[:, j4 * S + tm * 128:j4 * S + (tm + 1) * 128],
                                       p_t2[0][:, j4 * 128:(j4 + 1) * 128])
                            s.copy(krT[0:64, tm * 128:(tm + 1) * 128],
                                   p_t2[0][0:64, 512:640]).then_inc(EKT, 1)

                        for t in range(NT2):
                            par = t % 2
                            sl = [LA, LB][par]
                            base = [XB1, XB2][par]
                            par2 = par * 22
                            if t >= 2:
                                kcr_evac(t - 2)
                            s.wait_ge(sl, base + 16 * (t // 2 + 1))
                            sfence(s, s.activation(junk[:], xt[par][:, 0:2048], AF.Square,
                                         accum_out=stats[:, 6 + par2:7 + par2]))
                            s.activation(junk[:], xt[par][:, 2048:4096], AF.Square,
                                         accum_out=stats[:, 7 + par2:8 + par2]).then_inc(SQ2, 1)
                            if t >= 2:
                                s.wait_ge(MKV, t - 1)
                            for g in range(4):
                                s.wait_ge(MTR2, t * 4 + g + 1)
                                s.copy(xcT[par][:, g * 1024:(g + 1) * 1024], p_t2[g % 2][:]).then_inc(EXT2, 1)
                            s.wait_ge(MMT2, t + 2)
                            s.copy(mntb[:], p_mt[:]).then_inc(EMT2, 1)
                            s.wait_ge(VVAR2, t + 1)
                            sfence(s, s.activation(stats[:, 12:13], stats[:, 10 + par:11 + par], AF.Ln, bias=eps_sb[:]))
                            sfence(s, s.activation(stats[:, 14 + par:15 + par], stats[:, 12:13], AF.Exp, scale=-0.5))
                            s.wait_ge(MKV, t + 1)
                            if t >= 1:
                                s.wait_ge(VKC, t)
                                s.wait_ge(VKR, t + 1)
                            s.activation(kvln[:], p_akv[:, 0:KVR + RD], AF.Copy,
                                         scale=stats[:, 14 + par:15 + par]).then_inc(EKL, 1)
                            s.wait_ge(EKL, t + 1)
                            sfence(s, s.activation(junk[:, 0:KVR], kvln[:, 0:KVR], AF.Square, accum_out=stats[:, 22:23]))
                            sfence(s, s.activation(stats[:, 24:25], stats[:, 22:23], AF.Ln, scale=1.0 / KVR))
                            s.activation(stats[:, 16:17], stats[:, 24:25], AF.Exp, scale=0.5).then_inc(SUK, 1)
                            if t == NT2 - 1:
                                kcr_evac(NT2 - 2)
                                kcr_evac(NT2 - 1)

                    @block.vector
                    def _(v):
                        for t in range(NT2):
                            par = t % 2
                            sl = [LA, LB][par]
                            cl = [LC, LD2][par]
                            base = [XB1, XB2][par]
                            v.wait_ge(sl, base + 16 * (t // 2 + 1))
                            if t >= 2:
                                v.wait_ge(MTR2, (t - 2) * 4 + 4)
                            v.tensor_copy(xcb[par][:], xt[par][:]).then_inc(VXC2, 1)
                            vfence(v, v.reduce_sum(stats[:, 0:1], xt[par][:], axis=AX.X, negate=True))
                            if t >= 2:
                                v.wait_ge(MMT2, t)
                            v.tensor_scalar(stats[:, 2 + par:3 + par], stats[:, 0:1], 1.0 / D,
                                            None, ALU.mult).then_inc(VMN2, 1)
                            v.wait_ge(VMN2, t + 1)
                            v.tensor_mul(stats[:, 4:5], stats[:, 2 + par:3 + par], stats[:, 2 + par:3 + par])
                            par2 = par * 22
                            v.wait_ge(SQ2, t + 1)
                            vfence(v, v.tensor_add(stats[:, 5:6], stats[:, 6 + par2:7 + par2], stats[:, 7 + par2:8 + par2]))
                            vfence(v, v.tensor_scalar(stats[:, 8:9], stats[:, 5:6], 1.0 / D, None, ALU.mult))
                            v.tensor_sub(stats[:, 10 + par:11 + par], stats[:, 8:9], stats[:, 4:5]).then_inc(VVAR2, 1)
                            v.wait_ge(SUK, t + 1)
                            vfence(v, v.tensor_scalar(stats[:, 20:21], stats[:, 16:17], RMS_EPS, None, ALU.add))
                            vfence(v, v.reciprocal(stats[:, 18:19], stats[:, 20:21]))
                            if t >= 2:
                                v.wait_ge(MKT, t - 1)
                            v.scalar_tensor_tensor(kcb[par][:], kvln[:, 0:KVR], stats[:, 18:19], gkva_b[:],
                                                   ALU.mult, ALU.mult).then_inc(VKC, 1)
                            v.wait_ge(cl, 32 * (t // 2 + 1))
                            if t >= 1:
                                v.wait_ge(MKT, t)
                            v.tensor_scalar(krt1[:, 0:32], kvln[:, KVR + 32:KVR + 64], -1.0, None, ALU.mult)
                            vfence(v, v.tensor_copy(krt1[:, 32:64], kvln[:, KVR:KVR + 32]))
                            v.tensor_mul(krt2[:], kvln[:, KVR:KVR + RD], ckb[par][:])
                            vfence(v, v.tensor_mul(krt1[:], krt1[:], skb[par][:]))
                            v.tensor_add(krf[:], krt2[:], krt1[:]).then_inc(VKR, 1)

                    @block.tensor
                    def _(tn):
                        tn.wait_ge(CLD, cld_n[0])
                        tn.wait_ge(WLD, wld_n[0])

                        def kcr_transposes(t):
                            par = t % 2
                            tn.wait_ge(VKC, t + 1)
                            tn.wait_ge(VKR, t + 2)
                            tn.wait_ge(EXT2, min((t + 1) * 4 + 3, NT2 * 4))
                            if t >= 1:
                                tn.wait_ge(EKT, t)
                            for j4 in range(4):
                                tn.matmul(p_t2[0][:, j4 * 128:(j4 + 1) * 128],
                                          kcb[par][:, j4 * 128:(j4 + 1) * 128], ident_sb[:],
                                          start=True, stop=True)
                            mm = tn.matmul(p_t2[0][0:64, 512:640], krf[:], ident_sb[:],
                                           start=True, stop=True)
                            mm.then_inc(MKT, 1)

                        tn.wait_ge(WLD, wld_n[0])
                        for nch in range(2):
                            lo = nch * 512
                            wdt = 512 if nch == 0 else 64
                            for j in range(32):
                                mm = tn.matmul(p_akv[0:1, lo:lo + wdt],
                                               ones_sb[:, 0:1],
                                               wkva_sb[:, j * (KVR + RD) + lo:j * (KVR + RD) + lo + wdt],
                                               start=(j == 0), stop=(j == 31))
                        mm.then_inc(MMT2, 1)
                        for t in range(NT2):
                            par = t % 2
                            tn.wait_ge(VXC2, t + 1)
                            for g in range(4):
                                if g >= 2:
                                    tn.wait_ge(EXT2, t * 4 + g - 1)
                                elif t >= 1:
                                    if g == 0:
                                        if t >= 2:
                                            tn.wait_ge(EKT, t - 1)
                                    else:
                                        tn.wait_ge(EXT2, (t - 1) * 4 + 4)
                                for j8 in range(8):
                                    j = g * 8 + j8
                                    mm = tn.matmul(p_t2[g % 2][:, j8 * 128:(j8 + 1) * 128],
                                                   xcb[par][:, j * 128:(j + 1) * 128], ident_sb[:],
                                                   start=True, stop=True)
                                mm.then_inc(MTR2, 1)
                            tn.wait_ge(VMN2, t + 1)
                            if t >= 1:
                                tn.wait_ge(EMT2, t + 1)
                            tn.matmul(p_mt[:], stats[0:128, 2 + par:3 + par], identf_sb[:],
                                      start=True, stop=True).then_inc(MMT2, 1)
                            tn.wait_ge(EXT2, t * 4 + 4)
                            if t >= 1:
                                tn.wait_ge(EKL, t)
                            for j in range(32):
                                for nch in range(2):
                                    lo = nch * 512
                                    wdt = 512 if nch == 0 else 64
                                    mm = tn.matmul(p_akv[:, lo:lo + wdt],
                                                   xcT[par][:, j * 128:(j + 1) * 128],
                                                   wkva_sb[:, j * (KVR + RD) + lo:j * (KVR + RD) + lo + wdt],
                                                   start=(j == 0), stop=False)
                            tn.wait_ge(EMT2, t + 2)
                            tn.matmul(p_akv[:, 0:512], mntb[:], colkva[:, 0:512], start=False, stop=True)
                            mm = tn.matmul(p_akv[:, 512:576], mntb[:], colkva[:, 512:576], start=False, stop=True)
                            mm.then_inc(MKV, 1)
                            if t >= 1:
                                kcr_transposes(t - 1)
                        kcr_transposes(NT2 - 1)

                    barrier()

            if taps:
                @block.sync
                def _(sync):
                    sync.wait_ge(QSP, 16)
                    sync.dma_start(out=tap_aps["t_qcT"], in_=d_qcT).then_inc(SG, 16)
                    sync.dma_start(out=tap_aps["t_kcT"], in_=kcT[:]).then_inc(SG, 16)
                    sync.dma_start(out=tap_aps["t_krT"], in_=krT[:]).then_inc(SG, 16)
                    sync.wait_ge(SG, 48)
                barrier()
            if max_phase <= 1:
                return nc

            with ExitStack() as ctxT:
                ET = ctxT.enter_context
                qcT2 = ET(nc.sbuf_tensor("qcT2", [128, 12 * R], BF16))
                wqb_sb = [ET(nc.sbuf_tensor(f"wqb{i}", [128, 12 * QK], BF16)) for i in range(2)]
                wkvb_sb = [ET(nc.sbuf_tensor(f"wkvb{i}", [128, 4 * (ND + VD)], BF16)) for i in range(2)]
                cqT = ET(nc.sbuf_tensor("cqT", [RD, R], F32))
                sqT = ET(nc.sbuf_tensor("sqT", [RD, R], F32))
                qT_nope = ET(nc.sbuf_tensor("qT_nope", [128, R], BF16))
                qT_s2 = ET(nc.sbuf_tensor("qT_s2", [65, R], BF16))
                qrr = ET(nc.sbuf_tensor("qrr", [RD, R], F32))
                rt1 = ET(nc.sbuf_tensor("rt1", [RD, R], F32))
                kT_nope = ET(nc.sbuf_tensor("kT_nope", [128, S], BF16))
                Vb = ET(nc.sbuf_tensor("Vb", [128, 64 * VD], BF16))
                sqb = [ET(nc.sbuf_tensor(f"sqb{i}", [128, 512], BF16)) for i in range(2)]
                sqb2 = [ET(nc.sbuf_tensor(f"sqb2{i}", [65, 512], BF16)) for i in range(2)]
                sqq0 = ET(nc.sbuf_tensor("sqq0", [128, 512], BF16))
                sqq1 = ET(nc.sbuf_tensor("sqq1", [64, 512], BF16))
                PT = ET(nc.sbuf_tensor("PT", [128, 4 * 1024], BF16))
                maxb = ET(nc.sbuf_tensor("maxb", [128, 18], F32))
                lkv = ET(nc.sbuf_tensor("lkv", [1, 4], F32))
                nqln = ET(nc.sbuf_tensor("nqln", [1, 1024], F32))
                lacc = ET(nc.sbuf_tensor("lacc", [128, 1024], F32))
                ptp = ET(nc.sbuf_tensor("ptp", [128, 1024], BF16))
                rlb = ET(nc.sbuf_tensor("rlb", [128, 1024], F32))
                OT_h = ET(nc.sbuf_tensor("OT_h", [128, R], BF16))
                p_sp = [ET(nc.psum_tensor(f"p_sp{i}", [128, 1024], F32)) for i in range(2)]
                p_pvT = ET(nc.psum_tensor("p_pvT", [128, 1024], F32))
                p_gen = [ET(nc.psum_tensor(f"p_gen{i}", [128, 512], F32)) for i in range(2)]

                AKGM = sem("a_kgm")
                AKGE = sem("a_kge")
                ASQK = sem("a_sqk")
                ANKM = sem("a_nkm")
                ANKR = sem("a_nkr")
                AMK = sem("a_mk")
                ALK = sem("a_lk")
                ALKH = sem("a_lkh")
                AVGM = sem("a_vgm")
                AVGE = sem("a_vge")
                AQGM = sem("a_qgm")
                AQGE = sem("a_qge")
                AQRM = sem("a_qrm")
                AQRE = sem("a_qre")
                AROP = sem("a_rop")
                ASQQ = sem("a_sqq")
                ANQM = sem("a_nqm")
                AMRW = sem("a_mrw")
                AEVQ = sem("a_evq")
                ASP = sem("a_sp")
                AEXP = sem("a_exp")
                APV = sem("a_pv")
                ALC = sem("a_lc")
                ABC = sem("a_bc")
                AOTE = sem("a_ote")

                @block.sync
                def _(sync):
                    sync.dma_start(out=cqT[:], in_=cosqT).then_inc(CLD, 16)
                    sync.dma_start(out=sqT[:], in_=sinqT).then_inc(CLD, 16)
                    sync.wait_ge(QSP, 16)
                    sync.dma_start(out=qcT2[:], in_=d_qcT).then_inc(CLD, 16)
                cld_n[0] += 48

                @block.gpsimd
                def _(gp):
                    for h in range(nheads):
                        par = h % 2
                        sl = [LE, LF][par]
                        if h >= 2:
                            gp.wait_ge(AQRM, (h - 2) * 2 + 2)
                            gp.wait_ge(AVGM, (h - 2) * 64 + 64)
                        gp.dma_start(
                            out=wqb_sb[par][:].rearrange("p (s n) -> p s n", s=12),
                            in_=w_qb[:, h * QK:(h + 1) * QK].rearrange("(s p) n -> p s n", p=128),
                        ).then_inc(sl, 16)
                        gp.dma_start(
                            out=wkvb_sb[par][:].rearrange("p (s n) -> p s n", s=4),
                            in_=w_kvb[:, h * (ND + VD):(h + 1) * (ND + VD)].rearrange("(s p) n -> p s n", p=128),
                        ).then_inc(sl, 16)

                @block.tensor
                def _(tn):
                    tn.wait_ge(CLD, cld_n[0])
                    tn.wait_ge(WLD, wld_n[0])
                    for h in range(nheads):
                        par = h % 2
                        sl = [LE, LF][par]
                        tn.wait_ge(sl, 32 * (h // 2 + 1))
                        hb16 = h * 16
                        hb64 = h * 64
                        hb2 = h * 2
                        for c in range(16):
                            gb = p_gen[c % 2]
                            if c >= 2:
                                tn.wait_ge(AKGE, hb16 + c - 1)
                            elif h >= 1:
                                tn.wait_ge(AVGE, (h - 1) * 64 + 63 + c)
                            for j in range(4):
                                mm = tn.matmul(gb[:],
                                               wkvb_sb[par][:, j * (ND + VD):j * (ND + VD) + ND],
                                               kcT[:, j * S + c * 512:j * S + c * 512 + 512],
                                               start=(j == 0), stop=(j == 3))
                            mm.then_inc(AKGM, 1)
                        tn.wait_ge(AKGE, hb16 + 16)
                        for j in range(12):
                            for qc in range(2):
                                mm = tn.matmul(p_gen[qc][:],
                                               wqb_sb[par][:, j * QK:j * QK + ND],
                                               qcT2[:, j * R + qc * 512:j * R + qc * 512 + 512],
                                               start=(j == 0), stop=(j == 11))
                                if j == 11:
                                    mm.then_inc(AQGM, 1)
                        tn.wait_ge(AQGE, hb2 + 2)
                        for j in range(12):
                            for qc in range(2):
                                mm = tn.matmul(p_gen[qc][0:64, :],
                                               wqb_sb[par][:, j * QK + ND:j * QK + QK],
                                               qcT2[:, j * R + qc * 512:j * R + qc * 512 + 512],
                                               start=(j == 0), stop=(j == 11))
                                if j == 11:
                                    mm.then_inc(AQRM, 1)
                        for qc in range(2):
                            gb = p_gen[qc % 2]
                            tn.wait_ge(ASQQ, h * 4 + qc * 2 + 2)
                            tn.wait_ge(AQRE, hb2 + qc + 1)
                            mm = tn.matmul(gb[:], ones_sb[:], sqq0[:], start=True, stop=False)
                            mm = tn.matmul(gb[:], ones_sb[0:64, :], sqq1[0:64, :], start=False, stop=True)
                            mm.then_inc(ANQM, 1)
                        for c in range(16):
                            gb = p_gen[c % 2]
                            tn.wait_ge(ASQK, hb16 + c + 1)
                            if c >= 2:
                                tn.wait_ge(ANKR, hb16 + c - 1)
                            else:
                                tn.wait_ge(AEVQ, hb2 + c + 1)
                            mm = tn.matmul(gb[:], ones_sb[:], sqb[c % 2][:], start=True, stop=False)
                            mm = tn.matmul(gb[:], ones_sb[0:65, :], sqb2[c % 2][:],
                                           start=False, stop=True)
                            mm.then_inc(ANKM, 1)
                        for kt in range(64):
                            gb = p_gen[kt % 2]
                            if kt >= 2:
                                tn.wait_ge(AVGE, hb64 + kt - 1)
                            else:
                                tn.wait_ge(ANKR, hb16 + 15 + kt)
                            for j in range(4):
                                mm = tn.matmul(gb[:, 0:128],
                                               kcT[:, j * S + kt * 128:j * S + (kt + 1) * 128],
                                               wkvb_sb[par][:, j * (ND + VD) + ND:j * (ND + VD) + ND + VD],
                                               start=(j == 0), stop=(j == 3))
                            mm.then_inc(AVGM, 1)
                        def emit_pv(kt):
                            tn.wait_ge(AEXP, hb64 + kt + 1)
                            vs = Vb[:, kt * VD:(kt + 1) * VD]
                            slot = (kt % 4) * 1024
                            tn.matmul(p_pvT[:, 0:512], vs, PT[:, slot:slot + 512],
                                      start=(kt == 0), stop=(kt == 63))
                            mm = tn.matmul(p_pvT[:, 512:1024], vs, PT[:, slot + 512:slot + 1024],
                                           start=(kt == 0), stop=(kt == 63))
                            mm.then_inc(APV, 1)

                        for kt in range(64):
                            spb = p_sp[kt % 2]
                            if kt >= 2:
                                tn.wait_ge(AEXP, hb64 + kt - 1)
                            elif h >= 1:
                                tn.wait_ge(AOTE, h)
                            if kt == 0:
                                tn.wait_ge(AMRW, hb2 + 2)
                                tn.wait_ge(AROP, hb2 + 2)
                                tn.wait_ge(AKGE, hb16 + 16)
                                tn.wait_ge(AVGE, hb64 + 64)
                            ks = kT_nope[:, kt * 128:(kt + 1) * 128]
                            rs = krT[:, kt * 128:(kt + 1) * 128]
                            tn.matmul(spb[:, 0:512], ks, qT_nope[:, 0:512], start=True, stop=False)
                            tn.matmul(spb[:, 512:1024], ks, qT_nope[:, 512:1024], start=True, stop=False)
                            tn.matmul(spb[:, 0:512], rs, qT_s2[:, 0:512], start=False, stop=True)
                            mm = tn.matmul(spb[:, 512:1024], rs, qT_s2[:, 512:1024], start=False, stop=True)
                            mm.then_inc(ASP, 1)
                            if kt >= 1:
                                emit_pv(kt - 1)
                        emit_pv(63)
                        tn.wait_ge(ALC, h * 32 + 32)
                        tn.wait_ge(AEXP, hb64 + 64)
                        tn.matmul(p_sp[0][:, 0:512], onesf2_sb[:], lacc[:, 0:512], start=True, stop=True)
                        mm = tn.matmul(p_sp[0][:, 512:1024], onesf2_sb[:], lacc[:, 512:1024], start=True, stop=True)
                        mm.then_inc(ABC, 1)

                @block.scalar
                def _(s):
                    for h in range(nheads):
                        hb16, hb64, hb2 = h * 16, h * 64, h * 2
                        for c in range(16):
                            s.wait_ge(AKGM, hb16 + c + 1)
                            s.copy(kT_nope[:, c * 512:(c + 1) * 512], p_gen[c % 2][:]).then_inc(AKGE, 1)
                        for qc in range(2):
                            s.wait_ge(AQGM, hb2 + qc + 1)
                            s.activation(qT_nope[:, qc * 512:qc * 512 + 512], p_gen[qc % 2][:],
                                         AF.Copy, scale=SC).then_inc(AQGE, 1)
                        for qc in range(2):
                            s.wait_ge(AQRM, hb2 + qc + 1)
                            s.activation(qrr[:, qc * 512:qc * 512 + 512], p_gen[qc % 2][0:64, :],
                                         AF.Copy, scale=SC).then_inc(AQRE, 1)
                        for qc in range(2):
                            s.wait_ge(ANQM, hb2 + qc + 1)
                            s.activation(nqln[0:1, qc * 512:qc * 512 + 512],
                                         p_gen[qc % 2][0:1, :], AF.Ln).then_inc(AEVQ, 1)
                        s.wait_ge(AMK, h + 1)
                        s.activation(lkv[0:1, 0:1], maxb[0:1, 16:17], AF.Ln).then_inc(ALK, 1)
                        for qc in range(2):
                            s.wait_ge(ALKH, h + 1)
                            s.wait_ge(AEVQ, hb2 + qc + 1)
                            s.activation(qT_s2[64:65, qc * 512:qc * 512 + 512],
                                         nqln[0:1, qc * 512:qc * 512 + 512], AF.Exp,
                                         scale=0.5, bias=lkv[0:1, 1:2]).then_inc(AMRW, 1)
                        hb32 = h * 32
                        for kt in range(64):
                            s.wait_ge(ASP, hb64 + kt + 1)
                            pvw = hb64 + kt - 3
                            if pvw >= 1:
                                s.wait_ge(APV, pvw)
                            lcw = hb32 + (kt - 4) // 2 + 1
                            if kt >= 4 or h >= 1:
                                s.wait_ge(ALC, lcw)
                            s.activation(PT[:, (kt % 4) * 1024:(kt % 4) * 1024 + 1024],
                                         p_sp[kt % 2][:], AF.Exp).then_inc(AEXP, 1)

                @block.vector
                def _(v):
                    for h in range(nheads):
                        hb16, hb64, hb2 = h * 16, h * 64, h * 2
                        for qc in range(2):
                            lo = qc * 512
                            v.wait_ge(AQRE, hb2 + qc + 1)
                            v.tensor_scalar(rt1[0:32, lo:lo + 512], qrr[32:64, lo:lo + 512], -1.0, None, ALU.mult)
                            vfence(v, v.tensor_copy(rt1[32:64, lo:lo + 512], qrr[0:32, lo:lo + 512]))
                            vfence(v, v.tensor_mul(qrr[:, lo:lo + 512], qrr[:, lo:lo + 512], cqT[:, lo:lo + 512]))
                            vfence(v, v.tensor_mul(rt1[:, lo:lo + 512], rt1[:, lo:lo + 512], sqT[:, lo:lo + 512]))
                            v.tensor_add(qT_s2[0:64, lo:lo + 512], qrr[:, lo:lo + 512],
                                         rt1[:, lo:lo + 512]).then_inc(AROP, 1)
                            v.wait_ge(AQGE, hb2 + qc + 1)
                            if h >= 1 or qc >= 1:
                                v.wait_ge(ANQM, hb2 + qc)
                            v.tensor_mul(sqq0[:], qT_nope[:, lo:lo + 512],
                                         qT_nope[:, lo:lo + 512]).then_inc(ASQQ, 1)
                            v.wait_ge(AROP, hb2 + qc + 1)
                            v.tensor_mul(sqq1[:], qT_s2[0:64, lo:lo + 512],
                                         qT_s2[0:64, lo:lo + 512]).then_inc(ASQQ, 1)
                        for c in range(16):
                            v.wait_ge(AKGE, hb16 + c + 1)
                            if c >= 2:
                                v.wait_ge(ANKM, hb16 + c - 1)
                            elif h >= 1:
                                v.wait_ge(ANKM, (h - 1) * 16 + 15 + c)
                            v.tensor_mul(sqb[c % 2][:], kT_nope[:, c * 512:(c + 1) * 512],
                                         kT_nope[:, c * 512:(c + 1) * 512])
                            v.tensor_mul(sqb2[c % 2][:], krT[:, c * 512:(c + 1) * 512],
                                         krT[:, c * 512:(c + 1) * 512]).then_inc(ASQK, 1)
                            if c >= 2:
                                v.wait_ge(ANKM, hb16 + c - 1)
                                v.reduce_max(maxb[:, c - 2:c - 1], p_gen[c % 2][:], axis=AX.X).then_inc(ANKR, 1)
                        for c in range(14, 16):
                            v.wait_ge(ANKM, hb16 + c + 1)
                            v.reduce_max(maxb[:, c:c + 1], p_gen[c % 2][:], axis=AX.X).then_inc(ANKR, 1)
                        v.wait_ge(ANKR, hb16 + 16)
                        v.reduce_max(maxb[:, 16:17], maxb[:, 0:16], axis=AX.X).then_inc(AMK, 1)
                        v.wait_ge(ALK, h + 1)
                        v.tensor_scalar(lkv[0:1, 1:2], lkv[0:1, 0:1], 0.5, None, ALU.mult).then_inc(ALKH, 1)
                        for kt in range(64):
                            v.wait_ge(AVGM, hb64 + kt + 1)
                            v.tensor_copy(Vb[:, kt * VD:(kt + 1) * VD],
                                          p_gen[kt % 2][:, 0:128]).then_inc(AVGE, 1)
                        hb32 = h * 32
                        v.wait_ge(ABC, h)
                        vfence(v, v.memset(lacc[:], 0.0))
                        for p2 in range(32):
                            v.wait_ge(AEXP, hb64 + 2 * p2 + 2)
                            if p2 >= 1:
                                v.wait_ge(ALC, hb32 + p2)
                            s0 = (2 * p2 % 4) * 1024
                            s1 = ((2 * p2 + 1) % 4) * 1024
                            vfence(v, v.tensor_add(ptp[:], PT[:, s0:s0 + 1024], PT[:, s1:s1 + 1024]))
                            v.tensor_add(lacc[:], lacc[:], ptp[:]).then_inc(ALC, 1)
                        v.wait_ge(ABC, h + 1)
                        vfence(v, v.reciprocal(rlb[:], p_sp[0][:]))
                        v.wait_ge(APV, hb64 + 64)
                        if h >= 1:
                            v.wait_ge(SA, 16 * h)
                        v.tensor_mul(OT_h[:], p_pvT[:], rlb[:]).then_inc(AOTE, 1)

                @block.sync
                def _(sync):
                    for h in range(nheads):
                        sync.wait_ge(AOTE, h + 1)
                        sync.dma_start(out=d_OT[h, :, :], in_=OT_h[:]).then_inc(SA, 16)

                if taps:
                    @block.sync
                    def _(sync):
                        sync.wait_ge(AKGE, 16)
                        sync.dma_start(out=tap_aps["t_kT0"], in_=kT_nope[:]).then_inc(SG, 16)
                        sync.wait_ge(AVGE, 64)
                        sync.dma_start(out=tap_aps["t_V0"], in_=Vb[:]).then_inc(SG, 16)
                        sync.wait_ge(AOTE, 1)
                        sync.dma_start(out=tap_aps["t_OT0"], in_=OT_h[:]).then_inc(SG, 16)

                barrier()
                if max_phase <= 2:
                    return nc

        with ExitStack() as ctxLF:
            ELF = ctxLF.enter_context
            ln2T = ELF(nc.sbuf_tensor("ln2T", [128, 32 * R], BF16))
            sums = ELF(nc.sbuf_tensor("sums", [128, 8], F32))
            ssqs = ELF(nc.sbuf_tensor("ssqs", [128, 8], F32))

            with ExitStack() as ctxP:
                EP = ctxP.enter_context
                ctxPP = ExitStack()
                EPP = ctxPP.enter_context
                OT_all = EPP(nc.sbuf_tensor("OT_all", [128, 32 * R], BF16))
                wo_sb = [EPP(nc.sbuf_tensor(f"wo{i}", [128, 32 * 512], BF16)) for i in range(2)]
                atile = [EPP(nc.sbuf_tensor(f"atile{i}", [128, 512], F32)) for i in range(2)]
                tstat = EPP(nc.sbuf_tensor("tstat", [128, 4], F32))
                p_pr = [EPP(nc.psum_tensor(f"p_pr{i}", [128, 512], F32)) for i in range(4)]

                PMM = sem("p_mm")
                PEV = sem("p_ev")
                PSQ = sem("p_sq")
                PST = sem("p_stat")

                @block.sync
                def _(sync):
                    sync.wait_ge(SA, 16 * nheads)
                    for s2 in range(2):
                        sync.dma_start(
                            out=OT_all[:].rearrange("p (s n) -> p s n", s=32)[:, s2 * 16:(s2 + 1) * 16, :],
                            in_=d_OT.rearrange("h p q -> p h q")[:, s2 * 16:(s2 + 1) * 16, :],
                        ).then_inc(CLD, 16)
                cld_n[0] += 32

                @block.gpsimd
                def _(gp):
                    gp.wait_ge(SA, 16 * nheads)
                    for dc in range(8):
                        par = dc % 2
                        sl = [LE, LF][par]
                        if dc >= 2:
                            gp.wait_ge(PMM, (dc - 2) * 8 + 8)
                        for s2 in range(2):
                            gp.dma_start(
                                out=wo_sb[par][:].rearrange("p (s n) -> p s n", s=32)[:, s2 * 16:(s2 + 1) * 16, :],
                                in_=w_o[:, dc * 512:(dc + 1) * 512].rearrange("(s p) n -> p s n", p=128)[:, s2 * 16:(s2 + 1) * 16, :],
                            ).then_inc(sl, 16)

                WOB = [32 * ((nheads + 1) // 2), 32 * (nheads // 2)]

                @block.tensor
                def _(tn):
                    tn.wait_ge(CLD, cld_n[0])
                    tn.wait_ge(WLD, wld_n[0])
                    for dc in range(8):
                        par = dc % 2
                        sl = [LE, LF][par]
                        tn.wait_ge(sl, WOB[par] + 32 * (dc // 2 + 1))
                        for qt in range(8):
                            idx = dc * 8 + qt
                            if idx >= 4:
                                tn.wait_ge(PEV, idx - 3)
                            for s32 in range(32):
                                mm = tn.matmul(p_pr[qt % 4][:],
                                               OT_all[:, s32 * R + qt * 128:s32 * R + (qt + 1) * 128],
                                               wo_sb[par][:, s32 * 512:(s32 + 1) * 512],
                                               start=(s32 == 0), stop=(s32 == 31))
                            mm.then_inc(PMM, 1)

                @block.scalar
                def _(s):
                    for dc in range(8):
                        for qt in range(8):
                            idx = dc * 8 + qt
                            par = idx % 2
                            s.wait_ge(PMM, idx + 1)
                            if idx >= 2:
                                s.wait_ge([ST0, ST1][par], 16 * (idx // 2))
                            if idx >= 2:
                                s.wait_ge(PST, idx - 1)
                            s.activation(atile[par][:], p_pr[qt % 4][:], AF.Copy,
                                         accum_out=tstat[:, 2 * par:2 * par + 1]).then_inc(PEV, 1)
                            s.wait_ge(PEV, idx + 1)
                            s.activation(junk[:, par * 512:(par + 1) * 512], atile[par][:], AF.Square,
                                         accum_out=tstat[:, 2 * par + 1:2 * par + 2]).then_inc(PSQ, 1)

                @block.vector
                def _(v):
                    v.memset(sums[:], 0.0)
                    vfence(v, v.memset(ssqs[:], 0.0))
                    for dc in range(8):
                        for qt in range(8):
                            idx = dc * 8 + qt
                            par = idx % 2
                            v.wait_ge(PSQ, idx + 1)
                            vfence(v, v.tensor_add(sums[:, qt:qt + 1], sums[:, qt:qt + 1], tstat[:, 2 * par:2 * par + 1]))
                            v.tensor_add(ssqs[:, qt:qt + 1], ssqs[:, qt:qt + 1],
                                         tstat[:, 2 * par + 1:2 * par + 2]).then_inc(PST, 1)

                @block.sync
                def _(sync):
                    for dc in range(8):
                        for qt in range(8):
                            idx = dc * 8 + qt
                            par = idx % 2
                            sync.wait_ge(PSQ, idx + 1)
                            sync.dma_start(out=d_attn[qt * 128:(qt + 1) * 128, dc * 512:(dc + 1) * 512],
                                           in_=atile[par][:]).then_inc([ST0, ST1][par], 16)

                barrier()

                if taps:
                    @block.sync
                    def _(sync):
                        sync.wait_ge(ST0, 16 * 32)
                        sync.wait_ge(ST1, 16 * 32)
                        sync.dma_start(out=tap_aps["t_attn"], in_=d_attn).then_inc(SG, 16)
                    barrier()
                if max_phase <= 2.5:
                    ctxPP.close()
                    return nc

                ctxPP.close()
                xrt = [EP(nc.sbuf_tensor(f"xrt{i}", [128, D], F32)) for i in range(2)]
                art = [EP(nc.sbuf_tensor(f"art{i}", [128, D], F32)) for i in range(2)]
                x2t = EP(nc.sbuf_tensor("x2t", [128, D], F32))
                l2f = EP(nc.sbuf_tensor("l2f", [128, D], F32))
                l2bf = EP(nc.sbuf_tensor("l2bf", [128, D], BF16))
                p_t3 = [EP(nc.psum_tensor(f"p_t3{i}", [128, 1024], F32)) for i in range(2)]

                RVAR = sem("r_var")
                RRS = sem("r_rs")
                RX2 = sem("r_x2")
                RSQ = sem("r_sq")
                RVB = sem("r_vb")
                RRS2 = sem("r_rs2")
                RL2 = sem("r_l2")
                RMM = sem("r_mm")
                REV = sem("r_ev")

                XRB = [40 * 16, 32 * 16]
                X2SB = 16 * 32

                @block.sync
                def _(sync):
                    sync.wait_ge(ST0, 16 * 32)
                    sync.wait_ge(ST1, 16 * 32)
                    for qt in range(8):
                        par = qt % 2
                        sl = [LA, LB][par]
                        if qt >= 2:
                            sync.wait_ge(RL2, qt - 1)
                        sync.dma_start(out=xrt[par][:], in_=x_loc[qt * 128:(qt + 1) * 128, :]).then_inc(sl, 16)
                        sync.dma_start(out=art[par][:], in_=d_attn[qt * 128:(qt + 1) * 128, :]).then_inc(sl, 16)
                        if qt >= 1:
                            sync.wait_ge(RX2, qt)
                            sync.dma_start(out=d_x2[(qt - 1) * 128:qt * 128, :], in_=x2t[:]).then_inc(ST0, 16)
                    sync.wait_ge(RX2, 8)
                    sync.dma_start(out=d_x2[7 * 128:8 * 128, :], in_=x2t[:]).then_inc(ST0, 16)
                    sync.wait_ge(ST0, X2SB + 16 * 8)

                @block.vector
                def _(v):
                    for qt in range(8):
                        par = qt % 2
                        sl = [LA, LB][par]
                        v.wait_ge(sl, XRB[par] + 32 * (qt // 2 + 1))
                        v.tensor_scalar(stats[:, 0:1], sums[:, qt:qt + 1], -1.0 / D, None, ALU.mult)
                        vfence(v, v.tensor_scalar(stats[:, 1:2], ssqs[:, qt:qt + 1], 1.0 / D, None, ALU.mult))
                        vfence(v, v.tensor_mul(stats[:, 2:3], stats[:, 0:1], stats[:, 0:1]))
                        v.tensor_sub(stats[:, 3:4], stats[:, 1:2], stats[:, 2:3]).then_inc(RVAR, 1)
                        v.wait_ge(RRS, qt + 1)
                        vfence(v, v.tensor_scalar(art[par][:], art[par][:], stats[:, 0:1], None, ALU.add))
                        vfence(v, v.tensor_scalar(art[par][:], art[par][:], stats[:, 4:5], None, ALU.mult))
                        if qt >= 1:
                            v.wait_ge(RSQ, qt)
                            v.wait_ge(ST0, X2SB + 16 * qt)
                        v.tensor_add(x2t[:], xrt[par][:], art[par][:]).then_inc(RX2, 1)
                        v.wait_ge(RX2, qt + 1)
                        vfence(v, v.reduce_sum(stats[:, 5:6], x2t[:], axis=AX.X, negate=True))
                        vfence(v, v.tensor_scalar(stats[:, 6:7], stats[:, 5:6], 1.0 / D, None, ALU.mult))
                        v.tensor_mul(stats[:, 7:8], stats[:, 6:7], stats[:, 6:7])
                        v.wait_ge(RSQ, qt + 1)
                        vfence(v, v.tensor_add(stats[:, 23:24], stats[:, 9:10], stats[:, 22:23]))
                        vfence(v, v.tensor_scalar(stats[:, 8:9], stats[:, 23:24], 1.0 / D, None, ALU.mult))
                        v.tensor_sub(stats[:, 10:11], stats[:, 8:9], stats[:, 7:8]).then_inc(RVB, 1)
                        v.wait_ge(RRS2, qt + 1)
                        vfence(v, v.tensor_scalar(l2f[:], x2t[:], stats[:, 6:7], None, ALU.add))
                        if qt >= 1:
                            v.wait_ge(RMM, 4 * qt)
                        v.tensor_scalar(l2bf[:], l2f[:], stats[:, 11:12], None, ALU.mult).then_inc(RL2, 1)

                @block.scalar
                def _(s):
                    for qt in range(8):
                        s.wait_ge(RVAR, qt + 1)
                        sfence(s, s.activation(stats[:, 20:21], stats[:, 3:4], AF.Ln, bias=eps_sb[:]))
                        s.activation(stats[:, 4:5], stats[:, 20:21], AF.Exp, scale=-0.5).then_inc(RRS, 1)
                        s.wait_ge(RX2, qt + 1)
                        sfence(s, s.activation(junk[:], x2t[:, 0:2048], AF.Square, accum_out=stats[:, 9:10]))
                        s.activation(junk[:], x2t[:, 2048:4096], AF.Square, accum_out=stats[:, 22:23]).then_inc(RSQ, 1)
                        s.wait_ge(RVB, qt + 1)
                        sfence(s, s.activation(stats[:, 21:22], stats[:, 10:11], AF.Ln, bias=eps_sb[:]))
                        s.activation(stats[:, 11:12], stats[:, 21:22], AF.Exp, scale=-0.5).then_inc(RRS2, 1)
                        for g in range(4):
                            s.wait_ge(RMM, qt * 4 + g + 1)
                            for j8 in range(8):
                                j = g * 8 + j8
                                last = s.copy(ln2T[:, j * R + qt * 128:j * R + (qt + 1) * 128],
                                              p_t3[g % 2][:, j8 * 128:(j8 + 1) * 128])
                            last.then_inc(REV, 1)

                @block.tensor
                def _(tn):
                    for qt in range(8):
                        tn.wait_ge(RL2, qt + 1)
                        for g in range(4):
                            if g >= 2:
                                tn.wait_ge(REV, qt * 4 + g - 1)
                            elif qt >= 1:
                                tn.wait_ge(REV, (qt - 1) * 4 + g + 3)
                            for j8 in range(8):
                                j = g * 8 + j8
                                mm = tn.matmul(p_t3[g % 2][:, j8 * 128:(j8 + 1) * 128],
                                               l2bf[:, j * 128:(j + 1) * 128], ident_sb[:],
                                               start=True, stop=True)
                            mm.then_inc(RMM, 1)

                if taps:
                    @block.sync
                    def _(sync):
                        sync.wait_ge(ST0, X2SB + 16 * 8)
                        sync.dma_start(out=tap_aps["t_x2"], in_=d_x2).then_inc(SG, 16)

                barrier()
                if max_phase <= 3:
                    return nc

            with ExitStack() as ctxF1:
                EF1 = ctxF1.enter_context
                w1_sb = [EF1(nc.sbuf_tensor(f"w1sb{i}", [128, 32 * 512], BF16)) for i in range(2)]
                b1_sb = EF1(nc.sbuf_tensor("b1_sb", [128, 128], F32))
                hT_sb = [EF1(nc.sbuf_tensor(f"hTsb{i}", [128, 1024], BF16)) for i in range(2)]
                p_h = [EF1(nc.psum_tensor(f"p_h{i}", [128, 1024], F32)) for i in range(2)]

                FMM = sem("f_mm")
                FEV = sem("f_ev")
                FST = sem("f_st")

                @block.sync
                def _(sync):
                    sync.dma_start(out=b1_sb[:], in_=b1r).then_inc(CLD, 16)
                cld_n[0] += 16

                WOB2 = [WOB[0] + 32 * 4, WOB[1] + 32 * 4]

                @block.gpsimd
                def _(gp):
                    gp.wait_ge(ST0, 16 * 40)
                    gp.wait_ge(ST1, 16 * 32)
                    for fs in range(32):
                        par = fs % 2
                        sl = [LE, LF][par]
                        if fs >= 2:
                            gp.wait_ge(FMM, (fs - 2) * 4 + 4)
                        for s2 in range(2):
                            gp.dma_start(
                                out=w1_sb[par][:].rearrange("p (s n) -> p s n", s=32)[:, s2 * 16:(s2 + 1) * 16, :],
                                in_=w1[:, fs * 512:(fs + 1) * 512].rearrange("(s p) n -> p s n", p=128)[:, s2 * 16:(s2 + 1) * 16, :],
                            ).then_inc(sl, 16)

                @block.tensor
                def _(tn):
                    tn.wait_ge(CLD, cld_n[0])
                    tn.wait_ge(WLD, wld_n[0])
                    for fs in range(32):
                        par = fs % 2
                        sl = [LE, LF][par]
                        tn.wait_ge(sl, WOB2[par] + 32 * (fs // 2 + 1))
                        for f4 in range(4):
                            key = fs * 4 + f4
                            if key >= 2:
                                tn.wait_ge(FEV, key - 1)
                            for j in range(32):
                                for qc in range(2):
                                    mm = tn.matmul(p_h[f4 % 2][:, qc * 512:qc * 512 + 512],
                                                   w1_sb[par][:, j * 512 + f4 * 128:j * 512 + (f4 + 1) * 128],
                                                   ln2T[:, j * R + qc * 512:j * R + qc * 512 + 512],
                                                   start=(j == 0), stop=(j == 31))
                            mm.then_inc(FMM, 1)

                HSB = [16 * 40, 16 * 32]

                @block.scalar
                def _(s):
                    for key in range(128):
                        par = key % 2
                        s.wait_ge(FMM, key + 1)
                        if key >= 2:
                            s.wait_ge([ST0, ST1][par], HSB[par] + 16 * (key // 2))
                        s.activation(hT_sb[par][:], p_h[key % 2][:], AF.Gelu_apprx_tanh,
                                     bias=b1_sb[:, key:key + 1]).then_inc(FEV, 1)

                @block.sync
                def _(sync):
                    for key in range(128):
                        par = key % 2
                        sync.wait_ge(FEV, key + 1)
                        sync.dma_start(out=d_hT[key * 128:(key + 1) * 128, :],
                                       in_=hT_sb[par][:]).then_inc([ST0, ST1][par], 16)

                if taps:
                    @block.sync
                    def _(sync):
                        sync.wait_ge(ST0, HSB[0] + 16 * 64)
                        sync.wait_ge(ST1, HSB[1] + 16 * 64)
                        sync.dma_start(out=tap_aps["t_hT0"], in_=d_hT[0:128, :]).then_inc(SG, 16)

                barrier()
                if max_phase <= 4:
                    return nc

            with ExitStack() as ctxF2:
                EF2 = ctxF2.enter_context
                w2_sb = [EF2(nc.sbuf_tensor(f"w2sb{i}", [128, 16 * 512], BF16)) for i in range(2)]
                hTp = [EF2(nc.sbuf_tensor(f"hTp{i}", [128, 16 * 1024], BF16)) for i in range(2)]
                b2_sb = EF2(nc.sbuf_tensor("b2_sb", [1, D], F32))
                o2t = [EF2(nc.sbuf_tensor(f"o2t{i}", [128, 512], F32)) for i in range(2)]
                tstat2 = EF2(nc.sbuf_tensor("tstat2", [128, 4], F32))
                p_s = [EF2(nc.psum_tensor(f"p_s{i}", [128, 512], F32)) for i in range(8)]

                GFB = sem("g_fb")
                GMM = sem("g_mm")
                GEV = sem("g_ev")
                GSQ = sem("g_sq")
                GST = sem("g_stat")

                W2B = [WOB2[0] + 32 * 16, WOB2[1] + 32 * 16]
                HLB = [32 * 32, 32 * 32]

                F2SB = [16 * (32 + 8 + 64), 16 * (32 + 64)]

                @block.sync
                def _(sync):
                    sync.dma_start(out=b2_sb[:], in_=b2r).then_inc(CLD, 16)
                    sync.wait_ge(ST0, F2SB[0])
                    sync.wait_ge(ST1, F2SB[1])
                    for dc in range(8):
                        if dc >= 1:
                            for qt in range(8):
                                idx = (dc - 1) * 8 + qt
                                par = idx % 2
                                sync.wait_ge(GSQ, idx + 1)
                                sync.dma_start(out=d_out2[qt * 128:(qt + 1) * 128, (dc - 1) * 512:dc * 512],
                                               in_=o2t[par][:]).then_inc([ST0, ST1][par], 16)
                        for fb in range(8):
                            key = dc * 8 + fb
                            par = key % 2
                            sl = [LC, LD2][par]
                            if key >= 2:
                                sync.wait_ge(GFB, key - 1)
                            sync.dma_start(
                                out=hTp[par][:].rearrange("p (a q) -> p a q", a=16),
                                in_=d_hT[fb * 2048:(fb + 1) * 2048, :].rearrange("(a p) q -> p a q", p=128),
                            ).then_inc(sl, 16)
                    for qt in range(8):
                        idx = 7 * 8 + qt
                        par = idx % 2
                        sync.wait_ge(GSQ, idx + 1)
                        sync.dma_start(out=d_out2[qt * 128:(qt + 1) * 128, 7 * 512:8 * 512],
                                       in_=o2t[par][:]).then_inc([ST0, ST1][par], 16)
                    sync.wait_ge(ST0, F2SB[0] + 16 * 32)
                    sync.wait_ge(ST1, F2SB[1] + 16 * 32)
                    if taps:
                        sync.dma_start(out=tap_aps["t_out2"], in_=d_out2).then_inc(SG, 16)
                cld_n[0] += 16

                @block.gpsimd
                def _(gp):
                    gp.wait_ge(ST0, F2SB[0])
                    gp.wait_ge(ST1, F2SB[1])
                    for dc in range(8):
                        for fb in range(8):
                            key = dc * 8 + fb
                            par = key % 2
                            sl = [LE, LF][par]
                            if key >= 2:
                                gp.wait_ge(GFB, key - 1)
                            gp.dma_start(
                                out=w2_sb[par][:].rearrange("p (s n) -> p s n", s=16),
                                in_=w2[fb * 2048:(fb + 1) * 2048, dc * 512:(dc + 1) * 512].rearrange("(s p) n -> p s n", p=128),
                            ).then_inc(sl, 16)

                @block.tensor
                def _(tn):
                    tn.wait_ge(CLD, cld_n[0])
                    tn.wait_ge(WLD, wld_n[0])
                    for dc in range(8):
                        for fb in range(8):
                            key = dc * 8 + fb
                            par = key % 2
                            tn.wait_ge([LE, LF][par], W2B[par] + 16 * (key // 2 + 1))
                            tn.wait_ge([LC, LD2][par], HLB[par] + 16 * (key // 2 + 1))
                            if fb == 0 and dc >= 1:
                                tn.wait_ge(GSQ, dc * 8)
                            for qt in range(8):
                                for j in range(16):
                                    mm = tn.matmul(p_s[qt][:],
                                                   hTp[par][:, j * 1024 + qt * 128:j * 1024 + (qt + 1) * 128],
                                                   w2_sb[par][:, j * 512:(j + 1) * 512],
                                                   start=(fb == 0 and j == 0), stop=False)
                            mm.then_inc(GFB, 1)
                        for qt in range(8):
                            mm = tn.matmul(p_s[qt][:], onesf_sb[:], b2_sb[:, dc * 512:(dc + 1) * 512],
                                           start=False, stop=True)
                            mm.then_inc(GMM, 1)

                @block.scalar
                def _(s):
                    for dc in range(8):
                        for qt in range(8):
                            idx = dc * 8 + qt
                            par = idx % 2
                            s.wait_ge(GMM, idx + 1)
                            if idx >= 2:
                                s.wait_ge([ST0, ST1][par], F2SB[par] + 16 * (idx // 2))
                            if idx >= 2:
                                s.wait_ge(GST, idx - 1)
                            s.activation(o2t[par][:], p_s[qt][:], AF.Copy,
                                         accum_out=tstat2[:, 2 * par:2 * par + 1]).then_inc(GEV, 1)
                            s.wait_ge(GEV, idx + 1)
                            s.activation(junk[:, par * 512:(par + 1) * 512], o2t[par][:], AF.Square,
                                         accum_out=tstat2[:, 2 * par + 1:2 * par + 2]).then_inc(GSQ, 1)

                @block.vector
                def _(v):
                    v.memset(sums[:], 0.0)
                    vfence(v, v.memset(ssqs[:], 0.0))
                    for dc in range(8):
                        for qt in range(8):
                            idx = dc * 8 + qt
                            par = idx % 2
                            v.wait_ge(GSQ, idx + 1)
                            vfence(v, v.tensor_add(sums[:, qt:qt + 1], sums[:, qt:qt + 1], tstat2[:, 2 * par:2 * par + 1]))
                            v.tensor_add(ssqs[:, qt:qt + 1], ssqs[:, qt:qt + 1],
                                         tstat2[:, 2 * par + 1:2 * par + 2]).then_inc(GST, 1)

                barrier()

            with ExitStack() as ctxF3:
                EF3 = ctxF3.enter_context
                x2r = [EF3(nc.sbuf_tensor(f"x2r{i}", [128, D], F32)) for i in range(2)]
                o2r = [EF3(nc.sbuf_tensor(f"o2r{i}", [128, D], F32)) for i in range(2)]
                xo = [EF3(nc.sbuf_tensor(f"xo{i}", [128, D], F32)) for i in range(2)]

                ZVAR = sem("z_var")
                ZRS = sem("z_rs")
                ZXO = sem("z_xo")

                ZB = [(8 + 32 + 4 * 2) * 16, (32 + 4 * 2) * 16]

                @block.sync
                def _(sync):
                    for qt in range(8):
                        par = qt % 2
                        sl = [LA, LB][par]
                        if qt >= 2:
                            sync.wait_ge(ZXO, qt - 1)
                        sync.dma_start(out=x2r[par][:], in_=d_x2[qt * 128:(qt + 1) * 128, :]).then_inc(sl, 16)
                        sync.dma_start(out=o2r[par][:], in_=d_out2[qt * 128:(qt + 1) * 128, :]).then_inc(sl, 16)
                        if qt >= 2:
                            sync.dma_start(out=out_rows[(qt - 2) * 128:(qt - 1) * 128, :],
                                           in_=xo[(qt - 2) % 2][:]).then_inc([OST, OST1][qt % 2], 16)

                @block.vector
                def _(v):
                    for qt in range(8):
                        par = qt % 2
                        sl = [LA, LB][par]
                        v.wait_ge(sl, ZB[par] + 32 * (qt // 2 + 1))
                        v.tensor_scalar(stats[:, 0:1], sums[:, qt:qt + 1], -1.0 / D, None, ALU.mult)
                        vfence(v, v.tensor_scalar(stats[:, 1:2], ssqs[:, qt:qt + 1], 1.0 / D, None, ALU.mult))
                        vfence(v, v.tensor_mul(stats[:, 2:3], stats[:, 0:1], stats[:, 0:1]))
                        v.tensor_sub(stats[:, 3:4], stats[:, 1:2], stats[:, 2:3]).then_inc(ZVAR, 1)
                        v.wait_ge(ZRS, qt + 1)
                        vfence(v, v.tensor_scalar(o2r[par][:], o2r[par][:], stats[:, 0:1], None, ALU.add))
                        vfence(v, v.tensor_scalar(o2r[par][:], o2r[par][:], stats[:, 4:5], None, ALU.mult))
                        if qt >= 2:
                            v.wait_ge([OST, OST1][qt % 2], 16 * (qt // 2))
                        v.tensor_add(xo[par][:], x2r[par][:], o2r[par][:]).then_inc(ZXO, 1)

                @block.scalar
                def _(s):
                    for qt in range(8):
                        s.wait_ge(ZVAR, qt + 1)
                        sfence(s, s.activation(stats[:, 20:21], stats[:, 3:4], AF.Ln, bias=eps_sb[:]))
                        s.activation(stats[:, 4:5], stats[:, 20:21], AF.Exp, scale=-0.5).then_inc(ZRS, 1)

                @block.sync
                def _(sync):
                    for qt in range(6, 8):
                        par = qt % 2
                        sync.wait_ge(ZXO, qt + 1)
                        sync.dma_start(out=out_rows[qt * 128:(qt + 1) * 128, :],
                                       in_=xo[par][:]).then_inc([OST, OST1][qt % 2], 16)
                    sync.wait_ge(OST, 16 * 4)
                    sync.wait_ge(OST1, 16 * 4)
                    if taps:
                        sync.wait_ge(SG, 16 * 9)

    return nc


def _host_constants():
    inv_freq = 1.0 / (10000.0 ** (np.arange(0, RD, 2, dtype=np.float64) / RD))
    t = np.arange(S, dtype=np.float64)
    freqs = np.outer(t, inv_freq)
    emb = np.concatenate([freqs, freqs], axis=-1)
    cosk = np.cos(emb).astype(np.float32)
    sink = np.sin(emb).astype(np.float32)
    ident = np.eye(128, dtype=ml_dtypes.bfloat16)
    ones_c = np.ones((128, 128), dtype=ml_dtypes.bfloat16)
    return cosk, sink, ident, ones_c


def make_in_maps(inputs):
    x = np.ascontiguousarray(np.asarray(inputs["L0_attn_hidden_states"], dtype=np.float32).reshape(S, D))
    cosk, sink, ident, ones_c = _host_constants()
    common = {
        "x_full": x,
        "w_qa": np.ascontiguousarray(inputs["L0_attn_w_qa"], dtype=np.float32),
        "g_qa": np.ascontiguousarray(inputs["L0_attn_g_qa"], dtype=np.float32).reshape(1, QR),
        "w_qb": np.ascontiguousarray(inputs["L0_attn_w_qb"], dtype=np.float32),
        "w_kva": np.ascontiguousarray(inputs["L0_attn_w_kva"], dtype=np.float32),
        "g_kva": np.ascontiguousarray(inputs["L0_attn_g_kva"], dtype=np.float32).reshape(1, KVR),
        "w_kvb": np.ascontiguousarray(inputs["L0_attn_w_kvb"], dtype=np.float32),
        "w_o": np.ascontiguousarray(inputs["L0_attn_w_o"], dtype=np.float32),
        "w1": np.ascontiguousarray(inputs["L0_ffn_inter_w"], dtype=np.float32),
        "b1r": np.ascontiguousarray(np.asarray(inputs["L0_ffn_inter_b"], dtype=np.float32).reshape(128, 128).T),
        "w2": np.ascontiguousarray(inputs["L0_ffn_output_w"], dtype=np.float32),
        "b2r": np.ascontiguousarray(inputs["L0_ffn_output_b"], dtype=np.float32).reshape(1, D),
        "cosk": cosk,
        "sink": sink,
        "identc": ident,
        "epsc": np.full((128, 1), LN_EPS, np.float32),
        "onesfc": np.ones((1, 128), np.float32),
        "onesf2c": np.ones((128, 128), np.float32),
        "identfc": np.eye(128, dtype=np.float32),
        "onesc": ones_c,
    }
    in_maps = []
    for c in range(NCORES):
        m = dict(common)
        m["x_loc"] = np.ascontiguousarray(x[c * R:(c + 1) * R, :])
        m["cosqT"] = np.ascontiguousarray(cosk[c * R:(c + 1) * R, :].T)
        m["sinqT"] = np.ascontiguousarray(sink[c * R:(c + 1) * R, :].T)
        in_maps.append(m)
    return in_maps


def kernel(**inputs):
    in_maps = make_in_maps(inputs)
    nc = build_nc(taps=False)
    res = run_bass_kernel_spmd(nc, in_maps, core_ids=list(range(NCORES)))
    out = np.concatenate([res.results[c]["out_rows"] for c in range(NCORES)], axis=0)
    return out.reshape(1, S, D).astype(np.float32)
